# version 28
# baseline (speedup 1.0000x reference)
import contextlib
import numpy as np
import ml_dtypes
import concourse.bass as bass
import concourse.mybir as mybir
from concourse.bass_utils import run_bass_kernel_spmd

F32 = mybir.dt.float32
BF16 = mybir.dt.bfloat16
FP8 = mybir.dt.float8e4
AF = mybir.ActivationFunctionType
ALU = mybir.AluOpType
AX = mybir.AxisListType
BF = ml_dtypes.bfloat16

D = 2048
NH, DH = 16, 128
I = 8192
BLK = 32
NBD = D // BLK
NBI = I // BLK
QCLAMP = 448.0 * 1e-30
LN_EPS = 1e-5
RMS_EPS = 1e-6
TANH_SCALE = float((DH ** (-0.5)) / 50.0)


def _pi_src():
    pos = np.arange(D)
    return (32 * (pos % NBD) + pos // NBD).astype(np.int64)


def _rho_src():
    pos = np.arange(I)
    lf = pos % 2048
    return ((pos // 2048) * 2048 + 32 * (lf % 64) + lf // 64).astype(np.int64)


def _tiles_of(c, ntg, nc):
    return list(range(c, ntg, nc))


def _i0_of(g, nc):
    return max(0, -((-(g - (nc - 1))) // nc))


def _chunks512(lo, hi):
    out = []
    while lo < hi:
        nxt = min(hi, (lo // 512 + 1) * 512)
        out.append((lo, nxt))
        lo = nxt
    return out


def build_program(T, NC, out_stage=3):
    L = T // NC
    NT = L // 128
    NTG = T // 128
    assert T % (128 * NC) == 0 and NT >= 2 and NT % 2 == 0
    R1 = I // NC
    R2 = D // NC
    assert R1 % 128 == 0 and R2 % 128 == 0
    TTG = min(4, NT)
    HALF = TTG // 2

    nc_ = bass.Bass("TRN2", target_bir_lowering=False)

    def din(name, shape, dt=F32):
        return nc_.dram_tensor(name, list(shape), dt, kind="ExternalInput")

    x_in = din("x_loc", [L, D])
    wqT_d = din("wqT", [D // NC, D]); wkT_d = din("wkT", [D // NC, D])
    wvT_d = din("wvT", [D // NC, D])
    woT_d = din("woT", [D // NC, D])
    gsh = din("gsh", [R1, D]); ush = din("ush", [R1, D]); dsh = din("dsh", [R2, I])
    cg_q_in = din("cg_q", [1, D]); sg_q_in = din("sg_q", [1, D])
    cg_k_in = din("cg_k", [1, D]); sg_k_in = din("sg_k", [1, D])
    ident_in = din("ident_in", [128, 128], BF16)
    cmask_in = din("cmask_in", [128, NTG * 128], BF16)
    out_t = nc_.dram_tensor("out_loc", [L, D], F32, kind="ExternalOutput")

    wq_l = nc_.dram_tensor("wq_l", [D // NC, D], BF16)
    wk_l = nc_.dram_tensor("wk_l", [D // NC, D], BF16)
    wv_l = nc_.dram_tensor("wv_l", [D // NC, D], BF16)
    wo_l = nc_.dram_tensor("wo_l", [D // NC, D], BF16)
    gu_g_l = nc_.dram_tensor("gu_g_l", [D, R1], BF16)
    gu_u_l = nc_.dram_tensor("gu_u_l", [D, R1], BF16)
    ddqT_l = nc_.dram_tensor("ddqT_l", [I, R2], BF16)
    HD2 = D // 2
    kT_lh = [nc_.dram_tensor(f"kT_l{a}", [HD2, L], BF16) for a in range(2)]
    v_lh = [nc_.dram_tensor(f"v_l{a}", [L, HD2], BF16) for a in range(2)]
    _AS = "Shared" if NC > 4 else "Local"
    if NC > 1:
        gudqT_g = nc_.dram_tensor("gudqT_g", [NC * D, R1], BF16, addr_space=_AS)
        gudqT_u = nc_.dram_tensor("gudqT_u", [NC * D, R1], BF16, addr_space=_AS)
        ddqT = nc_.dram_tensor("ddqT", [NC * I, R2], BF16, addr_space=_AS)
        wq_bf = nc_.dram_tensor("wq_bf", [D, D], BF16, addr_space=_AS)
        wk_bf = nc_.dram_tensor("wk_bf", [D, D], BF16, addr_space=_AS)
        wv_bf = nc_.dram_tensor("wv_bf", [D, D], BF16, addr_space=_AS)
        wo_bf = nc_.dram_tensor("wo_bf", [D, D], BF16, addr_space=_AS)
        kT_agh = [nc_.dram_tensor(f"kT_ag{a}", [NC * HD2, L], BF16, addr_space=_AS)
                  for a in range(2)]
        v_agh = [nc_.dram_tensor(f"v_ag{a}", [NC * L, HD2], BF16, addr_space=_AS)
                 for a in range(2)]
    else:
        gudqT_g, gudqT_u, ddqT = gu_g_l, gu_u_l, ddqT_l
        wq_bf, wk_bf, wv_bf, wo_bf = wq_l, wk_l, wv_l, wo_l
        kT_agh, v_agh = kT_lh, v_lh

    st = contextlib.ExitStack()
    sems = {}
    for s in ["SY", "GP", "CC", "PE", "VE", "SE", "SY2", "SYX",
              "SYW0", "SYW1", "SYW2", "SYW3", "SYW4", "SYW5", "SYW6", "SYW7",
              "SYD0", "SYD1", "SYD2", "SYD3", "SYK0", "SYK1", "SYK2",
              "GW0", "GW1", "GW2", "GW3", "GE", "SYDQ0", "SYDQ1", "SYDQ2", "SYDQ3",
              "SYO0", "SYO1", "GPC", "GPX"]:
        sems[s] = st.enter_context(nc_.semaphore(s))

    class Px:
        def __init__(self):
            self.cnt = {k: 0 for k in sems}

        def inc(self, instr, sem, n=1):
            instr.then_inc(sems[sem], n)
            self.cnt[sem] += n
            return self.cnt[sem]

        def w(self, eng, sem, val=None):
            v = self.cnt[sem] if val is None else val
            if v > 0:
                eng.wait_ge(sems[sem], v)

    p = Px()

    with st:
        x_tm = st.enter_context(nc_.sbuf_tensor("x_tm", [128, NT, D], BF16))
        ident = st.enter_context(nc_.sbuf_tensor("ident", [128, 128], BF16))
        ones_cb = st.enter_context(nc_.sbuf_tensor("ones_cb", [128, 1], BF16))
        ones_cf = st.enter_context(nc_.sbuf_tensor("ones_cf", [128, 1], F32))
        ones_rf = st.enter_context(nc_.sbuf_tensor("ones_rf", [1, 128], F32))
        cst_eps = st.enter_context(nc_.sbuf_tensor("cst_eps", [128, 1], F32))
        nc_.const_aps.register(F32, LN_EPS, cst_eps[:, :])
        block = st.enter_context(nc_.Block())

        ENGLOG = {e: [] for e in ("tensor", "vector", "scalar", "gpsimd", "sync")}

        class _IP:
            __slots__ = ("incs",)
            def __init__(self):
                self.incs = []
            def then_inc(self, sem, n=1):
                self.incs.append((sem, n))
                return self

        class _EP:
            __slots__ = ("log",)
            def __init__(self, log):
                self.log = log
            def __getattr__(self, meth):
                log = self.log
                def call(*a, **k):
                    ip = _IP()
                    log.append((meth, a, k, ip))
                    return ip
                return call

        def on(engine_name, fn):
            fn(_EP(ENGLOG[engine_name]))

        def flush():
            for name in ("sync", "gpsimd", "tensor", "vector", "scalar"):
                log = ENGLOG[name]
                if not log:
                    continue
                ENGLOG[name] = []
                def _replay(eng, log=log):
                    for meth, a, k, ip in log:
                        r = getattr(eng, meth)(*a, **k)
                        for (sem, n) in ip.incs:
                            r.then_inc(sem, n)
                getattr(block, name)(_replay)

        def barrier():
            snap = {k: v for k, v in p.cnt.items() if k not in ("GPC", "CC")}
            for en in ["tensor", "vector", "scalar", "gpsimd", "sync"]:
                def _f(e, snap=snap):
                    for sn, v in snap.items():
                        if v > 0:
                            e.wait_ge(sems[sn], v)
                on(en, _f)
            flush()

        def _init_sync(y):
            p.inc(y.dma_start(out=ident[:, :], in_=ident_in.ap()), "SY", 16)
        on("sync", _init_sync)

        def _ldx_all(g):
            p.inc(g.dma_start(out=x_tm[:, :, :],
                              in_=x_in.ap().rearrange("(t p) d -> p t d", p=128)), "GPX", 16)
        on("gpsimd", _ldx_all)

        def _init_vec(v):
            v.memset(ones_cb[:, :], 1.0)
            v.memset(ones_cf[:, :], 1.0)
            v.memset(cst_eps[:, :], LN_EPS)
            p.inc(v.memset(ones_rf[:, :], 1.0), "VE")
        on("vector", _init_vec)

        def _cast_attn_w(g):
            for wsrc, wdst in ((wqT_d, wq_l), (wkT_d, wk_l), (wvT_d, wv_l), (woT_d, wo_l)):
                p.inc(g.dma_start(out=wdst.ap(), in_=wsrc.ap()), "GPC", 16)
        on("gpsimd", _cast_attn_w)

        with contextlib.ExitStack() as dq:
            wnat = dq.enter_context(nc_.sbuf_tensor("wnat", [128, 4, 2048], F32))
            wdq = dq.enter_context(nc_.sbuf_tensor("wdq", [128, 4, 2048], BF16))
            q8w = dq.enter_context(nc_.sbuf_tensor("q8w", [128, 4, 2048], FP8))
            bmw = dq.enter_context(nc_.sbuf_tensor("bmw", [128, 4, 64], F32))
            r2w = dq.enter_context(nc_.sbuf_tensor("r2w", [128, 4, 64], F32))
            s2w = dq.enter_context(nc_.sbuf_tensor("s2w", [128, 4, 64], F32))
            trs = dq.enter_context(nc_.sbuf_tensor("trs", [128, 2, 16, 512], BF16))
            dps = [dq.enter_context(nc_.psum_tensor(f"dps{i}", [128, 1024], BF16))
                   for i in range(2)]
            NBU = 2048 // BLK
            dq_state = {"i": 0,
                        "ve_mark": {b: 0 for b in range(4)},
                        "ge_mark": {b: 0 for b in range(4)},
                        "tr_mark": {b: 0 for b in range(4)},
                        "pending_dq2": None}

            def dequant_unit(in_ap, batch_out):
                b = dq_state["i"] % 4
                w = (dq_state["i"] // 4) % 2
                dq_state["i"] += 1

                def _load(y, b=b, in_ap=in_ap):
                    p.w(y, "VE", dq_state["ve_mark"][b])
                    p.w(y, "GE", dq_state["ge_mark"][b])
                    p.inc(y.dma_start(out=wnat[:, b, :], in_=in_ap), f"SYDQ{b}", 16)
                on("sync", _load)
                sy_h = p.cnt[f"SYDQ{b}"]

                def _scales(v, sy_h=sy_h, b=b):
                    p.w(v, f"SYDQ{b}", sy_h)
                    xv = wnat[:, b, :].rearrange("p (j bb) -> p bb j", bb=NBU, j=BLK)
                    p.inc(v.tensor_reduce(bmw[:, b, :], xv, axis=AX.X, op=ALU.max,
                                          apply_absolute_value=True), "VE")
                    p.w(v, "VE")
                    p.inc(v.tensor_scalar(bmw[:, b, :], bmw[:, b, :], QCLAMP, None, ALU.max), "VE")
                    p.w(v, "VE")
                    p.inc(v.reciprocal(r2w[:, b, :], bmw[:, b, :]), "VE")
                    p.w(v, "VE")
                    p.inc(v.tensor_scalar(r2w[:, b, :], r2w[:, b, :], 224.0, None, ALU.mult), "VE")
                    p.inc(v.tensor_scalar(s2w[:, b, :], bmw[:, b, :], 1.0 / 224.0, None, ALU.mult), "VE")
                on("vector", _scales)
                ve_s = p.cnt["VE"]

                def _y(g, b=b, ve_s=ve_s):
                    p.w(g, "VE", ve_s)
                    rb = r2w[:, b, :].unsqueeze(1).broadcast_to([128, BLK, NBU])
                    p.inc(g.tensor_tensor(q8w[:, b, :].rearrange("p (j bb) -> p j bb", bb=NBU),
                                          wnat[:, b, :].rearrange("p (j bb) -> p j bb", bb=NBU),
                                          rb, op=ALU.mult), "GE")
                on("gpsimd", _y)
                dq_state["ge_mark"][b] = p.cnt["GE"]
                ge_y = p.cnt["GE"]

                def tail(b=b, w=w, ge_y=ge_y, batch_out=batch_out):
                    def _dq2(v, b=b, ge_y=ge_y):
                        p.w(v, "GE", ge_y)
                        if dq_state["tr_mark"][b] > 0:
                            p.w(v, "PE", dq_state["tr_mark"][b])
                        sb_ = s2w[:, b, :].unsqueeze(1).broadcast_to([128, BLK, NBU])
                        p.inc(v.tensor_tensor(wdq[:, b, :].rearrange("p (j bb) -> p j bb", bb=NBU),
                                              q8w[:, b, :].rearrange("p (j bb) -> p j bb", bb=NBU),
                                              sb_, op=ALU.mult), "VE")
                    on("vector", _dq2)
                    dq_state["ve_mark"][b] = p.cnt["VE"]
                    ve_dq = p.cnt["VE"]

                    pe0 = p.cnt["PE"]
                    se0 = p.cnt["SE"]

                    def _tr(t, ve_dq=ve_dq, se0=se0, b=b):
                        p.w(t, "SY")
                        p.w(t, "VE", ve_dq)
                        p.w(t, "SE")
                        for kc in range(16):
                            if kc >= 2:
                                t.wait_ge(sems["SE"], se0 + kc - 1)
                            p.inc(t.transpose(dps[kc % 2][:, :128],
                                              wdq[:, b, kc * 128:(kc + 1) * 128], ident[:, :]), "PE")
                    on("tensor", _tr)
                    dq_state["tr_mark"][b] = p.cnt["PE"]

                    def _ev(s, pe0=pe0, b=b, w=w):
                        p.w(s, f"SYO{w}")
                        for kc in range(16):
                            s.wait_ge(sems["PE"], pe0 + kc + 1)
                            p.inc(s.copy(trs[:, w, kc, b * 128:(b + 1) * 128], dps[kc % 2][:, :128]), "SE")
                    on("scalar", _ev)

                    if b == 3:
                        se_e = p.cnt["SE"]

                        def _out(s, w=w, se_e=se_e, batch_out=batch_out):
                            s.wait_ge(sems["SE"], se_e)
                            for dst_ap, src_ap in batch_out(w):
                                p.inc(s.dma_start(out=dst_ap, in_=src_ap), f"SYO{w}", 16)
                        on("scalar", _out)

                if dq_state["pending_dq2"] is not None:
                    dq_state["pending_dq2"]()
                dq_state["pending_dq2"] = tail

            def flush_tail():
                if dq_state["pending_dq2"] is not None:
                    dq_state["pending_dq2"]()
                    dq_state["pending_dq2"] = None

            def _emit_gather(ins_t, outs_t, wv, pre=None):
                if NC <= 1:
                    return

                def _g(g, ins_t=ins_t, outs_t=outs_t, wv=wv, pre=pre):
                    p.w(g, "SYO0", wv[0])
                    p.w(g, "SYO1", wv[1])
                    if pre is not None:
                        pre(g)
                    p.inc(g.collective_compute("AllGather", ALU.bypass,
                                               replica_groups=[list(range(NC))],
                                               ins=[ins_t.ap().opt()], outs=[outs_t.ap().opt()]), "CC")
                on("gpsimd", _g)

            syo_after = {}
            for src_t, dst in ((gsh, gu_g_l), (ush, gu_u_l)):
                for rt in range(R1 // 128):
                    bo = None
                    if rt % 4 == 3:
                        def bo(w, rt0=rt - 3, dst=dst):
                            return [(bass.AP(dst, rt0 * 128 + q * 4 * 128 * R1,
                                             [[R1, 128], [128 * R1, 4], [1, 512]]),
                                     trs[:, w, q * 4:(q + 1) * 4, :]) for q in range(4)]
                    dequant_unit(src_t.ap()[rt * 128:(rt + 1) * 128, :], bo)
                flush_tail()
                syo_after[dst.name] = (p.cnt["SYO0"], p.cnt["SYO1"])

            _emit_gather(gu_g_l, gudqT_g, syo_after[gu_g_l.name])

            NRD = R2 // 128
            for rt in range(NRD):
                for s4 in range(4):
                    bo = None
                    if s4 == 3:
                        def bo(w, rt=rt):
                            return [(bass.AP(ddqT_l, rt * 128 + q4 * (16 * 128 * R2),
                                             [[R2, 128], [128 * R2, 16], [1, 128]]),
                                     trs[:, w, :, q4 * 128:(q4 + 1) * 128]) for q4 in range(4)]
                    dequant_unit(bass.AP(dsh, (rt * 128) * I + s4 * 2048, [[I, 128], [1, 2048]]), bo)
                if rt == (NRD - 1) // 2:
                    _emit_gather(gu_u_l, gudqT_u, syo_after[gu_u_l.name])
            flush_tail()
            _emit_gather(ddqT_l, ddqT, (p.cnt["SYO0"], p.cnt["SYO1"]))

            def _pre_w(g):
                p.w(g, "GPC", 64)
            for lt, gt in ((wq_l, wq_bf), (wk_l, wk_bf), (wv_l, wv_bf), (wo_l, wo_bf)):
                _emit_gather(lt, gt, (0, 0), pre=_pre_w if lt is wq_l else None)
            flush()
        barrier()

        def ln_stats(v, src_ap, scr_ap, nst):
            p.inc(v.tensor_reduce(nst[:, 0:1], src_ap, axis=AX.X, op=ALU.add, negate=True), "VE")
            p.inc(v.scalar_tensor_tensor(scr_ap, src_ap, 1.0, src_ap,
                                         op0=ALU.mult, op1=ALU.mult, accum_out=nst[:, 1:2]), "VE")
            p.w(v, "VE")
            p.inc(v.tensor_scalar(nst[:, 0:1], nst[:, 0:1], 1.0 / D, None, ALU.mult), "VE")
            p.w(v, "VE")
            p.inc(v.tensor_tensor(nst[:, 2:3], nst[:, 0:1], nst[:, 0:1], op=ALU.mult), "VE")
            p.w(v, "VE")
            p.inc(v.scalar_tensor_tensor(nst[:, 3:4], nst[:, 1:2], 1.0 / D, nst[:, 2:3],
                                         op0=ALU.mult, op1=ALU.subtract), "VE")

        def ln_apply(src_ap, dst_ap, nst):
            ve_v = p.cnt["VE"]

            def _sq(s, ve_v=ve_v):
                p.w(s, "VE", ve_v)
                p.inc(s.activation(nst[:, 4:5], nst[:, 3:4], AF.Sqrt, bias=LN_EPS), "SE")
            on("scalar", _sq)
            se_s = p.cnt["SE"]

            def _xn(v, se_s=se_s):
                p.w(v, "SE", se_s)
                p.inc(v.reciprocal(nst[:, 5:6], nst[:, 4:5]), "VE")
                p.w(v, "VE")
                p.inc(v.tensor_scalar(dst_ap, src_ap, nst[:, 0:1], nst[:, 5:6],
                                      op0=ALU.add, op1=ALU.mult), "VE")
                p.w(v, "VE")
            on("vector", _xn)

        def ffn_stage(first_ffn, write_out):
            sfx = "_f1" if first_ffn else "_f3"
            with contextlib.ExitStack() as fs:
                xn = fs.enter_context(nc_.sbuf_tensor("xn" + sfx, [128, D], F32))
                nst = fs.enter_context(nc_.sbuf_tensor("nst" + sfx, [128, 8], F32))
                q8x = fs.enter_context(nc_.sbuf_tensor("q8x" + sfx, [128, D], FP8))
                xq_tm = fs.enter_context(nc_.sbuf_tensor("xq_tm" + sfx, [128, TTG, D], BF16))
                xqT = fs.enter_context(nc_.sbuf_tensor("xqT" + sfx, [128, TTG, 16, 128], BF16))
                bmx = fs.enter_context(nc_.sbuf_tensor("bmx" + sfx, [128, NBI], F32))
                r2x = fs.enter_context(nc_.sbuf_tensor("r2x" + sfx, [128, NBI], F32))
                s2x = fs.enter_context(nc_.sbuf_tensor("s2x" + sfx, [128, NBI], F32))
                hbuf = fs.enter_context(nc_.sbuf_tensor("hbuf" + sfx, [128, TTG, I], BF16))
                q8h = fs.enter_context(nc_.sbuf_tensor("q8h" + sfx, [128, I], FP8))
                guw = fs.enter_context(nc_.sbuf_tensor("guw" + sfx, [128, 4, 2, 4, 512], BF16))
                dnw = fs.enter_context(nc_.sbuf_tensor("dnw" + sfx, [128, 4, 1024], BF16))
                outs = fs.enter_context(nc_.sbuf_tensor("outs" + sfx, [128, D], F32))
                sg_ev = fs.enter_context(nc_.sbuf_tensor("sg_ev" + sfx, [128, 512], F32))
                ps_gu = [fs.enter_context(nc_.psum_tensor(f"psg{i}" + sfx, [128, 512], F32))
                         for i in range(8)]
                psA = ps_gu[6][:, :].bitcast(BF16)
                psB = ps_gu[7][:, :].bitcast(BF16)

                def quant_rowblock(src_ap, qdst_ap, dqdst_ap, nblk, feat):
                    nch = feat // 2048

                    def _st(v):
                        p.w(v, "GE")
                        xv = src_ap.rearrange("p (c j b) -> p c b j", c=nch, j=BLK, b=64)
                        bmv = bmx[:, :nblk].rearrange("p (c b) -> p c b", c=nch)
                        p.inc(v.tensor_reduce(bmv, xv, axis=AX.X, op=ALU.max,
                                              apply_absolute_value=True), "VE")
                        p.w(v, "VE")
                        p.inc(v.tensor_scalar(bmx[:, :nblk], bmx[:, :nblk], QCLAMP, None, ALU.max), "VE")
                        p.w(v, "VE")
                        p.inc(v.reciprocal(r2x[:, :nblk], bmx[:, :nblk]), "VE")
                        p.w(v, "VE")
                        p.inc(v.tensor_scalar(r2x[:, :nblk], r2x[:, :nblk], 224.0, None, ALU.mult), "VE")
                        p.inc(v.tensor_scalar(s2x[:, :nblk], bmx[:, :nblk], 1.0 / 224.0, None, ALU.mult), "VE")
                    on("vector", _st)
                    ve_s = p.cnt["VE"]

                    def _yd(v, ve_s=ve_s):
                        p.w(v, "VE", ve_s)
                        rb = (r2x[:, :nblk].rearrange("p (c b) -> p c b", c=nch)
                              .unsqueeze(2).broadcast_to([128, nch, BLK, 64]))
                        sb_ = (s2x[:, :nblk].rearrange("p (c b) -> p c b", c=nch)
                               .unsqueeze(2).broadcast_to([128, nch, BLK, 64]))
                        p.inc(v.tensor_tensor(qdst_ap.rearrange("p (c j b) -> p c j b", c=nch, j=BLK, b=64),
                                              src_ap.rearrange("p (c j b) -> p c j b", c=nch, j=BLK, b=64),
                                              rb, op=ALU.mult), "VE")
                        p.w(v, "VE")
                        p.inc(v.tensor_tensor(dqdst_ap.rearrange("p (c j b) -> p c j b", c=nch, j=BLK, b=64),
                                              qdst_ap.rearrange("p (c j b) -> p c j b", c=nch, j=BLK, b=64),
                                              sb_, op=ALU.mult), "VE")
                    on("vector", _yd)

                def chunk_transpose(src_ap, dst_fn, nkc):
                    pe0 = p.cnt["PE"]
                    se0 = p.cnt["SE"]

                    def _tr(t, se0=se0):
                        p.w(t, "VE")
                        p.w(t, "SE")
                        for kc in range(nkc):
                            if kc >= 2:
                                t.wait_ge(sems["SE"], se0 + kc - 1)
                            pst = (psA, psB)[kc % 2]
                            p.inc(t.transpose(pst[:, :128], src_ap[:, kc * 128:(kc + 1) * 128], ident[:, :]), "PE")
                    on("tensor", _tr)

                    def _ev(s):
                        for kc in range(nkc):
                            pst = (psA, psB)[kc % 2]
                            s.wait_ge(sems["PE"], pe0 + kc + 1)
                            p.inc(s.copy(dst_fn(kc), pst[:, :128]), "SE")
                    on("scalar", _ev)

                se_xqT_by_group = {}
                prep_pe_mark = [0]

                def prep_ve(ttg0):
                    for i in range(TTG):
                        tt = ttg0 + i
                        if first_ffn:
                            def _wx(v, tt=tt):
                                p.w(v, "GPX", 16)
                            on("vector", _wx)

                        def _ln(v, tt=tt):
                            p.w(v, "SY2")
                            if prep_pe_mark[0] > 0:
                                p.w(v, "PE", prep_pe_mark[0])
                            ln_stats(v, x_tm[:, tt, :], xn[:, :], nst)
                        on("vector", _ln)
                        ln_apply(x_tm[:, tt, :], xn[:, :], nst)
                        quant_rowblock(xn[:, :], q8x[:, :], xq_tm[:, i, :], NBD, D)

                def prep_tr(ttg0):
                    for i in range(TTG):
                        chunk_transpose(xq_tm[:, i, :], lambda kc, i=i: xqT[:, i, kc, :], 16)
                    prep_pe_mark[0] = p.cnt["PE"]
                    se_xqT_by_group[ttg0] = p.cnt["SE"]

                ngroups = NT // TTG
                prep_ve(0)
                prep_tr(0)
                for ttg0 in range(0, NT, TTG):
                    g_n = TTG
                    se_xqT = se_xqT_by_group[ttg0]

                    NCCG = I // 512
                    pe_after = {}
                    ve_evac = {0: 0, 1: 0}
                    pe_mm_h = {0: 0, 1: 0}
                    for ccg in range(NCCG):
                        rank = (ccg * 512) // R1
                        coff = (ccg * 512) % R1
                        for wi in range(2):
                            wt = gudqT_g if wi == 0 else gudqT_u
                            for kg in range(4):
                                def _ldw(y, rank=rank, coff=coff, kg=kg, ccg=ccg, wi=wi, wt=wt):
                                    if NC > 1:
                                        p.w(y, "CC", 1 + wi)
                                    key = (wi, ccg - 1, kg)
                                    if key in pe_after:
                                        p.w(y, "PE", pe_after[key])
                                    src_ap = bass.AP(wt, (rank * D + kg * 4 * 128) * R1 + coff,
                                                     [[R1, 128], [128 * R1, 4], [1, 512]])
                                    p.inc(y.dma_start(out=guw[:, kg, wi, :, :], in_=src_ap),
                                          f"SYW{kg + 4 * wi}", 16)
                                on("sync", _ldw)
                                sy_w = (f"SYW{kg + 4 * wi}", p.cnt[f"SYW{kg + 4 * wi}"])

                                for half in range(2):
                                    hb = half * 4
                                    tts = list(range(half * HALF, half * HALF + HALF))

                                    def _mm(t, kg=kg, sy_w=sy_w, ccg=ccg, tts=tts, hb=hb,
                                            half=half, wi=wi):
                                        p.w(t, sy_w[0], sy_w[1])
                                        p.w(t, "SE", se_xqT)
                                        if NC > 1:
                                            p.w(t, "CC", 1 + wi)
                                        if ccg == 0 and wi == 0 and kg == 0:
                                            p.w(t, "VE")
                                        if ccg > 0 and wi == 0 and kg == 0:
                                            p.w(t, "VE", ve_evac[half])
                                        mm = None
                                        for k4 in range(4):
                                            k = kg * 4 + k4
                                            for i_ in tts:
                                                mm = t.matmul(ps_gu[hb + 2 * (i_ - tts[0]) + wi][:, :],
                                                              xqT[:, i_, k, :],
                                                              guw[:, kg, wi, k4, :],
                                                              start=(k == 0), stop=(k == 15))
                                        p.inc(mm, "PE")
                                    on("tensor", _mm)
                                    if wi == 1 and kg == 3:
                                        pe_mm_h[half] = p.cnt["PE"]
                                pe_after[(wi, ccg, kg)] = p.cnt["PE"]
                        pe_mm = p.cnt["PE"]

                        for half in range(2):
                            hb = half * 4
                            for j, i_ in enumerate(range(half * HALF, half * HALF + HALF)):
                                def _si(s, hb=hb, j=j, pe_mm=pe_mm_h[half]):
                                    p.w(s, "PE", pe_mm)
                                    p.w(s, "VE")
                                    p.inc(s.activation(sg_ev[:, :], ps_gu[hb + 2 * j][:, :], AF.Silu), "SE")
                                on("scalar", _si)
                                se_g = p.cnt["SE"]

                                def _h(v, hb=hb, j=j, i_=i_, ccg=ccg, se_g=se_g):
                                    p.w(v, "SE", se_g)
                                    p.inc(v.tensor_tensor(hbuf[:, i_, ccg * 512:(ccg + 1) * 512],
                                                          sg_ev[:, :], ps_gu[hb + 2 * j + 1][:, :],
                                                          op=ALU.mult), "VE")
                                on("vector", _h)
                            ve_evac[half] = p.cnt["VE"]

                    for i_ in range(g_n):
                        quant_rowblock(hbuf[:, i_, :], q8h[:, :], hbuf[:, i_, :], NBI, I)
                        chunk_transpose(hbuf[:, i_, :],
                                        lambda kc, i_=i_: hbuf[:, i_, kc * 128:(kc + 1) * 128], 64)

                    dn_ve = {0: p.cnt["VE"], 1: 0}
                    dn_se = {0: p.cnt["SE"], 1: p.cnt["SE"]}
                    if ttg0 + TTG < NT:
                        prep_ve(ttg0 + TTG)

                    pe_after_d = {}
                    for fh in range(2):
                        for kc in range(64):
                            buf = kc % 4

                            def _ldd(y, fh=fh, kc=kc, buf=buf):
                                if NC > 1:
                                    p.w(y, "CC", 3)
                                key = fh * 64 + kc - 4
                                if key >= 0:
                                    p.w(y, "PE", pe_after_d[key])
                                c0 = fh * 1024
                                if R2 >= 1024:
                                    src_ap = bass.AP(ddqT, ((c0 // R2) * I + kc * 128) * R2 + (c0 % R2),
                                                     [[R2, 128], [1, 1024]])
                                    dst_ap = dnw[:, buf, :]
                                else:
                                    nr = 1024 // R2
                                    src_ap = bass.AP(ddqT, ((c0 // R2) * I + kc * 128) * R2,
                                                     [[R2, 128], [I * R2, nr], [1, R2]])
                                    dst_ap = dnw[:, buf, :].rearrange("p (n r) -> p n r", n=nr)
                                p.inc(y.dma_start(out=dst_ap, in_=src_ap), f"SYD{buf}", 16)
                            on("sync", _ldd)
                            sy_d = (f"SYD{buf}", p.cnt[f"SYD{buf}"])

                            def _dmm(t, kc=kc, buf=buf, sy_d=sy_d, fh=fh):
                                p.w(t, sy_d[0], sy_d[1])
                                if NC > 1:
                                    p.w(t, "CC", 3)
                                if kc == 0:
                                    p.w(t, "SE", dn_se[fh])
                                    p.w(t, "VE", dn_ve[fh])
                                mm = None
                                for i_ in range(g_n):
                                    mm = t.matmul(ps_gu[i_ * 2][:, :], hbuf[:, i_, kc * 128:(kc + 1) * 128],
                                                  dnw[:, buf, 0:512], start=(kc == 0), stop=(kc == 63))
                                    mm = t.matmul(ps_gu[i_ * 2 + 1][:, :], hbuf[:, i_, kc * 128:(kc + 1) * 128],
                                                  dnw[:, buf, 512:1024], start=(kc == 0), stop=(kc == 63))
                                p.inc(mm, "PE")
                            on("tensor", _dmm)
                            pe_after_d[fh * 64 + kc] = p.cnt["PE"]
                        pe_dn = p.cnt["PE"]

                        for i_ in range(g_n):
                            tt = ttg0 + i_

                            def _res(v, i_=i_, tt=tt, fh=fh, pe_dn=pe_dn):
                                p.w(v, "PE", pe_dn)
                                if write_out:
                                    p.w(v, "SY2")
                                for cb in range(2):
                                    c0 = fh * 1024 + cb * 512
                                    dst = outs[:, c0:c0 + 512] if write_out else x_tm[:, tt, c0:c0 + 512]
                                    p.inc(v.scalar_tensor_tensor(dst, ps_gu[i_ * 2 + cb][:, :], 0.5,
                                                                 x_tm[:, tt, c0:c0 + 512],
                                                                 op0=ALU.mult, op1=ALU.add), "VE")
                            on("vector", _res)

                            if write_out:
                                ve_o = p.cnt["VE"]

                                def _sto(y, tt=tt, fh=fh, ve_o=ve_o):
                                    p.w(y, "VE", ve_o)
                                    c0 = fh * 1024
                                    p.inc(y.dma_start(out=out_t.ap()[tt * 128:(tt + 1) * 128, c0:c0 + 1024],
                                                      in_=outs[:, c0:c0 + 1024]), "SY2", 16)
                                on("sync", _sto)

                        if fh == 0:
                            dn_ve[1] = p.cnt["VE"]
                            if ttg0 + TTG < NT:
                                prep_tr(ttg0 + TTG)
                                dn_se[1] = p.cnt["SE"]
                flush()

        TTA = min(4, NT)

        def attn_stage():
            with contextlib.ExitStack() as ats:
                QT = ats.enter_context(nc_.sbuf_tensor("QT", [128, NH, L], BF16))
                attno = ats.enter_context(nc_.sbuf_tensor("attno", [128, NH, L], BF16))

                with contextlib.ExitStack() as qs:
                    xnb = qs.enter_context(nc_.sbuf_tensor("xnb", [128, D], BF16))
                    xnT = qs.enter_context(nc_.sbuf_tensor("xnT", [128, NT, 16, 128], BF16))
                    sq2 = qs.enter_context(nc_.sbuf_tensor("sq2", [128, 2, D], F32))
                    nst = qs.enter_context(nc_.sbuf_tensor("nst2", [128, 8], F32))
                    cgq = qs.enter_context(nc_.sbuf_tensor("cgq", [128, D], BF16))
                    sgq = qs.enter_context(nc_.sbuf_tensor("sgq", [128, D], BF16))
                    cgk = qs.enter_context(nc_.sbuf_tensor("cgk", [128, D], BF16))
                    sgk = qs.enter_context(nc_.sbuf_tensor("sgk", [128, D], BF16))
                    wqk = qs.enter_context(nc_.sbuf_tensor("wqk", [128, 4, 2, 512], BF16))
                    msq = qs.enter_context(nc_.sbuf_tensor("msq", [128, 2, 4], F32))
                    rstd = qs.enter_context(nc_.sbuf_tensor("rstd", [128, 2, 4], F32))
                    qtm = qs.enter_context(nc_.sbuf_tensor("qtm", [128, 512], BF16))
                    ktm = qs.enter_context(nc_.sbuf_tensor("ktm", [128, 512], BF16))
                    asc = qs.enter_context(nc_.sbuf_tensor("asc", [128, 2, 512], BF16))
                    bsc = qs.enter_context(nc_.sbuf_tensor("bsc", [128, 2, 512], BF16))
                    kst = qs.enter_context(nc_.sbuf_tensor("kst", [128, 4, 128], BF16))
                    vstg = qs.enter_context(nc_.sbuf_tensor("vstg", [128, 512], BF16))
                    ps = [qs.enter_context(nc_.psum_tensor(f"aps{i}", [128, 512], F32))
                          for i in range(8)]

                    for ri, (rin, dstt) in enumerate(((cg_q_in, cgq), (sg_q_in, sgq),
                                                      (cg_k_in, cgk), (sg_k_in, sgk))):
                        def _lr(y, rin=rin):
                            p.w(y, "PE")
                            p.inc(y.dma_start(out=sq2[0:1, 0, :], in_=rin.ap()), "SYX", 16)
                        on("sync", _lr)
                        sy_r = p.cnt["SYX"]

                        def _b1(t, sy_r=sy_r):
                            p.w(t, "SYX", sy_r)
                            p.w(t, "VE")
                            p.w(t, "SE")
                            mm = None
                            for j in range(4):
                                mm = t.matmul(ps[j][:, :], ones_rf[:, :], sq2[0:1, 0, j * 512:(j + 1) * 512],
                                              start=True, stop=True)
                            p.inc(mm, "PE")
                        on("tensor", _b1)
                        pe_r = p.cnt["PE"]

                        def _c1(s, dstt=dstt, pe_r=pe_r):
                            p.w(s, "PE", pe_r)
                            for j in range(4):
                                p.inc(s.copy(dstt[:, j * 512:(j + 1) * 512], ps[j][:, :]), "SE")
                        on("scalar", _c1)
                    se_rope = p.cnt["SE"]

                    psA2 = ps[6][:, :].bitcast(BF16)
                    psB2 = ps[7][:, :].bitcast(BF16)
                    pe_buf = {0: 0, 1: 0, 2: 0, 3: 0}
                    ms_mark = {0: 0, 1: 0}
                    sqr_mark = {0: 0, 1: 0}
                    trq_mark = {0: 0, 1: 0}
                    qk_ve_mark = {0: 0, 1: 0}
                    qk_se_mark = {0: 0, 1: 0}

                    for tt in range(NT):
                        def _ln2(v, tt=tt):
                            p.w(v, "SE")
                            ln_stats(v, x_tm[:, tt, :], sq2[:, 0, :], nst)
                        on("vector", _ln2)
                        ln_apply(x_tm[:, tt, :], xnb[:, :], nst)

                        pe0 = p.cnt["PE"]
                        se0 = p.cnt["SE"]

                        def _trx(t, se0=se0):
                            p.w(t, "VE")
                            p.w(t, "SE")
                            for kc in range(16):
                                if kc >= 2:
                                    t.wait_ge(sems["SE"], se0 + kc - 1)
                                pst = (psA2, psB2)[kc % 2]
                                p.inc(t.transpose(pst[:, :128], xnb[:, kc * 128:(kc + 1) * 128],
                                                  ident[:, :]), "PE")
                        on("tensor", _trx)

                        def _evx(s, pe0=pe0, tt=tt):
                            for kc in range(16):
                                pst = (psA2, psB2)[kc % 2]
                                s.wait_ge(sems["PE"], pe0 + kc + 1)
                                p.inc(s.copy(xnT[:, tt, kc, :], pst[:, :128]), "SE")
                        on("scalar", _evx)
                    se_xnT = p.cnt["SE"]

                    for hfh in range(2):
                        for dqg in (2 * hfh, 2 * hfh + 1):
                          for g0 in range(0, NT, TTA):
                           for sub in range(TTA // 2):
                            for k in range(16):
                                buf = k % 4

                                def _lw(y, k=k, buf=buf, dqg=dqg):
                                    if NC > 1:
                                        p.w(y, "CC", 5)
                                    else:
                                        p.w(y, "GPC", 64)
                                    p.w(y, "PE", pe_buf[buf])
                                    for wi, wt in enumerate((wq_bf, wk_bf)):
                                        sap = bass.AP(wt, k * 128 * D + dqg * 512, [[D, 128], [1, 512]])
                                        p.inc(y.dma_start(out=wqk[:, buf, wi, :], in_=sap), f"GW{buf}", 16)
                                on("sync", _lw)
                                gw_c = (f"GW{buf}", p.cnt[f"GW{buf}"])

                                def _qkm(t, k=k, buf=buf, gw_c=gw_c, dqg=dqg, g0=g0, sub=sub):
                                    p.w(t, gw_c[0], gw_c[1])
                                    p.w(t, "SE", se_xnT)
                                    if k == 0:
                                        if qk_ve_mark[sub] > 0:
                                            p.w(t, "VE", qk_ve_mark[sub])
                                        p.w(t, "SE", qk_se_mark[sub])
                                    mm = None
                                    for i in range(sub * 2, sub * 2 + 2):
                                        mm = t.matmul(ps[i][:, :], xnT[:, g0 + i, k, :], wqk[:, buf, 0, :],
                                                      start=(k == 0), stop=(k == 15))
                                        mm = t.matmul(ps[4 + i][:, :], xnT[:, g0 + i, k, :], wqk[:, buf, 1, :],
                                                      start=(k == 0), stop=(k == 15))
                                    p.inc(mm, "PE")
                                on("tensor", _qkm)
                                pe_buf[buf] = p.cnt["PE"]
                            pe_qk = p.cnt["PE"]

                            for i in range(sub * 2, sub * 2 + 2):
                                tt = g0 + i
                                for which in range(2):
                                    psu = ps[i] if which == 0 else ps[4 + i]
                                    cg_t = cgq if which == 0 else cgk
                                    sg_t = sgq if which == 0 else sgk
                                    dtm = qtm if which == 0 else ktm
                                    ch = which

                                    def _sq2(s, psu=psu, ch=ch):
                                        p.w(s, "PE", pe_qk)
                                        p.w(s, "SE", se_rope)
                                        p.w(s, "VE", ms_mark[ch])
                                        p.inc(s.activation(sq2[:, ch, :512], psu[:, :], AF.Square), "SE")
                                    on("scalar", _sq2)
                                    se_q2 = p.cnt["SE"]

                                    def _ms(v, se_q2=se_q2, ch=ch):
                                        p.w(v, "SE", se_q2)
                                        p.w(v, "SE", sqr_mark[ch])
                                        sv = sq2[:, ch, :512].rearrange("p (h d) -> p h d", h=4)
                                        p.inc(v.tensor_reduce(msq[:, ch, :], sv, axis=AX.X, op=ALU.add), "VE")
                                        p.w(v, "VE")
                                        p.inc(v.tensor_scalar(msq[:, ch, :], msq[:, ch, :], 1.0 / DH, RMS_EPS,
                                                              op0=ALU.mult, op1=ALU.add), "VE")
                                    on("vector", _ms)
                                    ms_mark[ch] = p.cnt["VE"]
                                    ve_m = p.cnt["VE"]

                                    def _sqr(s, ve_m=ve_m, ch=ch):
                                        p.w(s, "VE", ve_m)
                                        p.inc(s.activation(rstd[:, ch, :], msq[:, ch, :], AF.Sqrt), "SE")
                                    on("scalar", _sqr)
                                    sqr_mark[ch] = p.cnt["SE"]
                                    se_r2 = p.cnt["SE"]

                                    def _rr(v, se_r2=se_r2, psu=psu, cg_t=cg_t, sg_t=sg_t,
                                            dtm=dtm, dqg=dqg, ch=ch):
                                        p.w(v, "SE", se_r2)
                                        p.w(v, "PE", trq_mark[ch])
                                        p.inc(v.reciprocal(rstd[:, ch, :], rstd[:, ch, :]), "VE")
                                        p.w(v, "VE")
                                        sl = slice(dqg * 512, (dqg + 1) * 512)
                                        p.inc(v.tensor_tensor(asc[:, ch, :], psu[:, :], cg_t[:, sl], op=ALU.mult), "VE")
                                        swap = bass.AP(psu.handle if hasattr(psu, "handle") else psu,
                                                       1, [[512, 128], [2, 256], [-1, 2]])
                                        p.inc(v.scalar_tensor_tensor(
                                            bsc[:, ch, :].rearrange("p (a t) -> p a t", t=2), swap, 1.0,
                                            sg_t[:, sl].rearrange("p (a t) -> p a t", t=2),
                                            op0=ALU.mult, op1=ALU.mult), "VE")
                                        p.w(v, "VE")
                                        p.inc(v.tensor_tensor(asc[:, ch, :], asc[:, ch, :], bsc[:, ch, :], op=ALU.add), "VE")
                                        p.w(v, "VE")
                                        for h4 in range(4):
                                            p.inc(v.tensor_scalar(dtm[:, h4 * 128:(h4 + 1) * 128],
                                                                  asc[:, ch, h4 * 128:(h4 + 1) * 128],
                                                                  rstd[:, ch, h4:h4 + 1], None, ALU.mult), "VE")
                                    on("vector", _rr)
                                ve_rope = p.cnt["VE"]

                                scrA = ps[i][:, :].bitcast(BF16)
                                scrB = ps[4 + i][:, :].bitcast(BF16)
                                pe1 = p.cnt["PE"]
                                se1 = p.cnt["SE"]

                                def _trq(t, ve_rope=ve_rope, scrA=scrA, scrB=scrB, se1=se1):
                                    p.w(t, "VE", ve_rope)
                                    p.w(t, "SE")
                                    for j in range(8):
                                        if j >= 2:
                                            t.wait_ge(sems["SE"], se1 + j - 1)
                                        srcb = (qtm if j < 4 else ktm)[:, (j % 4) * 128:(j % 4 + 1) * 128]
                                        p.inc(t.transpose((scrA, scrB)[j % 2][:, :128], srcb, ident[:, :]), "PE")
                                on("tensor", _trq)
                                trq_mark[0] = p.cnt["PE"]
                                trq_mark[1] = p.cnt["PE"]

                                def _evq(s, pe1=pe1, i=i, tt=tt, dqg=dqg, scrA=scrA, scrB=scrB):
                                    p.w(s, "SY2")
                                    for j in range(8):
                                        s.wait_ge(sems["PE"], pe1 + j + 1)
                                        if j < 4:
                                            dst = QT[:, dqg * 4 + j, tt * 128:(tt + 1) * 128]
                                        else:
                                            dst = kst[:, j - 4, :]
                                        p.inc(s.copy(dst, (scrA, scrB)[j % 2][:, :128]), "SE")
                                on("scalar", _evq)
                                se_k = p.cnt["SE"]

                                def _ko(y, tt=tt, dqg=dqg, se_k=se_k):
                                    p.w(y, "SE", se_k)
                                    dst_ap = bass.AP(kT_lh[dqg // 2], ((dqg % 2) * 512) * L + tt * 128,
                                                     [[L, 128], [128 * L, 4], [1, 128]])
                                    p.inc(y.dma_start(out=dst_ap, in_=kst[:, :, :]), "SY2", 16)
                                on("sync", _ko)
                            qk_ve_mark[sub] = p.cnt["VE"]
                            qk_se_mark[sub] = p.cnt["SE"]

                        for dqg in (2 * hfh, 2 * hfh + 1):
                          for g0 in range(0, NT, TTA):
                            for k in range(16):
                                buf = k % 4

                                def _lv(y, k=k, buf=buf, dqg=dqg):
                                    if NC > 1:
                                        p.w(y, "CC", 6)
                                    else:
                                        p.w(y, "GPC", 64)
                                    p.w(y, "PE", pe_buf[buf])
                                    sap = bass.AP(wv_bf, k * 128 * D + dqg * 512, [[D, 128], [1, 512]])
                                    p.inc(y.dma_start(out=wqk[:, buf, 0, :], in_=sap), f"GW{buf}", 16)
                                on("sync", _lv)
                                gw_c = (f"GW{buf}", p.cnt[f"GW{buf}"])

                                def _vmm(t, k=k, buf=buf, gw_c=gw_c, g0=g0):
                                    p.w(t, gw_c[0], gw_c[1])
                                    p.w(t, "SE", se_xnT)
                                    if k == 0:
                                        p.w(t, "VE")
                                        p.w(t, "SE")
                                    mm = None
                                    for i in range(TTA):
                                        mm = t.matmul(ps[i][:, :], xnT[:, g0 + i, k, :], wqk[:, buf, 0, :],
                                                      start=(k == 0), stop=(k == 15))
                                    p.inc(mm, "PE")
                                on("tensor", _vmm)
                                pe_buf[buf] = p.cnt["PE"]
                            pe_v = p.cnt["PE"]

                            for i in range(TTA):
                                tt = g0 + i

                                def _vev(s, i=i, pe_v=pe_v):
                                    p.w(s, "PE", pe_v)
                                    p.w(s, "SY2")
                                    p.inc(s.copy(vstg[:, :], ps[i][:, :]), "SE")
                                on("scalar", _vev)
                                se_v = p.cnt["SE"]

                                def _vo(y, tt=tt, dqg=dqg, se_v=se_v):
                                    p.w(y, "SE", se_v)
                                    hf = dqg // 2
                                    p.inc(y.dma_start(out=v_lh[hf].ap()[tt * 128:(tt + 1) * 128,
                                                                        (dqg % 2) * 512:(dqg % 2 + 1) * 512],
                                                      in_=vstg[:, :]), "SY2", 16)
                                on("sync", _vo)

                        if NC > 1:
                            def _gkv(g, hfh=hfh):
                                p.w(g, "SY2")
                                rg = [list(range(NC))]
                                p.inc(g.collective_compute("AllGather", ALU.bypass, replica_groups=rg,
                                                           ins=[kT_lh[hfh].ap().opt()],
                                                           outs=[kT_agh[hfh].ap().opt()]), "CC")
                                p.inc(g.collective_compute("AllGather", ALU.bypass, replica_groups=rg,
                                                           ins=[v_lh[hfh].ap().opt()],
                                                           outs=[v_agh[hfh].ap().opt()]), "CC")
                            on("gpsimd", _gkv)
                    flush()
                barrier()

                with contextlib.ExitStack() as fl:
                    kbuf = fl.enter_context(nc_.sbuf_tensor("kbuf", [128, 3, L], BF16))
                    vbuf = fl.enter_context(nc_.sbuf_tensor("vbuf", [128, 3, NT, 128], BF16))
                    pbuf = fl.enter_context(nc_.sbuf_tensor("pbuf", [128, 3, L], BF16))
                    tbuf = fl.enter_context(nc_.sbuf_tensor("tbuf", [128, 3, L], F32))
                    paccum = fl.enter_context(nc_.sbuf_tensor("paccum", [128, 2, L], F32))
                    cmask = fl.enter_context(nc_.sbuf_tensor("cmask", [128, NTG, 128], BF16))
                    lr_sb = fl.enter_context(nc_.sbuf_tensor("lr_sb", [1, L], F32))
                    ao_raw = fl.enter_context(nc_.sbuf_tensor("ao_raw", [128, L], BF16))
                    ps_sT = [fl.enter_context(nc_.psum_tensor(f"ps_sT{i}", [128, L], F32)) for i in range(2)]
                    ps_oT = [fl.enter_context(nc_.psum_tensor(f"ps_oT{i}", [128, L], F32)) for i in range(2)]

                    def _ldm(y):
                        p.inc(y.dma_start(out=cmask[:, :, :],
                                          in_=cmask_in.ap().rearrange("p (g q) -> p g q", g=NTG)), "SY", 16)
                    on("sync", _ldm)
                    sy_m = p.cnt["SY"]

                    sy_kv = {}
                    se_tanh = {}
                    se_exp = {}
                    pe_av = {}
                    pe_av_kv = {}
                    ve_pacc = {}
                    last_g = NTG - 1
                    pidx = 0
                    av_q = []
                    pending_epi1 = None
                    epi_q = []
                    ve_gate = 0
                    se_gate = 0
                    se_ao_cur = 0
                    pe_lmm_mark = {0: 0, 1: 0}
                    sea_mark = {0: 0, 1: 0}
                    veg_mark = {0: 0, 1: 0}

                    NT2 = NT // 2
                    for h in range(NH):
                        for r in range(NC):
                            kvidx = h * NC + r
                            kvp = kvidx % 3

                            def _lkv(y, h=h, r=r, kvidx=kvidx, kvp=kvp):
                                hf, hh = h // 8, h % 8
                                if NC > 1:
                                    p.w(y, "CC", 9 + 2 * hf)
                                if kvidx >= 3:
                                    p.w(y, "PE", pe_av_kv[kvidx - 3])
                                ksrc = kT_agh[hf].ap()[r * HD2 + hh * 128:r * HD2 + (hh + 1) * 128, :]
                                p.inc(y.dma_start(out=kbuf[:, kvp, :], in_=ksrc), f"SYK{kvp}", 16)
                                for vh in range(2):
                                    vsrc = bass.AP(v_agh[hf],
                                                   (r * L + vh * NT2 * 128) * HD2 + hh * 128,
                                                   [[HD2, 128], [128 * HD2, NT2], [1, 128]])
                                    p.inc(y.dma_start(out=vbuf[:, kvp, vh * NT2:(vh + 1) * NT2, :],
                                                      in_=vsrc), f"SYK{kvp}", 16)
                            on("sync", _lkv)
                            sy_kv[kvidx] = (f"SYK{kvp}", p.cnt[f"SYK{kvp}"])

                            for s_ in range(NT):
                                g = s_ * NC + r
                                i0 = _i0_of(g, NC)
                                lo = i0 * 128
                                par = pidx % 2
                                pa3 = pidx % 3
                                hp = h % 2

                                def _qkm(t, r=r, s_=s_, h=h, lo=lo, par=par, kvidx=kvidx,
                                         pidx=pidx, kvp=kvp, g=g):
                                    p.w(t, sy_kv[kvidx][0], sy_kv[kvidx][1])
                                    p.w(t, "SY", sy_m)
                                    if pidx - 2 >= 0 and (pidx - 2) in se_tanh:
                                        p.w(t, "SE", se_tanh[pidx - 2])
                                    mm = None
                                    for (c0, c1) in _chunks512(lo, L):
                                        mm = t.matmul(ps_sT[par][:, c0:c1],
                                                      kbuf[:, kvp, s_ * 128:(s_ + 1) * 128],
                                                      QT[:, h, c0:c1], start=True, stop=True,
                                                      skip_group_check=True)
                                    mm = t.matmul(ps_sT[par][:, lo:lo + 128], ident[:, :],
                                                  cmask[:, g, :], start=False, stop=True,
                                                  skip_group_check=True)
                                    p.inc(mm, "PE")
                                on("tensor", _qkm)
                                pe_here = p.cnt["PE"]

                                if pending_epi1 is None and epi_q:
                                    epi_q.pop(0)()
                                while len(av_q) >= 2:
                                    av_q.pop(0)()

                                def _th(s, par=par, pa3=pa3, lo=lo, pe_here=pe_here, pidx=pidx):
                                    p.w(s, "PE", pe_here)
                                    p.inc(s.activation(tbuf[:, pa3, lo:L], ps_sT[par][:, lo:L],
                                                       AF.Tanh, scale=TANH_SCALE), "SE")
                                on("scalar", _th)
                                se_tanh[pidx] = p.cnt["SE"]
                                se_t = p.cnt["SE"]

                                def _ex(s, pa3=pa3, lo=lo, se_t=se_t, pidx=pidx):
                                    p.w(s, "SE", se_t)
                                    if pidx - 3 >= 0 and (pidx - 3) in pe_av:
                                        p.w(s, "PE", pe_av[pidx - 3])
                                    if pidx - 3 >= 0 and (pidx - 3) in ve_pacc:
                                        p.w(s, "VE", ve_pacc[pidx - 3])
                                    p.inc(s.activation(pbuf[:, pa3, lo:L], tbuf[:, pa3, lo:L],
                                                       AF.Exp, scale=50.0), "SE")
                                on("scalar", _ex)
                                se_exp[pidx] = p.cnt["SE"]
                                se_e = p.cnt["SE"]

                                if pending_epi1 is not None:
                                    pending_epi1()
                                    pending_epi1 = None

                                def _pa(v, g=g, lo=lo, pa3=pa3, se_e=se_e, pidx=pidx, hp=hp):
                                    p.w(v, "SE", se_e)
                                    if pidx - 1 >= 0 and (pidx - 1) in ve_pacc:
                                        p.w(v, "VE", ve_pacc[pidx - 1])
                                    if g == 0 and pe_lmm_mark[hp] > 0:
                                        p.w(v, "PE", pe_lmm_mark[hp])
                                    if g == 0:
                                        p.inc(v.tensor_copy(paccum[:, hp, :], pbuf[:, pa3, :]), "VE")
                                    else:
                                        p.inc(v.tensor_tensor(paccum[:, hp, lo:L], paccum[:, hp, lo:L],
                                                              pbuf[:, pa3, lo:L], op=ALU.add), "VE")
                                on("vector", _pa)
                                ve_pacc[pidx] = p.cnt["VE"]

                                def make_av(r=r, s_=s_, g=g, lo=lo, pa3=pa3, se_e=se_e, hp=hp,
                                            pidx=pidx, kvidx=kvidx, kvp=kvp, first_of_head=(g == 0)):
                                    def _av(t):
                                        p.w(t, "SE", se_e)
                                        if first_of_head:
                                            if sea_mark[hp] > 0:
                                                p.w(t, "SE", sea_mark[hp])
                                            if veg_mark[hp] > 0:
                                                p.w(t, "VE", veg_mark[hp])
                                        mm = None
                                        for (c0, c1) in _chunks512(lo, L):
                                            mm = t.matmul(ps_oT[hp][:, c0:c1], vbuf[:, kvp, s_, :],
                                                          pbuf[:, pa3, c0:c1],
                                                          start=(g == 0), stop=(g == last_g),
                                                          skip_group_check=True)
                                        p.inc(mm, "PE")
                                        pe_av[pidx] = p.cnt["PE"]
                                        pe_av_kv[kvidx] = p.cnt["PE"]
                                    def emit():
                                        on("tensor", _av)
                                    return emit
                                av_q.append(make_av())
                                pidx += 1

                        for _e in av_q:
                            _e()
                        av_q = []
                        pe_hd0 = p.cnt["PE"]
                        ve_pl = ve_pacc[pidx - 1]

                        def make_epi1(pe_hd0=pe_hd0, hp=hp):
                            def emit():
                                nonlocal se_ao_cur

                                def _aoe(s, pe_hd0=pe_hd0, hp=hp):
                                    p.w(s, "PE", pe_hd0)
                                    if ve_gate > 0:
                                        p.w(s, "VE", ve_gate)
                                    p.inc(s.copy(ao_raw[:, :], ps_oT[hp][:, :]), "SE")
                                on("scalar", _aoe)
                                se_ao_cur = p.cnt["SE"]
                                sea_mark[hp] = p.cnt["SE"]
                            return emit

                        epi_state = {}

                        def make_epi2a(hp=hp, ve_pl=ve_pl, epi_state=epi_state):
                            def emit():
                                se_ao = se_ao_cur
                                epi_state["se_ao"] = se_ao

                                def _lmm(t, ve_pl=ve_pl, se_ao=se_ao, hp=hp):
                                    p.w(t, "SE", se_ao)
                                    p.w(t, "VE", ve_pl)
                                    mm = None
                                    for (c0, c1) in _chunks512(0, L):
                                        mm = t.matmul(ps_oT[hp][0:1, c0:c1], ones_cf[:, :],
                                                      paccum[:, hp, c0:c1],
                                                      start=True, stop=True, skip_group_check=True)
                                    p.inc(mm, "PE")
                                on("tensor", _lmm)
                                pe_lmm_mark[hp] = p.cnt["PE"]
                                pe_lmm = p.cnt["PE"]

                                def _lcp(s, pe_lmm=pe_lmm, hp=hp):
                                    p.w(s, "PE", pe_lmm)
                                    p.w(s, "VE", ve_gate)
                                    p.inc(s.copy(lr_sb[:, :], ps_oT[hp][0:1, :]), "SE")
                                on("scalar", _lcp)
                                se_lcp = p.cnt["SE"]

                                def _lev(v, se_lcp=se_lcp):
                                    p.w(v, "SE", se_lcp)
                                    p.inc(v.reciprocal(lr_sb[:, :], lr_sb[:, :]), "VE")
                                on("vector", _lev)
                                epi_state["ve_l"] = p.cnt["VE"]
                            return emit

                        def make_epi2b(h=h, hp=hp, epi_state=epi_state):
                            def emit():
                                nonlocal ve_gate, se_gate
                                se_ao = epi_state["se_ao"]
                                ve_l = epi_state["ve_l"]

                                def _lbc(t, ve_l=ve_l, hp=hp):
                                    p.w(t, "VE", ve_l)
                                    mm = None
                                    for (c0, c1) in _chunks512(0, L):
                                        mm = t.matmul(ps_oT[hp][:, c0:c1], ones_rf[:, :], lr_sb[:, c0:c1],
                                                      start=True, stop=True, skip_group_check=True)
                                    p.inc(mm, "PE")
                                on("tensor", _lbc)
                                pe_lb = p.cnt["PE"]

                                def _aom(v, h=h, se_ao=se_ao, pe_lb=pe_lb, hp=hp):
                                    p.w(v, "SE", se_ao)
                                    p.w(v, "PE", pe_lb)
                                    p.inc(v.tensor_tensor(attno[:, h, :], ao_raw[:, :],
                                                          ps_oT[hp][:, :], op=ALU.mult), "VE")
                                on("vector", _aom)
                                ve_gate = p.cnt["VE"]
                                se_gate = p.cnt["SE"]
                                veg_mark[hp] = p.cnt["VE"]
                            return emit
                        pending_epi1 = make_epi1()
                        epi_q.append(make_epi2a())
                        epi_q.append(make_epi2b())
                    if pending_epi1 is not None:
                        pending_epi1()
                        pending_epi1 = None
                    while epi_q:
                        epi_q.pop(0)()
                    flush()
                barrier()

                with contextlib.ExitStack() as os_:
                    wos = os_.enter_context(nc_.sbuf_tensor("wos", [128, 2, 1024], BF16))
                    ps_o = [os_.enter_context(nc_.psum_tensor(f"ops{i}", [128, 512], F32))
                            for i in range(8)]
                    TTO = min(4, NT)

                    pe_bufo = {0: 0, 1: 0}
                    for fh in range(2):
                        for g0 in range(0, NT, TTO):
                            for kc in range(16):
                                buf = kc % 2

                                def _lwo(y, kc=kc, buf=buf, fh=fh):
                                    if NC > 1:
                                        p.w(y, "CC", 7)
                                    else:
                                        p.w(y, "GPC", 64)
                                    if pe_bufo[buf] == 0:
                                        p.w(y, "PE")
                                    else:
                                        p.w(y, "PE", pe_bufo[buf])
                                    sap = bass.AP(wo_bf, kc * 128 * D + fh * 1024, [[D, 128], [1, 1024]])
                                    p.inc(y.dma_start(out=wos[:, buf, :], in_=sap), f"GW{buf}", 16)
                                on("sync", _lwo)
                                gw_c = (f"GW{buf}", p.cnt[f"GW{buf}"])

                                def _om(t, kc=kc, buf=buf, gw_c=gw_c, g0=g0):
                                    p.w(t, gw_c[0], gw_c[1])
                                    if kc == 0:
                                        p.w(t, "VE")
                                    mm = None
                                    for i in range(TTO):
                                        tt = g0 + i
                                        mm = t.matmul(ps_o[2 * i][:, :], attno[:, kc, tt * 128:(tt + 1) * 128],
                                                      wos[:, buf, 0:512], start=(kc == 0), stop=(kc == 15))
                                        mm = t.matmul(ps_o[2 * i + 1][:, :], attno[:, kc, tt * 128:(tt + 1) * 128],
                                                      wos[:, buf, 512:1024], start=(kc == 0), stop=(kc == 15))
                                    p.inc(mm, "PE")
                                on("tensor", _om)
                                pe_bufo[buf] = p.cnt["PE"]
                            pe_o = p.cnt["PE"]

                            for i in range(TTO):
                                tt = g0 + i

                                def _ores(v, i=i, tt=tt, fh=fh, pe_o=pe_o):
                                    p.w(v, "PE", pe_o)
                                    for cb in range(2):
                                        c0 = fh * 1024 + cb * 512
                                        p.inc(v.scalar_tensor_tensor(x_tm[:, tt, c0:c0 + 512],
                                                                     ps_o[2 * i + cb][:, :], 1.0,
                                                                     x_tm[:, tt, c0:c0 + 512],
                                                                     op0=ALU.mult, op1=ALU.add), "VE")
                                on("vector", _ores)
                    flush()

        def write_x_out():
            with contextlib.ExitStack() as ws:
                stg = ws.enter_context(nc_.sbuf_tensor("stg", [128, D], F32))
                for tt in range(NT):
                    def _c(v, tt=tt):
                        p.w(v, "SY2")
                        p.inc(v.tensor_copy(stg[:, :], x_tm[:, tt, :]), "VE")
                    on("vector", _c)
                    ve_c = p.cnt["VE"]

                    def _o(y, tt=tt, ve_c=ve_c):
                        p.w(y, "VE", ve_c)
                        p.inc(y.dma_start(out=out_t.ap()[tt * 128:(tt + 1) * 128, :], in_=stg[:, :]), "SY2", 16)
                    on("sync", _o)
                flush()

        ffn_stage(first_ffn=True, write_out=False)
        barrier()
        if out_stage >= 2:
            attn_stage()
            barrier()
        if out_stage >= 3:
            ffn_stage(first_ffn=False, write_out=True)
        else:
            write_x_out()
        barrier()

        def _fin(y):
            p.w(y, "SY2")
            p.w(y, "SY")
            p.w(y, "GP")
        on("sync", _fin)
        flush()

    build_program.last_cnt = dict(p.cnt)
    return nc_


def _host_prep(inputs, T, NCORES):
    pi = _pi_src(); rho = _rho_src()
    x = np.asarray(inputs["L0_attn_x"]).reshape(-1, D)[:T]
    wq = np.asarray(inputs["L0_attn_wq"]); wk = np.asarray(inputs["L0_attn_wk"])
    wv = np.asarray(inputs["L0_attn_wv"]); wo = np.asarray(inputs["L0_attn_wo"])
    qnw = np.asarray(inputs["L0_attn_q_norm_w"]); knw = np.asarray(inputs["L0_attn_k_norm_w"])
    cos = np.asarray(inputs["L0_attn_rope_cos"]); sin = np.asarray(inputs["L0_attn_rope_sin"])
    gw = np.asarray(inputs["L0_ffn_gate_w"]); uw = np.asarray(inputs["L0_ffn_up_w"])
    dw = np.asarray(inputs["L0_ffn_down_w"])

    NTG = T // 128
    tiles = [_tiles_of(c, NTG, NCORES) for c in range(NCORES)]
    xp = np.ascontiguousarray(x[:, pi])
    wqT = np.ascontiguousarray(wq[:, pi].T); wkT = np.ascontiguousarray(wk[:, pi].T)
    wvT = np.ascontiguousarray(wv[:, pi].T)
    woT = np.ascontiguousarray(wo[pi, :].T)
    g_p = np.ascontiguousarray(gw[rho][:, pi]); u_p = np.ascontiguousarray(uw[rho][:, pi])
    d_p = np.ascontiguousarray(dw[pi][:, rho])
    R1 = I // NCORES; R2 = D // NCORES

    c16 = cos[:NH]; s16 = sin[:NH]
    crep = np.repeat(c16, 2, axis=1).reshape(-1)
    srep = np.repeat(s16, 2, axis=1).reshape(-1)
    sign = np.tile([-1.0, 1.0], D // 2)

    def mkrows(gv):
        gt = np.tile(gv, NH)
        gswap = gt.reshape(-1, 2)[:, ::-1].reshape(-1)
        cgr = (gt * crep).astype(np.float32)[None]
        sgr = (gswap * srep * sign).astype(np.float32)[None]
        return np.ascontiguousarray(cgr), np.ascontiguousarray(sgr)

    cg_q, sg_q = mkrows(qnw); cg_k, sg_k = mkrows(knw)
    ident = np.eye(128, dtype=BF)

    kk = np.arange(128)[:, None]; qq = np.arange(128)[None, :]
    diag = np.where(kk <= qq, 0.0, -3e9)
    keep = np.zeros((128, 128))
    full = np.full((128, 128), -3e9)

    in_maps = []
    for c in range(NCORES):
        rows = np.concatenate([np.arange(t * 128, (t + 1) * 128) for t in tiles[c]])
        cm = np.empty((128, NTG, 128), dtype=np.float32)
        for g in range(NTG):
            i0 = _i0_of(g, NCORES)
            G = NCORES * i0 + c
            cm[:, g, :] = keep if G > g else (diag if G == g else full)
        RS = D // NCORES
        in_maps.append({
            "x_loc": np.ascontiguousarray(xp[rows]),
            "wqT": np.ascontiguousarray(wqT[c * RS:(c + 1) * RS]),
            "wkT": np.ascontiguousarray(wkT[c * RS:(c + 1) * RS]),
            "wvT": np.ascontiguousarray(wvT[c * RS:(c + 1) * RS]),
            "woT": np.ascontiguousarray(woT[c * RS:(c + 1) * RS]),
            "gsh": np.ascontiguousarray(g_p[c * R1:(c + 1) * R1]),
            "ush": np.ascontiguousarray(u_p[c * R1:(c + 1) * R1]),
            "dsh": np.ascontiguousarray(d_p[c * R2:(c + 1) * R2]),
            "cg_q": cg_q, "sg_q": sg_q, "cg_k": cg_k, "sg_k": sg_k,
            "ident_in": ident,
            "cmask_in": np.ascontiguousarray(cm.reshape(128, NTG * 128).astype(BF)),
        })
    return in_maps, tiles, pi


def _host_finish(results, tiles, pi, T, NCORES):
    out = np.empty((T, D), dtype=np.float32)
    for c in range(NCORES):
        rows = np.concatenate([np.arange(t * 128, (t + 1) * 128) for t in tiles[c]])
        out[rows] = results[c]["out_loc"]
    inv = np.empty(D, dtype=np.int64); inv[pi] = np.arange(D)
    return np.ascontiguousarray(out[:, inv])[None]


def kernel(**inputs):
    T, NCORES = 8192, 8
    in_maps, tiles, pi = _host_prep(inputs, T, NCORES)
    nc_ = build_program(T, NCORES, out_stage=3)
    res = run_bass_kernel_spmd(nc_, in_maps, core_ids=list(range(NCORES)))
    return _host_finish(res.results, tiles, pi, T, NCORES)


# version 29
# speedup vs baseline: 1.0035x; 1.0035x over previous
import contextlib
import numpy as np
import ml_dtypes
import concourse.bass as bass
import concourse.mybir as mybir
from concourse.bass_utils import run_bass_kernel_spmd

F32 = mybir.dt.float32
BF16 = mybir.dt.bfloat16
FP8 = mybir.dt.float8e4
AF = mybir.ActivationFunctionType
ALU = mybir.AluOpType
AX = mybir.AxisListType
BF = ml_dtypes.bfloat16

D = 2048
NH, DH = 16, 128
I = 8192
BLK = 32
NBD = D // BLK
NBI = I // BLK
QCLAMP = 448.0 * 1e-30
LN_EPS = 1e-5
RMS_EPS = 1e-6
TANH_SCALE = float((DH ** (-0.5)) / 50.0)


def _pi_src():
    pos = np.arange(D)
    return (32 * (pos % NBD) + pos // NBD).astype(np.int64)


def _rho_src():
    pos = np.arange(I)
    lf = pos % 2048
    return ((pos // 2048) * 2048 + 32 * (lf % 64) + lf // 64).astype(np.int64)


def _tiles_of(c, ntg, nc):
    return list(range(c, ntg, nc))


def _i0_of(g, nc):
    return max(0, -((-(g - (nc - 1))) // nc))


def _chunks512(lo, hi):
    out = []
    while lo < hi:
        nxt = min(hi, (lo // 512 + 1) * 512)
        out.append((lo, nxt))
        lo = nxt
    return out


def build_program(T, NC, out_stage=3):
    L = T // NC
    NT = L // 128
    NTG = T // 128
    assert T % (128 * NC) == 0 and NT >= 2 and NT % 2 == 0
    R1 = I // NC
    R2 = D // NC
    assert R1 % 128 == 0 and R2 % 128 == 0
    TTG = min(4, NT)
    HALF = TTG // 2

    nc_ = bass.Bass("TRN2", target_bir_lowering=False)

    def din(name, shape, dt=F32):
        return nc_.dram_tensor(name, list(shape), dt, kind="ExternalInput")

    x_in = din("x_loc", [L, D])
    wqT_d = din("wqT", [D // NC, D]); wkT_d = din("wkT", [D // NC, D])
    wvT_d = din("wvT", [D // NC, D])
    woT_d = din("woT", [D // NC, D])
    gsh = din("gsh", [R1, D]); ush = din("ush", [R1, D]); dsh = din("dsh", [R2, I])
    cg_q_in = din("cg_q", [1, D]); sg_q_in = din("sg_q", [1, D])
    cg_k_in = din("cg_k", [1, D]); sg_k_in = din("sg_k", [1, D])
    ident_in = din("ident_in", [128, 128], BF16)
    cmask_in = din("cmask_in", [128, NTG * 128], BF16)
    out_t = nc_.dram_tensor("out_loc", [L, D], F32, kind="ExternalOutput")

    wq_l = nc_.dram_tensor("wq_l", [D // NC, D], BF16)
    wk_l = nc_.dram_tensor("wk_l", [D // NC, D], BF16)
    wv_l = nc_.dram_tensor("wv_l", [D // NC, D], BF16)
    wo_l = nc_.dram_tensor("wo_l", [D // NC, D], BF16)
    gu_g_l = nc_.dram_tensor("gu_g_l", [D, R1], BF16)
    gu_u_l = nc_.dram_tensor("gu_u_l", [D, R1], BF16)
    ddqT_l = nc_.dram_tensor("ddqT_l", [I, R2], BF16)
    HD2 = D // 2
    kT_lh = [nc_.dram_tensor(f"kT_l{a}", [HD2, L], BF16) for a in range(2)]
    v_lh = [nc_.dram_tensor(f"v_l{a}", [L, HD2], BF16) for a in range(2)]
    _AS = "Shared" if NC > 4 else "Local"
    if NC > 1:
        gudqT_g = nc_.dram_tensor("gudqT_g", [NC * D, R1], BF16, addr_space=_AS)
        gudqT_u = nc_.dram_tensor("gudqT_u", [NC * D, R1], BF16, addr_space=_AS)
        ddqT = nc_.dram_tensor("ddqT", [NC * I, R2], BF16, addr_space=_AS)
        wq_bf = nc_.dram_tensor("wq_bf", [D, D], BF16, addr_space=_AS)
        wk_bf = nc_.dram_tensor("wk_bf", [D, D], BF16, addr_space=_AS)
        wv_bf = nc_.dram_tensor("wv_bf", [D, D], BF16, addr_space=_AS)
        wo_bf = nc_.dram_tensor("wo_bf", [D, D], BF16, addr_space=_AS)
        kT_agh = [nc_.dram_tensor(f"kT_ag{a}", [NC * HD2, L], BF16, addr_space=_AS)
                  for a in range(2)]
        v_agh = [nc_.dram_tensor(f"v_ag{a}", [NC * L, HD2], BF16, addr_space=_AS)
                 for a in range(2)]
    else:
        gudqT_g, gudqT_u, ddqT = gu_g_l, gu_u_l, ddqT_l
        wq_bf, wk_bf, wv_bf, wo_bf = wq_l, wk_l, wv_l, wo_l
        kT_agh, v_agh = kT_lh, v_lh

    st = contextlib.ExitStack()
    sems = {}
    for s in ["SY", "GP", "CC", "PE", "VE", "SE", "SY2", "SYX",
              "SYW0", "SYW1", "SYW2", "SYW3", "SYW4", "SYW5", "SYW6", "SYW7",
              "SYD0", "SYD1", "SYD2", "SYD3", "SYK0", "SYK1", "SYK2",
              "GW0", "GW1", "GW2", "GW3", "GW4", "GW5", "GW6", "GW7",
              "GE", "SYDQ0", "SYDQ1", "SYDQ2", "SYDQ3",
              "SYO0", "SYO1", "GPC", "GPX"]:
        sems[s] = st.enter_context(nc_.semaphore(s))

    class Px:
        def __init__(self):
            self.cnt = {k: 0 for k in sems}

        def inc(self, instr, sem, n=1):
            instr.then_inc(sems[sem], n)
            self.cnt[sem] += n
            return self.cnt[sem]

        def w(self, eng, sem, val=None):
            v = self.cnt[sem] if val is None else val
            if v > 0:
                eng.wait_ge(sems[sem], v)

    p = Px()

    with st:
        x_tm = st.enter_context(nc_.sbuf_tensor("x_tm", [128, NT, D], BF16))
        ident = st.enter_context(nc_.sbuf_tensor("ident", [128, 128], BF16))
        ones_cb = st.enter_context(nc_.sbuf_tensor("ones_cb", [128, 1], BF16))
        ones_cf = st.enter_context(nc_.sbuf_tensor("ones_cf", [128, 1], F32))
        ones_rf = st.enter_context(nc_.sbuf_tensor("ones_rf", [1, 128], F32))
        cst_eps = st.enter_context(nc_.sbuf_tensor("cst_eps", [128, 1], F32))
        nc_.const_aps.register(F32, LN_EPS, cst_eps[:, :])
        block = st.enter_context(nc_.Block())

        ENGLOG = {e: [] for e in ("tensor", "vector", "scalar", "gpsimd", "sync")}

        class _IP:
            __slots__ = ("incs",)
            def __init__(self):
                self.incs = []
            def then_inc(self, sem, n=1):
                self.incs.append((sem, n))
                return self

        class _EP:
            __slots__ = ("log",)
            def __init__(self, log):
                self.log = log
            def __getattr__(self, meth):
                log = self.log
                def call(*a, **k):
                    ip = _IP()
                    log.append((meth, a, k, ip))
                    return ip
                return call

        def on(engine_name, fn):
            fn(_EP(ENGLOG[engine_name]))

        def flush():
            for name in ("sync", "gpsimd", "tensor", "vector", "scalar"):
                log = ENGLOG[name]
                if not log:
                    continue
                ENGLOG[name] = []
                def _replay(eng, log=log):
                    for meth, a, k, ip in log:
                        r = getattr(eng, meth)(*a, **k)
                        for (sem, n) in ip.incs:
                            r.then_inc(sem, n)
                getattr(block, name)(_replay)

        def barrier():
            snap = {k: v for k, v in p.cnt.items() if k not in ("GPC", "CC")}
            for en in ["tensor", "vector", "scalar", "gpsimd", "sync"]:
                def _f(e, snap=snap):
                    for sn, v in snap.items():
                        if v > 0:
                            e.wait_ge(sems[sn], v)
                on(en, _f)
            flush()

        def _init_sync(y):
            p.inc(y.dma_start(out=ident[:, :], in_=ident_in.ap()), "SY", 16)
        on("sync", _init_sync)

        def _ldx_all(g):
            p.inc(g.dma_start(out=x_tm[:, :, :],
                              in_=x_in.ap().rearrange("(t p) d -> p t d", p=128)), "GPX", 16)
        on("gpsimd", _ldx_all)

        def _init_vec(v):
            v.memset(ones_cb[:, :], 1.0)
            v.memset(ones_cf[:, :], 1.0)
            v.memset(cst_eps[:, :], LN_EPS)
            p.inc(v.memset(ones_rf[:, :], 1.0), "VE")
        on("vector", _init_vec)

        def _cast_attn_w(g):
            for wsrc, wdst in ((wqT_d, wq_l), (wkT_d, wk_l), (wvT_d, wv_l), (woT_d, wo_l)):
                p.inc(g.dma_start(out=wdst.ap(), in_=wsrc.ap()), "GPC", 16)
        on("gpsimd", _cast_attn_w)

        with contextlib.ExitStack() as dq:
            wnat = dq.enter_context(nc_.sbuf_tensor("wnat", [128, 4, 2048], F32))
            wdq = dq.enter_context(nc_.sbuf_tensor("wdq", [128, 4, 2048], BF16))
            q8w = dq.enter_context(nc_.sbuf_tensor("q8w", [128, 4, 2048], FP8))
            bmw = dq.enter_context(nc_.sbuf_tensor("bmw", [128, 4, 64], F32))
            r2w = dq.enter_context(nc_.sbuf_tensor("r2w", [128, 4, 64], F32))
            s2w = dq.enter_context(nc_.sbuf_tensor("s2w", [128, 4, 64], F32))
            trs = dq.enter_context(nc_.sbuf_tensor("trs", [128, 2, 16, 512], BF16))
            dps = [dq.enter_context(nc_.psum_tensor(f"dps{i}", [128, 1024], BF16))
                   for i in range(2)]
            NBU = 2048 // BLK
            dq_state = {"i": 0,
                        "ve_mark": {b: 0 for b in range(4)},
                        "ge_mark": {b: 0 for b in range(4)},
                        "pending_dq2": None}

            def dequant_unit(in_ap, batch_out):
                b = dq_state["i"] % 4
                w = (dq_state["i"] // 4) % 2
                dq_state["i"] += 1

                def _load(y, b=b, in_ap=in_ap):
                    p.w(y, "VE", dq_state["ve_mark"][b])
                    p.w(y, "GE", dq_state["ge_mark"][b])
                    p.inc(y.dma_start(out=wnat[:, b, :], in_=in_ap), f"SYDQ{b}", 16)
                on("sync", _load)
                sy_h = p.cnt[f"SYDQ{b}"]

                def _scales(v, sy_h=sy_h, b=b):
                    p.w(v, f"SYDQ{b}", sy_h)
                    xv = wnat[:, b, :].rearrange("p (j bb) -> p bb j", bb=NBU, j=BLK)
                    p.inc(v.tensor_reduce(bmw[:, b, :], xv, axis=AX.X, op=ALU.max,
                                          apply_absolute_value=True), "VE")
                    p.w(v, "VE")
                    p.inc(v.tensor_scalar(bmw[:, b, :], bmw[:, b, :], QCLAMP, None, ALU.max), "VE")
                    p.w(v, "VE")
                    p.inc(v.reciprocal(r2w[:, b, :], bmw[:, b, :]), "VE")
                    p.w(v, "VE")
                    p.inc(v.tensor_scalar(r2w[:, b, :], r2w[:, b, :], 224.0, None, ALU.mult), "VE")
                    p.inc(v.tensor_scalar(s2w[:, b, :], bmw[:, b, :], 1.0 / 224.0, None, ALU.mult), "VE")
                on("vector", _scales)
                ve_s = p.cnt["VE"]

                def _y(g, b=b, ve_s=ve_s):
                    p.w(g, "VE", ve_s)
                    rb = r2w[:, b, :].unsqueeze(1).broadcast_to([128, BLK, NBU])
                    p.inc(g.tensor_tensor(q8w[:, b, :].rearrange("p (j bb) -> p j bb", bb=NBU),
                                          wnat[:, b, :].rearrange("p (j bb) -> p j bb", bb=NBU),
                                          rb, op=ALU.mult), "GE")
                on("gpsimd", _y)
                dq_state["ge_mark"][b] = p.cnt["GE"]
                ge_y = p.cnt["GE"]

                def tail(b=b, w=w, ge_y=ge_y, batch_out=batch_out):
                    def _dq2(v, b=b, ge_y=ge_y):
                        p.w(v, "GE", ge_y)
                        sb_ = s2w[:, b, :].unsqueeze(1).broadcast_to([128, BLK, NBU])
                        p.inc(v.tensor_tensor(wdq[:, b, :].rearrange("p (j bb) -> p j bb", bb=NBU),
                                              q8w[:, b, :].rearrange("p (j bb) -> p j bb", bb=NBU),
                                              sb_, op=ALU.mult), "VE")
                    on("vector", _dq2)
                    dq_state["ve_mark"][b] = p.cnt["VE"]
                    ve_dq = p.cnt["VE"]

                    pe0 = p.cnt["PE"]
                    se0 = p.cnt["SE"]

                    def _tr(t, ve_dq=ve_dq, se0=se0, b=b):
                        p.w(t, "SY")
                        p.w(t, "VE", ve_dq)
                        p.w(t, "SE")
                        for kc in range(16):
                            if kc >= 2:
                                t.wait_ge(sems["SE"], se0 + kc - 1)
                            p.inc(t.transpose(dps[kc % 2][:, :128],
                                              wdq[:, b, kc * 128:(kc + 1) * 128], ident[:, :]), "PE")
                    on("tensor", _tr)

                    def _ev(s, pe0=pe0, b=b, w=w):
                        p.w(s, f"SYO{w}")
                        for kc in range(16):
                            s.wait_ge(sems["PE"], pe0 + kc + 1)
                            p.inc(s.copy(trs[:, w, kc, b * 128:(b + 1) * 128], dps[kc % 2][:, :128]), "SE")
                    on("scalar", _ev)

                    if b == 3:
                        se_e = p.cnt["SE"]

                        def _out(y, se_e=se_e, w=w, batch_out=batch_out):
                            p.w(y, "SE", se_e)
                            for dst_ap, src_ap in batch_out(w):
                                p.inc(y.dma_start(out=dst_ap, in_=src_ap), f"SYO{w}", 16)
                        on("sync", _out)

                if dq_state["pending_dq2"] is not None:
                    dq_state["pending_dq2"]()
                dq_state["pending_dq2"] = tail

            def _emit_gather(ins_t, outs_t, pre=None):
                if dq_state["pending_dq2"] is not None:
                    dq_state["pending_dq2"]()
                    dq_state["pending_dq2"] = None
                if NC <= 1:
                    return

                def _g(g, ins_t=ins_t, outs_t=outs_t, pre=pre):
                    p.w(g, "SYO0")
                    p.w(g, "SYO1")
                    if pre is not None:
                        pre(g)
                    p.inc(g.collective_compute("AllGather", ALU.bypass,
                                               replica_groups=[list(range(NC))],
                                               ins=[ins_t.ap().opt()], outs=[outs_t.ap().opt()]), "CC")
                on("gpsimd", _g)

            for src_t, dst in ((gsh, gu_g_l), (ush, gu_u_l)):
                for rt in range(R1 // 128):
                    bo = None
                    if rt % 4 == 3:
                        def bo(w, rt0=rt - 3, dst=dst):
                            return [(bass.AP(dst, rt0 * 128, [[R1, 128], [128 * R1, 16], [1, 512]]),
                                     trs[:, w, :, :])]
                    dequant_unit(src_t.ap()[rt * 128:(rt + 1) * 128, :], bo)
                _emit_gather(dst, gudqT_g if dst is gu_g_l else gudqT_u)

            for rt in range(R2 // 128):
                for s4 in range(4):
                    bo = None
                    if s4 == 3:
                        def bo(w, rt=rt):
                            return [(bass.AP(ddqT_l, rt * 128 + q4 * (16 * 128 * R2),
                                             [[R2, 128], [128 * R2, 16], [1, 128]]),
                                     trs[:, w, :, q4 * 128:(q4 + 1) * 128]) for q4 in range(4)]
                    dequant_unit(bass.AP(dsh, (rt * 128) * I + s4 * 2048, [[I, 128], [1, 2048]]), bo)
            _emit_gather(ddqT_l, ddqT)

            def _pre_w(g):
                p.w(g, "GPC", 64)
            for lt, gt in ((wq_l, wq_bf), (wk_l, wk_bf), (wv_l, wv_bf), (wo_l, wo_bf)):
                _emit_gather(lt, gt, pre=_pre_w if lt is wq_l else None)
            flush()
        barrier()

        def ln_stats(v, src_ap, scr_ap, nst):
            p.inc(v.tensor_reduce(nst[:, 0:1], src_ap, axis=AX.X, op=ALU.add, negate=True), "VE")
            p.inc(v.scalar_tensor_tensor(scr_ap, src_ap, 1.0, src_ap,
                                         op0=ALU.mult, op1=ALU.mult, accum_out=nst[:, 1:2]), "VE")
            p.w(v, "VE")
            p.inc(v.tensor_scalar(nst[:, 0:1], nst[:, 0:1], 1.0 / D, None, ALU.mult), "VE")
            p.w(v, "VE")
            p.inc(v.tensor_tensor(nst[:, 2:3], nst[:, 0:1], nst[:, 0:1], op=ALU.mult), "VE")
            p.w(v, "VE")
            p.inc(v.scalar_tensor_tensor(nst[:, 3:4], nst[:, 1:2], 1.0 / D, nst[:, 2:3],
                                         op0=ALU.mult, op1=ALU.subtract), "VE")

        def ln_apply(src_ap, dst_ap, nst):
            ve_v = p.cnt["VE"]

            def _sq(s, ve_v=ve_v):
                p.w(s, "VE", ve_v)
                p.inc(s.activation(nst[:, 4:5], nst[:, 3:4], AF.Sqrt, bias=LN_EPS), "SE")
            on("scalar", _sq)
            se_s = p.cnt["SE"]

            def _xn(v, se_s=se_s):
                p.w(v, "SE", se_s)
                p.inc(v.reciprocal(nst[:, 5:6], nst[:, 4:5]), "VE")
                p.w(v, "VE")
                p.inc(v.tensor_scalar(dst_ap, src_ap, nst[:, 0:1], nst[:, 5:6],
                                      op0=ALU.add, op1=ALU.mult), "VE")
                p.w(v, "VE")
            on("vector", _xn)

        def ffn_stage(first_ffn, write_out):
            sfx = "_f1" if first_ffn else "_f3"
            with contextlib.ExitStack() as fs:
                xn = fs.enter_context(nc_.sbuf_tensor("xn" + sfx, [128, D], F32))
                nst = fs.enter_context(nc_.sbuf_tensor("nst" + sfx, [128, 8], F32))
                q8x = fs.enter_context(nc_.sbuf_tensor("q8x" + sfx, [128, D], FP8))
                xq_tm = fs.enter_context(nc_.sbuf_tensor("xq_tm" + sfx, [128, D], BF16))
                xqT = fs.enter_context(nc_.sbuf_tensor("xqT" + sfx, [128, TTG, 16, 128], BF16))
                bmx = fs.enter_context(nc_.sbuf_tensor("bmx" + sfx, [128, NBI], F32))
                r2x = fs.enter_context(nc_.sbuf_tensor("r2x" + sfx, [128, NBI], F32))
                s2x = fs.enter_context(nc_.sbuf_tensor("s2x" + sfx, [128, NBI], F32))
                hbuf = fs.enter_context(nc_.sbuf_tensor("hbuf" + sfx, [128, TTG, I], BF16))
                q8h = fs.enter_context(nc_.sbuf_tensor("q8h" + sfx, [128, I], FP8))
                guw = fs.enter_context(nc_.sbuf_tensor("guw" + sfx, [128, 4, 2, 4, 512], BF16))
                dnw = fs.enter_context(nc_.sbuf_tensor("dnw" + sfx, [128, 4, 1024], BF16))
                outs = fs.enter_context(nc_.sbuf_tensor("outs" + sfx, [128, D], F32))
                sg_ev = fs.enter_context(nc_.sbuf_tensor("sg_ev" + sfx, [128, 512], F32))
                ps_gu = [fs.enter_context(nc_.psum_tensor(f"psg{i}" + sfx, [128, 512], F32))
                         for i in range(8)]
                psA = ps_gu[6][:, :].bitcast(BF16)
                psB = ps_gu[7][:, :].bitcast(BF16)

                def quant_rowblock(src_ap, qdst_ap, dqdst_ap, nblk, feat):
                    nch = feat // 2048

                    def _st(v):
                        p.w(v, "GE")
                        xv = src_ap.rearrange("p (c j b) -> p c b j", c=nch, j=BLK, b=64)
                        bmv = bmx[:, :nblk].rearrange("p (c b) -> p c b", c=nch)
                        p.inc(v.tensor_reduce(bmv, xv, axis=AX.X, op=ALU.max,
                                              apply_absolute_value=True), "VE")
                        p.w(v, "VE")
                        p.inc(v.tensor_scalar(bmx[:, :nblk], bmx[:, :nblk], QCLAMP, None, ALU.max), "VE")
                        p.w(v, "VE")
                        p.inc(v.reciprocal(r2x[:, :nblk], bmx[:, :nblk]), "VE")
                        p.w(v, "VE")
                        p.inc(v.tensor_scalar(r2x[:, :nblk], r2x[:, :nblk], 224.0, None, ALU.mult), "VE")
                        p.inc(v.tensor_scalar(s2x[:, :nblk], bmx[:, :nblk], 1.0 / 224.0, None, ALU.mult), "VE")
                    on("vector", _st)
                    ve_s = p.cnt["VE"]

                    def _yd(v, ve_s=ve_s):
                        p.w(v, "VE", ve_s)
                        rb = (r2x[:, :nblk].rearrange("p (c b) -> p c b", c=nch)
                              .unsqueeze(2).broadcast_to([128, nch, BLK, 64]))
                        sb_ = (s2x[:, :nblk].rearrange("p (c b) -> p c b", c=nch)
                               .unsqueeze(2).broadcast_to([128, nch, BLK, 64]))
                        p.inc(v.tensor_tensor(qdst_ap.rearrange("p (c j b) -> p c j b", c=nch, j=BLK, b=64),
                                              src_ap.rearrange("p (c j b) -> p c j b", c=nch, j=BLK, b=64),
                                              rb, op=ALU.mult), "VE")
                        p.w(v, "VE")
                        p.inc(v.tensor_tensor(dqdst_ap.rearrange("p (c j b) -> p c j b", c=nch, j=BLK, b=64),
                                              qdst_ap.rearrange("p (c j b) -> p c j b", c=nch, j=BLK, b=64),
                                              sb_, op=ALU.mult), "VE")
                    on("vector", _yd)

                def chunk_transpose(src_ap, dst_fn, nkc):
                    pe0 = p.cnt["PE"]
                    se0 = p.cnt["SE"]

                    def _tr(t, se0=se0):
                        p.w(t, "VE")
                        p.w(t, "SE")
                        for kc in range(nkc):
                            if kc >= 2:
                                t.wait_ge(sems["SE"], se0 + kc - 1)
                            pst = (psA, psB)[kc % 2]
                            p.inc(t.transpose(pst[:, :128], src_ap[:, kc * 128:(kc + 1) * 128], ident[:, :]), "PE")
                    on("tensor", _tr)

                    def _ev(s):
                        for kc in range(nkc):
                            pst = (psA, psB)[kc % 2]
                            s.wait_ge(sems["PE"], pe0 + kc + 1)
                            p.inc(s.copy(dst_fn(kc), pst[:, :128]), "SE")
                    on("scalar", _ev)

                se_xqT_by_group = {}

                def prep_group(ttg0):
                    for i in range(TTG):
                        tt = ttg0 + i
                        if first_ffn:
                            def _wx(v, tt=tt):
                                p.w(v, "GPX", 16)
                            on("vector", _wx)

                        def _ln(v, tt=tt):
                            p.w(v, "SY2")
                            ln_stats(v, x_tm[:, tt, :], xn[:, :], nst)
                        on("vector", _ln)
                        ln_apply(x_tm[:, tt, :], xn[:, :], nst)
                        quant_rowblock(xn[:, :], q8x[:, :], xq_tm[:, :], NBD, D)
                        chunk_transpose(xq_tm[:, :], lambda kc, i=i: xqT[:, i, kc, :], 16)
                    se_xqT_by_group[ttg0] = p.cnt["SE"]

                ngroups = NT // TTG
                prep_group(0)
                for ttg0 in range(0, NT, TTG):
                    g_n = TTG
                    se_xqT = se_xqT_by_group[ttg0]

                    NCCG = I // 512
                    pe_after = {}
                    ve_evac = {0: 0, 1: 0}
                    pe_mm_h = {0: 0, 1: 0}
                    for ccg in range(NCCG):
                        rank = (ccg * 512) // R1
                        coff = (ccg * 512) % R1
                        for wi in range(2):
                            wt = gudqT_g if wi == 0 else gudqT_u
                            for kg in range(4):
                                def _ldw(y, rank=rank, coff=coff, kg=kg, ccg=ccg, wi=wi, wt=wt):
                                    if NC > 1:
                                        p.w(y, "CC", 1 + wi)
                                    key = (wi, ccg - 1, kg)
                                    if key in pe_after:
                                        p.w(y, "PE", pe_after[key])
                                    src_ap = bass.AP(wt, (rank * D + kg * 4 * 128) * R1 + coff,
                                                     [[R1, 128], [128 * R1, 4], [1, 512]])
                                    p.inc(y.dma_start(out=guw[:, kg, wi, :, :], in_=src_ap),
                                          f"SYW{kg + 4 * wi}", 16)
                                on("sync", _ldw)
                                sy_w = (f"SYW{kg + 4 * wi}", p.cnt[f"SYW{kg + 4 * wi}"])

                                for half in range(2):
                                    hb = half * 4
                                    tts = list(range(half * HALF, half * HALF + HALF))

                                    def _mm(t, kg=kg, sy_w=sy_w, ccg=ccg, tts=tts, hb=hb,
                                            half=half, wi=wi):
                                        p.w(t, sy_w[0], sy_w[1])
                                        p.w(t, "SE", se_xqT)
                                        if NC > 1:
                                            p.w(t, "CC", 1 + wi)
                                        if ccg == 0 and wi == 0 and kg == 0:
                                            p.w(t, "VE")
                                        if ccg > 0 and wi == 0 and kg == 0:
                                            p.w(t, "VE", ve_evac[half])
                                        mm = None
                                        for k4 in range(4):
                                            k = kg * 4 + k4
                                            for i_ in tts:
                                                mm = t.matmul(ps_gu[hb + 2 * (i_ - tts[0]) + wi][:, :],
                                                              xqT[:, i_, k, :],
                                                              guw[:, kg, wi, k4, :],
                                                              start=(k == 0), stop=(k == 15))
                                        p.inc(mm, "PE")
                                    on("tensor", _mm)
                                    if wi == 1 and kg == 3:
                                        pe_mm_h[half] = p.cnt["PE"]
                                pe_after[(wi, ccg, kg)] = p.cnt["PE"]
                        pe_mm = p.cnt["PE"]

                        for half in range(2):
                            hb = half * 4
                            for j, i_ in enumerate(range(half * HALF, half * HALF + HALF)):
                                def _si(s, hb=hb, j=j, pe_mm=pe_mm_h[half]):
                                    p.w(s, "PE", pe_mm)
                                    p.w(s, "VE")
                                    p.inc(s.activation(sg_ev[:, :], ps_gu[hb + 2 * j][:, :], AF.Silu), "SE")
                                on("scalar", _si)
                                se_g = p.cnt["SE"]

                                def _h(v, hb=hb, j=j, i_=i_, ccg=ccg, se_g=se_g):
                                    p.w(v, "SE", se_g)
                                    p.inc(v.tensor_tensor(hbuf[:, i_, ccg * 512:(ccg + 1) * 512],
                                                          sg_ev[:, :], ps_gu[hb + 2 * j + 1][:, :],
                                                          op=ALU.mult), "VE")
                                on("vector", _h)
                            ve_evac[half] = p.cnt["VE"]

                    for i_ in range(g_n):
                        quant_rowblock(hbuf[:, i_, :], q8h[:, :], hbuf[:, i_, :], NBI, I)
                        chunk_transpose(hbuf[:, i_, :],
                                        lambda kc, i_=i_: hbuf[:, i_, kc * 128:(kc + 1) * 128], 64)

                    if ttg0 + TTG < NT:
                        prep_group(ttg0 + TTG)

                    pe_after_d = {}
                    for fh in range(2):
                        for kc in range(64):
                            buf = kc % 4

                            def _ldd(y, fh=fh, kc=kc, buf=buf):
                                if NC > 1:
                                    p.w(y, "CC", 3)
                                key = fh * 64 + kc - 4
                                if key >= 0:
                                    p.w(y, "PE", pe_after_d[key])
                                c0 = fh * 1024
                                if R2 >= 1024:
                                    src_ap = bass.AP(ddqT, ((c0 // R2) * I + kc * 128) * R2 + (c0 % R2),
                                                     [[R2, 128], [1, 1024]])
                                    dst_ap = dnw[:, buf, :]
                                else:
                                    nr = 1024 // R2
                                    src_ap = bass.AP(ddqT, ((c0 // R2) * I + kc * 128) * R2,
                                                     [[R2, 128], [I * R2, nr], [1, R2]])
                                    dst_ap = dnw[:, buf, :].rearrange("p (n r) -> p n r", n=nr)
                                p.inc(y.dma_start(out=dst_ap, in_=src_ap), f"SYD{buf}", 16)
                            on("sync", _ldd)
                            sy_d = (f"SYD{buf}", p.cnt[f"SYD{buf}"])

                            def _dmm(t, kc=kc, buf=buf, sy_d=sy_d, fh=fh):
                                p.w(t, sy_d[0], sy_d[1])
                                if NC > 1:
                                    p.w(t, "CC", 3)
                                if kc == 0:
                                    p.w(t, "SE")
                                    p.w(t, "VE")
                                mm = None
                                for i_ in range(g_n):
                                    mm = t.matmul(ps_gu[i_ * 2][:, :], hbuf[:, i_, kc * 128:(kc + 1) * 128],
                                                  dnw[:, buf, 0:512], start=(kc == 0), stop=(kc == 63))
                                    mm = t.matmul(ps_gu[i_ * 2 + 1][:, :], hbuf[:, i_, kc * 128:(kc + 1) * 128],
                                                  dnw[:, buf, 512:1024], start=(kc == 0), stop=(kc == 63))
                                p.inc(mm, "PE")
                            on("tensor", _dmm)
                            pe_after_d[fh * 64 + kc] = p.cnt["PE"]
                        pe_dn = p.cnt["PE"]

                        for i_ in range(g_n):
                            tt = ttg0 + i_

                            def _res(v, i_=i_, tt=tt, fh=fh, pe_dn=pe_dn):
                                p.w(v, "PE", pe_dn)
                                if write_out:
                                    p.w(v, "SY2")
                                for cb in range(2):
                                    c0 = fh * 1024 + cb * 512
                                    dst = outs[:, c0:c0 + 512] if write_out else x_tm[:, tt, c0:c0 + 512]
                                    p.inc(v.scalar_tensor_tensor(dst, ps_gu[i_ * 2 + cb][:, :], 0.5,
                                                                 x_tm[:, tt, c0:c0 + 512],
                                                                 op0=ALU.mult, op1=ALU.add), "VE")
                            on("vector", _res)

                            if write_out:
                                ve_o = p.cnt["VE"]

                                def _sto(y, tt=tt, fh=fh, ve_o=ve_o):
                                    p.w(y, "VE", ve_o)
                                    c0 = fh * 1024
                                    p.inc(y.dma_start(out=out_t.ap()[tt * 128:(tt + 1) * 128, c0:c0 + 1024],
                                                      in_=outs[:, c0:c0 + 1024]), "SY2", 16)
                                on("sync", _sto)
                flush()

        TTA = min(4, NT)

        def attn_stage():
            with contextlib.ExitStack() as ats:
                QT = ats.enter_context(nc_.sbuf_tensor("QT", [128, NH, L], BF16))
                attno = ats.enter_context(nc_.sbuf_tensor("attno", [128, NH, L], BF16))

                with contextlib.ExitStack() as qs:
                    xnb = qs.enter_context(nc_.sbuf_tensor("xnb", [128, D], BF16))
                    xnT = qs.enter_context(nc_.sbuf_tensor("xnT", [128, NT, 16, 128], BF16))
                    sq2 = qs.enter_context(nc_.sbuf_tensor("sq2", [128, 2, D], F32))
                    nst = qs.enter_context(nc_.sbuf_tensor("nst2", [128, 8], F32))
                    cgq = qs.enter_context(nc_.sbuf_tensor("cgq", [128, D], BF16))
                    sgq = qs.enter_context(nc_.sbuf_tensor("sgq", [128, D], BF16))
                    cgk = qs.enter_context(nc_.sbuf_tensor("cgk", [128, D], BF16))
                    sgk = qs.enter_context(nc_.sbuf_tensor("sgk", [128, D], BF16))
                    wqk = qs.enter_context(nc_.sbuf_tensor("wqk", [128, 8, 2, 512], BF16))
                    msq = qs.enter_context(nc_.sbuf_tensor("msq", [128, 2, 4], F32))
                    rstd = qs.enter_context(nc_.sbuf_tensor("rstd", [128, 2, 4], F32))
                    qtm = qs.enter_context(nc_.sbuf_tensor("qtm", [128, 512], BF16))
                    ktm = qs.enter_context(nc_.sbuf_tensor("ktm", [128, 512], BF16))
                    asc = qs.enter_context(nc_.sbuf_tensor("asc", [128, 2, 512], BF16))
                    bsc = qs.enter_context(nc_.sbuf_tensor("bsc", [128, 2, 512], BF16))
                    kst = qs.enter_context(nc_.sbuf_tensor("kst", [128, 4, 128], BF16))
                    vstg = qs.enter_context(nc_.sbuf_tensor("vstg", [128, 512], BF16))
                    ps = [qs.enter_context(nc_.psum_tensor(f"aps{i}", [128, 512], F32))
                          for i in range(8)]

                    for ri, (rin, dstt) in enumerate(((cg_q_in, cgq), (sg_q_in, sgq),
                                                      (cg_k_in, cgk), (sg_k_in, sgk))):
                        def _lr(y, rin=rin):
                            p.w(y, "PE")
                            p.inc(y.dma_start(out=sq2[0:1, 0, :], in_=rin.ap()), "SYX", 16)
                        on("sync", _lr)
                        sy_r = p.cnt["SYX"]

                        def _b1(t, sy_r=sy_r):
                            p.w(t, "SYX", sy_r)
                            p.w(t, "VE")
                            p.w(t, "SE")
                            mm = None
                            for j in range(4):
                                mm = t.matmul(ps[j][:, :], ones_rf[:, :], sq2[0:1, 0, j * 512:(j + 1) * 512],
                                              start=True, stop=True)
                            p.inc(mm, "PE")
                        on("tensor", _b1)
                        pe_r = p.cnt["PE"]

                        def _c1(s, dstt=dstt, pe_r=pe_r):
                            p.w(s, "PE", pe_r)
                            for j in range(4):
                                p.inc(s.copy(dstt[:, j * 512:(j + 1) * 512], ps[j][:, :]), "SE")
                        on("scalar", _c1)
                    se_rope = p.cnt["SE"]

                    psA2 = ps[6][:, :].bitcast(BF16)
                    psB2 = ps[7][:, :].bitcast(BF16)
                    pe_buf = {i: 0 for i in range(8)}
                    ms_mark = {0: 0, 1: 0}
                    sqr_mark = {0: 0, 1: 0}
                    trq_mark = {0: 0, 1: 0}

                    for tt in range(NT):
                        def _ln2(v, tt=tt):
                            p.w(v, "SE")
                            ln_stats(v, x_tm[:, tt, :], sq2[:, 0, :], nst)
                        on("vector", _ln2)
                        ln_apply(x_tm[:, tt, :], xnb[:, :], nst)

                        pe0 = p.cnt["PE"]
                        se0 = p.cnt["SE"]

                        def _trx(t, se0=se0):
                            p.w(t, "VE")
                            p.w(t, "SE")
                            for kc in range(16):
                                if kc >= 2:
                                    t.wait_ge(sems["SE"], se0 + kc - 1)
                                pst = (psA2, psB2)[kc % 2]
                                p.inc(t.transpose(pst[:, :128], xnb[:, kc * 128:(kc + 1) * 128],
                                                  ident[:, :]), "PE")
                        on("tensor", _trx)

                        def _evx(s, pe0=pe0, tt=tt):
                            for kc in range(16):
                                pst = (psA2, psB2)[kc % 2]
                                s.wait_ge(sems["PE"], pe0 + kc + 1)
                                p.inc(s.copy(xnT[:, tt, kc, :], pst[:, :128]), "SE")
                        on("scalar", _evx)
                    se_xnT = p.cnt["SE"]

                    for hfh in range(2):
                        for dqg in (2 * hfh, 2 * hfh + 1):
                          for g0 in range(0, NT, TTA):
                            for k in range(16):
                                buf = k % 8

                                def _lw(y, k=k, buf=buf, dqg=dqg):
                                    if NC > 1:
                                        p.w(y, "CC", 5)
                                    else:
                                        p.w(y, "GPC", 64)
                                    p.w(y, "PE", pe_buf[buf])
                                    for wi, wt in enumerate((wq_bf, wk_bf)):
                                        sap = bass.AP(wt, k * 128 * D + dqg * 512, [[D, 128], [1, 512]])
                                        p.inc(y.dma_start(out=wqk[:, buf, wi, :], in_=sap), f"GW{buf}", 16)
                                on("sync", _lw)
                                gw_c = (f"GW{buf}", p.cnt[f"GW{buf}"])

                                def _qkm(t, k=k, buf=buf, gw_c=gw_c, dqg=dqg, g0=g0):
                                    p.w(t, gw_c[0], gw_c[1])
                                    p.w(t, "SE", se_xnT)
                                    if k == 0:
                                        p.w(t, "VE")
                                        p.w(t, "SE")
                                    mm = None
                                    for i in range(TTA):
                                        mm = t.matmul(ps[i][:, :], xnT[:, g0 + i, k, :], wqk[:, buf, 0, :],
                                                      start=(k == 0), stop=(k == 15))
                                        mm = t.matmul(ps[4 + i][:, :], xnT[:, g0 + i, k, :], wqk[:, buf, 1, :],
                                                      start=(k == 0), stop=(k == 15))
                                    p.inc(mm, "PE")
                                on("tensor", _qkm)
                                pe_buf[buf] = p.cnt["PE"]
                            pe_qk = p.cnt["PE"]

                            for i in range(TTA):
                                tt = g0 + i
                                for which in range(2):
                                    psu = ps[i] if which == 0 else ps[4 + i]
                                    cg_t = cgq if which == 0 else cgk
                                    sg_t = sgq if which == 0 else sgk
                                    dtm = qtm if which == 0 else ktm
                                    ch = which

                                    def _sq2(s, psu=psu, ch=ch):
                                        p.w(s, "PE", pe_qk)
                                        p.w(s, "SE", se_rope)
                                        p.w(s, "VE", ms_mark[ch])
                                        p.inc(s.activation(sq2[:, ch, :512], psu[:, :], AF.Square), "SE")
                                    on("scalar", _sq2)
                                    se_q2 = p.cnt["SE"]

                                    def _ms(v, se_q2=se_q2, ch=ch):
                                        p.w(v, "SE", se_q2)
                                        p.w(v, "SE", sqr_mark[ch])
                                        sv = sq2[:, ch, :512].rearrange("p (h d) -> p h d", h=4)
                                        p.inc(v.tensor_reduce(msq[:, ch, :], sv, axis=AX.X, op=ALU.add), "VE")
                                        p.w(v, "VE")
                                        p.inc(v.tensor_scalar(msq[:, ch, :], msq[:, ch, :], 1.0 / DH, RMS_EPS,
                                                              op0=ALU.mult, op1=ALU.add), "VE")
                                    on("vector", _ms)
                                    ms_mark[ch] = p.cnt["VE"]
                                    ve_m = p.cnt["VE"]

                                    def _sqr(s, ve_m=ve_m, ch=ch):
                                        p.w(s, "VE", ve_m)
                                        p.inc(s.activation(rstd[:, ch, :], msq[:, ch, :], AF.Sqrt), "SE")
                                    on("scalar", _sqr)
                                    sqr_mark[ch] = p.cnt["SE"]
                                    se_r2 = p.cnt["SE"]

                                    def _rr(v, se_r2=se_r2, psu=psu, cg_t=cg_t, sg_t=sg_t,
                                            dtm=dtm, dqg=dqg, ch=ch):
                                        p.w(v, "SE", se_r2)
                                        p.w(v, "PE", trq_mark[ch])
                                        p.inc(v.reciprocal(rstd[:, ch, :], rstd[:, ch, :]), "VE")
                                        p.w(v, "VE")
                                        sl = slice(dqg * 512, (dqg + 1) * 512)
                                        p.inc(v.tensor_tensor(asc[:, ch, :], psu[:, :], cg_t[:, sl], op=ALU.mult), "VE")
                                        swap = bass.AP(psu.handle if hasattr(psu, "handle") else psu,
                                                       1, [[512, 128], [2, 256], [-1, 2]])
                                        p.inc(v.scalar_tensor_tensor(
                                            bsc[:, ch, :].rearrange("p (a t) -> p a t", t=2), swap, 1.0,
                                            sg_t[:, sl].rearrange("p (a t) -> p a t", t=2),
                                            op0=ALU.mult, op1=ALU.mult), "VE")
                                        p.w(v, "VE")
                                        p.inc(v.tensor_tensor(asc[:, ch, :], asc[:, ch, :], bsc[:, ch, :], op=ALU.add), "VE")
                                        p.w(v, "VE")
                                        for h4 in range(4):
                                            p.inc(v.tensor_scalar(dtm[:, h4 * 128:(h4 + 1) * 128],
                                                                  asc[:, ch, h4 * 128:(h4 + 1) * 128],
                                                                  rstd[:, ch, h4:h4 + 1], None, ALU.mult), "VE")
                                    on("vector", _rr)
                                ve_rope = p.cnt["VE"]

                                scrA = ps[i][:, :].bitcast(BF16)
                                scrB = ps[4 + i][:, :].bitcast(BF16)
                                pe1 = p.cnt["PE"]
                                se1 = p.cnt["SE"]

                                def _trq(t, ve_rope=ve_rope, scrA=scrA, scrB=scrB, se1=se1):
                                    p.w(t, "VE", ve_rope)
                                    p.w(t, "SE")
                                    for j in range(8):
                                        if j >= 2:
                                            t.wait_ge(sems["SE"], se1 + j - 1)
                                        srcb = (qtm if j < 4 else ktm)[:, (j % 4) * 128:(j % 4 + 1) * 128]
                                        p.inc(t.transpose((scrA, scrB)[j % 2][:, :128], srcb, ident[:, :]), "PE")
                                on("tensor", _trq)
                                trq_mark[0] = p.cnt["PE"]
                                trq_mark[1] = p.cnt["PE"]

                                def _evq(s, pe1=pe1, i=i, tt=tt, dqg=dqg, scrA=scrA, scrB=scrB):
                                    p.w(s, "SY2")
                                    for j in range(8):
                                        s.wait_ge(sems["PE"], pe1 + j + 1)
                                        if j < 4:
                                            dst = QT[:, dqg * 4 + j, tt * 128:(tt + 1) * 128]
                                        else:
                                            dst = kst[:, j - 4, :]
                                        p.inc(s.copy(dst, (scrA, scrB)[j % 2][:, :128]), "SE")
                                on("scalar", _evq)
                                se_k = p.cnt["SE"]

                                def _ko(y, tt=tt, dqg=dqg, se_k=se_k):
                                    p.w(y, "SE", se_k)
                                    dst_ap = bass.AP(kT_lh[dqg // 2], ((dqg % 2) * 512) * L + tt * 128,
                                                     [[L, 128], [128 * L, 4], [1, 128]])
                                    p.inc(y.dma_start(out=dst_ap, in_=kst[:, :, :]), "SY2", 16)
                                on("sync", _ko)

                        for dqg in (2 * hfh, 2 * hfh + 1):
                          for g0 in range(0, NT, TTA):
                            for k in range(16):
                                buf = k % 8

                                def _lv(y, k=k, buf=buf, dqg=dqg):
                                    if NC > 1:
                                        p.w(y, "CC", 6)
                                    else:
                                        p.w(y, "GPC", 64)
                                    p.w(y, "PE", pe_buf[buf])
                                    sap = bass.AP(wv_bf, k * 128 * D + dqg * 512, [[D, 128], [1, 512]])
                                    p.inc(y.dma_start(out=wqk[:, buf, 0, :], in_=sap), f"GW{buf}", 16)
                                on("sync", _lv)
                                gw_c = (f"GW{buf}", p.cnt[f"GW{buf}"])

                                def _vmm(t, k=k, buf=buf, gw_c=gw_c, g0=g0):
                                    p.w(t, gw_c[0], gw_c[1])
                                    p.w(t, "SE", se_xnT)
                                    if k == 0:
                                        p.w(t, "VE")
                                        p.w(t, "SE")
                                    mm = None
                                    for i in range(TTA):
                                        mm = t.matmul(ps[i][:, :], xnT[:, g0 + i, k, :], wqk[:, buf, 0, :],
                                                      start=(k == 0), stop=(k == 15))
                                    p.inc(mm, "PE")
                                on("tensor", _vmm)
                                pe_buf[buf] = p.cnt["PE"]
                            pe_v = p.cnt["PE"]

                            for i in range(TTA):
                                tt = g0 + i

                                def _vev(s, i=i, pe_v=pe_v):
                                    p.w(s, "PE", pe_v)
                                    p.w(s, "SY2")
                                    p.inc(s.copy(vstg[:, :], ps[i][:, :]), "SE")
                                on("scalar", _vev)
                                se_v = p.cnt["SE"]

                                def _vo(y, tt=tt, dqg=dqg, se_v=se_v):
                                    p.w(y, "SE", se_v)
                                    hf = dqg // 2
                                    p.inc(y.dma_start(out=v_lh[hf].ap()[tt * 128:(tt + 1) * 128,
                                                                        (dqg % 2) * 512:(dqg % 2 + 1) * 512],
                                                      in_=vstg[:, :]), "SY2", 16)
                                on("sync", _vo)

                        if NC > 1:
                            def _gkv(g, hfh=hfh):
                                p.w(g, "SY2")
                                rg = [list(range(NC))]
                                p.inc(g.collective_compute("AllGather", ALU.bypass, replica_groups=rg,
                                                           ins=[kT_lh[hfh].ap().opt()],
                                                           outs=[kT_agh[hfh].ap().opt()]), "CC")
                                p.inc(g.collective_compute("AllGather", ALU.bypass, replica_groups=rg,
                                                           ins=[v_lh[hfh].ap().opt()],
                                                           outs=[v_agh[hfh].ap().opt()]), "CC")
                            on("gpsimd", _gkv)
                    flush()
                barrier()

                with contextlib.ExitStack() as fl:
                    kbuf = fl.enter_context(nc_.sbuf_tensor("kbuf", [128, 3, L], BF16))
                    vbuf = fl.enter_context(nc_.sbuf_tensor("vbuf", [128, 3, NT, 128], BF16))
                    pbuf = fl.enter_context(nc_.sbuf_tensor("pbuf", [128, 3, L], BF16))
                    tbuf = fl.enter_context(nc_.sbuf_tensor("tbuf", [128, 3, L], F32))
                    paccum = fl.enter_context(nc_.sbuf_tensor("paccum", [128, 2, L], F32))
                    cmask = fl.enter_context(nc_.sbuf_tensor("cmask", [128, NTG, 128], BF16))
                    lr_sb = fl.enter_context(nc_.sbuf_tensor("lr_sb", [1, L], F32))
                    ao_raw = fl.enter_context(nc_.sbuf_tensor("ao_raw", [128, L], BF16))
                    ps_sT = [fl.enter_context(nc_.psum_tensor(f"ps_sT{i}", [128, L], F32)) for i in range(2)]
                    ps_oT = [fl.enter_context(nc_.psum_tensor(f"ps_oT{i}", [128, L], F32)) for i in range(2)]

                    def _ldm(y):
                        p.inc(y.dma_start(out=cmask[:, :, :],
                                          in_=cmask_in.ap().rearrange("p (g q) -> p g q", g=NTG)), "SY", 16)
                    on("sync", _ldm)
                    sy_m = p.cnt["SY"]

                    sy_kv = {}
                    se_tanh = {}
                    se_exp = {}
                    pe_av = {}
                    pe_av_kv = {}
                    ve_pacc = {}
                    last_g = NTG - 1
                    pidx = 0
                    av_q = []
                    pending_epi1 = None
                    epi_q = []
                    ve_gate = 0
                    se_gate = 0
                    se_ao_cur = 0
                    pe_lmm_mark = {0: 0, 1: 0}
                    sea_mark = {0: 0, 1: 0}
                    veg_mark = {0: 0, 1: 0}

                    NT2 = NT // 2
                    for h in range(NH):
                        for r in range(NC):
                            kvidx = h * NC + r
                            kvp = kvidx % 3

                            def _lkv(y, h=h, r=r, kvidx=kvidx, kvp=kvp):
                                hf, hh = h // 8, h % 8
                                if NC > 1:
                                    p.w(y, "CC", 9 + 2 * hf)
                                if kvidx >= 3:
                                    p.w(y, "PE", pe_av_kv[kvidx - 3])
                                ksrc = kT_agh[hf].ap()[r * HD2 + hh * 128:r * HD2 + (hh + 1) * 128, :]
                                p.inc(y.dma_start(out=kbuf[:, kvp, :], in_=ksrc), f"SYK{kvp}", 16)
                                for vh in range(2):
                                    vsrc = bass.AP(v_agh[hf],
                                                   (r * L + vh * NT2 * 128) * HD2 + hh * 128,
                                                   [[HD2, 128], [128 * HD2, NT2], [1, 128]])
                                    p.inc(y.dma_start(out=vbuf[:, kvp, vh * NT2:(vh + 1) * NT2, :],
                                                      in_=vsrc), f"SYK{kvp}", 16)
                            on("sync", _lkv)
                            sy_kv[kvidx] = (f"SYK{kvp}", p.cnt[f"SYK{kvp}"])

                            for s_ in range(NT):
                                g = s_ * NC + r
                                i0 = _i0_of(g, NC)
                                lo = i0 * 128
                                par = pidx % 2
                                pa3 = pidx % 3
                                hp = h % 2

                                def _qkm(t, r=r, s_=s_, h=h, lo=lo, par=par, kvidx=kvidx,
                                         pidx=pidx, kvp=kvp, g=g):
                                    p.w(t, sy_kv[kvidx][0], sy_kv[kvidx][1])
                                    p.w(t, "SY", sy_m)
                                    if pidx - 2 >= 0 and (pidx - 2) in se_tanh:
                                        p.w(t, "SE", se_tanh[pidx - 2])
                                    mm = None
                                    for (c0, c1) in _chunks512(lo, L):
                                        mm = t.matmul(ps_sT[par][:, c0:c1],
                                                      kbuf[:, kvp, s_ * 128:(s_ + 1) * 128],
                                                      QT[:, h, c0:c1], start=True, stop=True,
                                                      skip_group_check=True)
                                    mm = t.matmul(ps_sT[par][:, lo:lo + 128], ident[:, :],
                                                  cmask[:, g, :], start=False, stop=True,
                                                  skip_group_check=True)
                                    p.inc(mm, "PE")
                                on("tensor", _qkm)
                                pe_here = p.cnt["PE"]

                                if pending_epi1 is None and epi_q:
                                    epi_q.pop(0)()
                                while len(av_q) >= 2:
                                    av_q.pop(0)()

                                def _th(s, par=par, pa3=pa3, lo=lo, pe_here=pe_here, pidx=pidx):
                                    p.w(s, "PE", pe_here)
                                    p.inc(s.activation(tbuf[:, pa3, lo:L], ps_sT[par][:, lo:L],
                                                       AF.Tanh, scale=TANH_SCALE), "SE")
                                on("scalar", _th)
                                se_tanh[pidx] = p.cnt["SE"]
                                se_t = p.cnt["SE"]

                                def _ex(s, pa3=pa3, lo=lo, se_t=se_t, pidx=pidx):
                                    p.w(s, "SE", se_t)
                                    if pidx - 3 >= 0 and (pidx - 3) in pe_av:
                                        p.w(s, "PE", pe_av[pidx - 3])
                                    if pidx - 3 >= 0 and (pidx - 3) in ve_pacc:
                                        p.w(s, "VE", ve_pacc[pidx - 3])
                                    p.inc(s.activation(pbuf[:, pa3, lo:L], tbuf[:, pa3, lo:L],
                                                       AF.Exp, scale=50.0), "SE")
                                on("scalar", _ex)
                                se_exp[pidx] = p.cnt["SE"]
                                se_e = p.cnt["SE"]

                                if pending_epi1 is not None:
                                    pending_epi1()
                                    pending_epi1 = None

                                def _pa(v, g=g, lo=lo, pa3=pa3, se_e=se_e, pidx=pidx, hp=hp):
                                    p.w(v, "SE", se_e)
                                    if pidx - 1 >= 0 and (pidx - 1) in ve_pacc:
                                        p.w(v, "VE", ve_pacc[pidx - 1])
                                    if g == 0 and pe_lmm_mark[hp] > 0:
                                        p.w(v, "PE", pe_lmm_mark[hp])
                                    if g == 0:
                                        p.inc(v.tensor_copy(paccum[:, hp, :], pbuf[:, pa3, :]), "VE")
                                    else:
                                        p.inc(v.tensor_tensor(paccum[:, hp, lo:L], paccum[:, hp, lo:L],
                                                              pbuf[:, pa3, lo:L], op=ALU.add), "VE")
                                on("vector", _pa)
                                ve_pacc[pidx] = p.cnt["VE"]

                                def make_av(r=r, s_=s_, g=g, lo=lo, pa3=pa3, se_e=se_e, hp=hp,
                                            pidx=pidx, kvidx=kvidx, kvp=kvp, first_of_head=(g == 0)):
                                    def _av(t):
                                        p.w(t, "SE", se_e)
                                        if first_of_head:
                                            if sea_mark[hp] > 0:
                                                p.w(t, "SE", sea_mark[hp])
                                            if veg_mark[hp] > 0:
                                                p.w(t, "VE", veg_mark[hp])
                                        mm = None
                                        for (c0, c1) in _chunks512(lo, L):
                                            mm = t.matmul(ps_oT[hp][:, c0:c1], vbuf[:, kvp, s_, :],
                                                          pbuf[:, pa3, c0:c1],
                                                          start=(g == 0), stop=(g == last_g),
                                                          skip_group_check=True)
                                        p.inc(mm, "PE")
                                        pe_av[pidx] = p.cnt["PE"]
                                        pe_av_kv[kvidx] = p.cnt["PE"]
                                    def emit():
                                        on("tensor", _av)
                                    return emit
                                av_q.append(make_av())
                                pidx += 1

                        for _e in av_q:
                            _e()
                        av_q = []
                        pe_hd0 = p.cnt["PE"]
                        ve_pl = ve_pacc[pidx - 1]

                        def make_epi1(pe_hd0=pe_hd0, hp=hp):
                            def emit():
                                nonlocal se_ao_cur

                                def _aoe(s, pe_hd0=pe_hd0, hp=hp):
                                    p.w(s, "PE", pe_hd0)
                                    if ve_gate > 0:
                                        p.w(s, "VE", ve_gate)
                                    p.inc(s.copy(ao_raw[:, :], ps_oT[hp][:, :]), "SE")
                                on("scalar", _aoe)
                                se_ao_cur = p.cnt["SE"]
                                sea_mark[hp] = p.cnt["SE"]
                            return emit

                        epi_state = {}

                        def make_epi2a(hp=hp, ve_pl=ve_pl, epi_state=epi_state):
                            def emit():
                                se_ao = se_ao_cur
                                epi_state["se_ao"] = se_ao

                                def _lmm(t, ve_pl=ve_pl, se_ao=se_ao, hp=hp):
                                    p.w(t, "SE", se_ao)
                                    p.w(t, "VE", ve_pl)
                                    mm = None
                                    for (c0, c1) in _chunks512(0, L):
                                        mm = t.matmul(ps_oT[hp][0:1, c0:c1], ones_cf[:, :],
                                                      paccum[:, hp, c0:c1],
                                                      start=True, stop=True, skip_group_check=True)
                                    p.inc(mm, "PE")
                                on("tensor", _lmm)
                                pe_lmm_mark[hp] = p.cnt["PE"]
                                pe_lmm = p.cnt["PE"]

                                def _lcp(s, pe_lmm=pe_lmm, hp=hp):
                                    p.w(s, "PE", pe_lmm)
                                    p.w(s, "VE", ve_gate)
                                    p.inc(s.copy(lr_sb[:, :], ps_oT[hp][0:1, :]), "SE")
                                on("scalar", _lcp)
                                se_lcp = p.cnt["SE"]

                                def _lev(v, se_lcp=se_lcp):
                                    p.w(v, "SE", se_lcp)
                                    p.inc(v.reciprocal(lr_sb[:, :], lr_sb[:, :]), "VE")
                                on("vector", _lev)
                                epi_state["ve_l"] = p.cnt["VE"]
                            return emit

                        def make_epi2b(h=h, hp=hp, epi_state=epi_state):
                            def emit():
                                nonlocal ve_gate, se_gate
                                se_ao = epi_state["se_ao"]
                                ve_l = epi_state["ve_l"]

                                def _lbc(t, ve_l=ve_l, hp=hp):
                                    p.w(t, "VE", ve_l)
                                    mm = None
                                    for (c0, c1) in _chunks512(0, L):
                                        mm = t.matmul(ps_oT[hp][:, c0:c1], ones_rf[:, :], lr_sb[:, c0:c1],
                                                      start=True, stop=True, skip_group_check=True)
                                    p.inc(mm, "PE")
                                on("tensor", _lbc)
                                pe_lb = p.cnt["PE"]

                                def _aom(v, h=h, se_ao=se_ao, pe_lb=pe_lb, hp=hp):
                                    p.w(v, "SE", se_ao)
                                    p.w(v, "PE", pe_lb)
                                    p.inc(v.tensor_tensor(attno[:, h, :], ao_raw[:, :],
                                                          ps_oT[hp][:, :], op=ALU.mult), "VE")
                                on("vector", _aom)
                                ve_gate = p.cnt["VE"]
                                se_gate = p.cnt["SE"]
                                veg_mark[hp] = p.cnt["VE"]
                            return emit
                        pending_epi1 = make_epi1()
                        epi_q.append(make_epi2a())
                        epi_q.append(make_epi2b())
                    if pending_epi1 is not None:
                        pending_epi1()
                        pending_epi1 = None
                    while epi_q:
                        epi_q.pop(0)()
                    flush()
                barrier()

                with contextlib.ExitStack() as os_:
                    wos = os_.enter_context(nc_.sbuf_tensor("wos", [128, 2, 1024], BF16))
                    ps_o = [os_.enter_context(nc_.psum_tensor(f"ops{i}", [128, 512], F32))
                            for i in range(8)]
                    TTO = min(4, NT)

                    pe_bufo = {0: 0, 1: 0}
                    for fh in range(2):
                        for g0 in range(0, NT, TTO):
                            for kc in range(16):
                                buf = kc % 2

                                def _lwo(y, kc=kc, buf=buf, fh=fh):
                                    if NC > 1:
                                        p.w(y, "CC", 7)
                                    else:
                                        p.w(y, "GPC", 64)
                                    if pe_bufo[buf] == 0:
                                        p.w(y, "PE")
                                    else:
                                        p.w(y, "PE", pe_bufo[buf])
                                    sap = bass.AP(wo_bf, kc * 128 * D + fh * 1024, [[D, 128], [1, 1024]])
                                    p.inc(y.dma_start(out=wos[:, buf, :], in_=sap), f"GW{buf}", 16)
                                on("sync", _lwo)
                                gw_c = (f"GW{buf}", p.cnt[f"GW{buf}"])

                                def _om(t, kc=kc, buf=buf, gw_c=gw_c, g0=g0):
                                    p.w(t, gw_c[0], gw_c[1])
                                    if kc == 0:
                                        p.w(t, "VE")
                                    mm = None
                                    for i in range(TTO):
                                        tt = g0 + i
                                        mm = t.matmul(ps_o[2 * i][:, :], attno[:, kc, tt * 128:(tt + 1) * 128],
                                                      wos[:, buf, 0:512], start=(kc == 0), stop=(kc == 15))
                                        mm = t.matmul(ps_o[2 * i + 1][:, :], attno[:, kc, tt * 128:(tt + 1) * 128],
                                                      wos[:, buf, 512:1024], start=(kc == 0), stop=(kc == 15))
                                    p.inc(mm, "PE")
                                on("tensor", _om)
                                pe_bufo[buf] = p.cnt["PE"]
                            pe_o = p.cnt["PE"]

                            for i in range(TTO):
                                tt = g0 + i

                                def _ores(v, i=i, tt=tt, fh=fh, pe_o=pe_o):
                                    p.w(v, "PE", pe_o)
                                    for cb in range(2):
                                        c0 = fh * 1024 + cb * 512
                                        p.inc(v.scalar_tensor_tensor(x_tm[:, tt, c0:c0 + 512],
                                                                     ps_o[2 * i + cb][:, :], 1.0,
                                                                     x_tm[:, tt, c0:c0 + 512],
                                                                     op0=ALU.mult, op1=ALU.add), "VE")
                                on("vector", _ores)
                    flush()

        def write_x_out():
            with contextlib.ExitStack() as ws:
                stg = ws.enter_context(nc_.sbuf_tensor("stg", [128, D], F32))
                for tt in range(NT):
                    def _c(v, tt=tt):
                        p.w(v, "SY2")
                        p.inc(v.tensor_copy(stg[:, :], x_tm[:, tt, :]), "VE")
                    on("vector", _c)
                    ve_c = p.cnt["VE"]

                    def _o(y, tt=tt, ve_c=ve_c):
                        p.w(y, "VE", ve_c)
                        p.inc(y.dma_start(out=out_t.ap()[tt * 128:(tt + 1) * 128, :], in_=stg[:, :]), "SY2", 16)
                    on("sync", _o)
                flush()

        ffn_stage(first_ffn=True, write_out=False)
        barrier()
        if out_stage >= 2:
            attn_stage()
            barrier()
        if out_stage >= 3:
            ffn_stage(first_ffn=False, write_out=True)
        else:
            write_x_out()
        barrier()

        def _fin(y):
            p.w(y, "SY2")
            p.w(y, "SY")
            p.w(y, "GP")
        on("sync", _fin)
        flush()

    build_program.last_cnt = dict(p.cnt)
    return nc_


def _host_prep(inputs, T, NCORES):
    pi = _pi_src(); rho = _rho_src()
    x = np.asarray(inputs["L0_attn_x"]).reshape(-1, D)[:T]
    wq = np.asarray(inputs["L0_attn_wq"]); wk = np.asarray(inputs["L0_attn_wk"])
    wv = np.asarray(inputs["L0_attn_wv"]); wo = np.asarray(inputs["L0_attn_wo"])
    qnw = np.asarray(inputs["L0_attn_q_norm_w"]); knw = np.asarray(inputs["L0_attn_k_norm_w"])
    cos = np.asarray(inputs["L0_attn_rope_cos"]); sin = np.asarray(inputs["L0_attn_rope_sin"])
    gw = np.asarray(inputs["L0_ffn_gate_w"]); uw = np.asarray(inputs["L0_ffn_up_w"])
    dw = np.asarray(inputs["L0_ffn_down_w"])

    NTG = T // 128
    tiles = [_tiles_of(c, NTG, NCORES) for c in range(NCORES)]
    xp = np.ascontiguousarray(x[:, pi])
    wqT = np.ascontiguousarray(wq[:, pi].T); wkT = np.ascontiguousarray(wk[:, pi].T)
    wvT = np.ascontiguousarray(wv[:, pi].T)
    woT = np.ascontiguousarray(wo[pi, :].T)
    g_p = np.ascontiguousarray(gw[rho][:, pi]); u_p = np.ascontiguousarray(uw[rho][:, pi])
    d_p = np.ascontiguousarray(dw[pi][:, rho])
    R1 = I // NCORES; R2 = D // NCORES

    c16 = cos[:NH]; s16 = sin[:NH]
    crep = np.repeat(c16, 2, axis=1).reshape(-1)
    srep = np.repeat(s16, 2, axis=1).reshape(-1)
    sign = np.tile([-1.0, 1.0], D // 2)

    def mkrows(gv):
        gt = np.tile(gv, NH)
        gswap = gt.reshape(-1, 2)[:, ::-1].reshape(-1)
        cgr = (gt * crep).astype(np.float32)[None]
        sgr = (gswap * srep * sign).astype(np.float32)[None]
        return np.ascontiguousarray(cgr), np.ascontiguousarray(sgr)

    cg_q, sg_q = mkrows(qnw); cg_k, sg_k = mkrows(knw)
    ident = np.eye(128, dtype=BF)

    kk = np.arange(128)[:, None]; qq = np.arange(128)[None, :]
    diag = np.where(kk <= qq, 0.0, -3e9)
    keep = np.zeros((128, 128))
    full = np.full((128, 128), -3e9)

    in_maps = []
    for c in range(NCORES):
        rows = np.concatenate([np.arange(t * 128, (t + 1) * 128) for t in tiles[c]])
        cm = np.empty((128, NTG, 128), dtype=np.float32)
        for g in range(NTG):
            i0 = _i0_of(g, NCORES)
            G = NCORES * i0 + c
            cm[:, g, :] = keep if G > g else (diag if G == g else full)
        RS = D // NCORES
        in_maps.append({
            "x_loc": np.ascontiguousarray(xp[rows]),
            "wqT": np.ascontiguousarray(wqT[c * RS:(c + 1) * RS]),
            "wkT": np.ascontiguousarray(wkT[c * RS:(c + 1) * RS]),
            "wvT": np.ascontiguousarray(wvT[c * RS:(c + 1) * RS]),
            "woT": np.ascontiguousarray(woT[c * RS:(c + 1) * RS]),
            "gsh": np.ascontiguousarray(g_p[c * R1:(c + 1) * R1]),
            "ush": np.ascontiguousarray(u_p[c * R1:(c + 1) * R1]),
            "dsh": np.ascontiguousarray(d_p[c * R2:(c + 1) * R2]),
            "cg_q": cg_q, "sg_q": sg_q, "cg_k": cg_k, "sg_k": sg_k,
            "ident_in": ident,
            "cmask_in": np.ascontiguousarray(cm.reshape(128, NTG * 128).astype(BF)),
        })
    return in_maps, tiles, pi


def _host_finish(results, tiles, pi, T, NCORES):
    out = np.empty((T, D), dtype=np.float32)
    for c in range(NCORES):
        rows = np.concatenate([np.arange(t * 128, (t + 1) * 128) for t in tiles[c]])
        out[rows] = results[c]["out_loc"]
    inv = np.empty(D, dtype=np.int64); inv[pi] = np.arange(D)
    return np.ascontiguousarray(out[:, inv])[None]


def kernel(**inputs):
    T, NCORES = 8192, 8
    in_maps, tiles, pi = _host_prep(inputs, T, NCORES)
    nc_ = build_program(T, NCORES, out_stage=3)
    res = run_bass_kernel_spmd(nc_, in_maps, core_ids=list(range(NCORES)))
    return _host_finish(res.results, tiles, pi, T, NCORES)


# version 31 (final)
# speedup vs baseline: 1.0157x; 1.0121x over previous
import contextlib
import numpy as np
import ml_dtypes
import concourse.bass as bass
import concourse.mybir as mybir
from concourse.bass_utils import run_bass_kernel_spmd

F32 = mybir.dt.float32
BF16 = mybir.dt.bfloat16
FP8 = mybir.dt.float8e4
AF = mybir.ActivationFunctionType
ALU = mybir.AluOpType
AX = mybir.AxisListType
BF = ml_dtypes.bfloat16

D = 2048
NH, DH = 16, 128
I = 8192
BLK = 32
NBD = D // BLK
NBI = I // BLK
QCLAMP = 448.0 * 1e-30
LN_EPS = 1e-5
RMS_EPS = 1e-6
TANH_SCALE = float((DH ** (-0.5)) / 50.0)


def _pi_src():
    pos = np.arange(D)
    return (32 * (pos % NBD) + pos // NBD).astype(np.int64)


def _rho_src():
    pos = np.arange(I)
    lf = pos % 2048
    return ((pos // 2048) * 2048 + 32 * (lf % 64) + lf // 64).astype(np.int64)


def _tiles_of(c, ntg, nc):
    return list(range(c, ntg, nc))


def _i0_of(g, nc):
    return max(0, -((-(g - (nc - 1))) // nc))


def _chunks512(lo, hi):
    out = []
    while lo < hi:
        nxt = min(hi, (lo // 512 + 1) * 512)
        out.append((lo, nxt))
        lo = nxt
    return out


def build_program(T, NC, out_stage=3):
    L = T // NC
    NT = L // 128
    NTG = T // 128
    assert T % (128 * NC) == 0 and NT >= 2 and NT % 2 == 0
    R1 = I // NC
    R2 = D // NC
    assert R1 % 128 == 0 and R2 % 128 == 0
    TTG = min(4, NT)
    HALF = TTG // 2

    nc_ = bass.Bass("TRN2", target_bir_lowering=False)

    def din(name, shape, dt=F32):
        return nc_.dram_tensor(name, list(shape), dt, kind="ExternalInput")

    x_in = din("x_loc", [L, D])
    wqT_d = din("wqT", [D // NC, D]); wkT_d = din("wkT", [D // NC, D])
    wvT_d = din("wvT", [D // NC, D])
    woT_d = din("woT", [D // NC, D])
    gsh = din("gsh", [R1, D]); ush = din("ush", [R1, D]); dsh = din("dsh", [R2, I])
    cg_q_in = din("cg_q", [1, D]); sg_q_in = din("sg_q", [1, D])
    cg_k_in = din("cg_k", [1, D]); sg_k_in = din("sg_k", [1, D])
    ident_in = din("ident_in", [128, 128], BF16)
    cmask_in = din("cmask_in", [128, NTG * 128], BF16)
    out_t = nc_.dram_tensor("out_loc", [L, D], F32, kind="ExternalOutput")

    wq_l = nc_.dram_tensor("wq_l", [D // NC, D], BF16)
    wk_l = nc_.dram_tensor("wk_l", [D // NC, D], BF16)
    wv_l = nc_.dram_tensor("wv_l", [D // NC, D], BF16)
    wo_l = nc_.dram_tensor("wo_l", [D // NC, D], BF16)
    gu_g_l = nc_.dram_tensor("gu_g_l", [D, R1], BF16)
    gu_u_l = nc_.dram_tensor("gu_u_l", [D, R1], BF16)
    ddqT_l = nc_.dram_tensor("ddqT_l", [I, R2], BF16)
    HD2 = D // 2
    kT_lh = [nc_.dram_tensor(f"kT_l{a}", [HD2, L], BF16) for a in range(2)]
    v_lh = [nc_.dram_tensor(f"v_l{a}", [L, HD2], BF16) for a in range(2)]
    _AS = "Shared" if NC > 4 else "Local"
    if NC > 1:
        gudqT_g = nc_.dram_tensor("gudqT_g", [NC * D, R1], BF16, addr_space=_AS)
        gudqT_u = nc_.dram_tensor("gudqT_u", [NC * D, R1], BF16, addr_space=_AS)
        ddqT = nc_.dram_tensor("ddqT", [NC * I, R2], BF16, addr_space=_AS)
        wq_bf = nc_.dram_tensor("wq_bf", [D, D], BF16, addr_space=_AS)
        wk_bf = nc_.dram_tensor("wk_bf", [D, D], BF16, addr_space=_AS)
        wv_bf = nc_.dram_tensor("wv_bf", [D, D], BF16, addr_space=_AS)
        wo_bf = nc_.dram_tensor("wo_bf", [D, D], BF16, addr_space=_AS)
        kT_agh = [nc_.dram_tensor(f"kT_ag{a}", [NC * HD2, L], BF16, addr_space=_AS)
                  for a in range(2)]
        v_agh = [nc_.dram_tensor(f"v_ag{a}", [NC * L, HD2], BF16, addr_space=_AS)
                 for a in range(2)]
    else:
        gudqT_g, gudqT_u, ddqT = gu_g_l, gu_u_l, ddqT_l
        wq_bf, wk_bf, wv_bf, wo_bf = wq_l, wk_l, wv_l, wo_l
        kT_agh, v_agh = kT_lh, v_lh

    st = contextlib.ExitStack()
    sems = {}
    for s in ["SY", "GP", "CC", "PE", "VE", "SE", "SY2", "SYX",
              "SYW0", "SYW1", "SYW2", "SYW3", "SYW4", "SYW5", "SYW6", "SYW7",
              "SYD0", "SYD1", "SYD2", "SYD3", "SYD4", "SYD5", "SYD6", "SYD7", "SYK0", "SYK1", "SYK2",
              "GW0", "GW1", "GW2", "GW3", "GW4", "GW5", "GW6", "GW7", "GE", "SYDQ0", "SYDQ1", "SYDQ2", "SYDQ3",
              "SYO0", "SYO1", "GPC", "GPX"]:
        sems[s] = st.enter_context(nc_.semaphore(s))

    class Px:
        def __init__(self):
            self.cnt = {k: 0 for k in sems}

        def inc(self, instr, sem, n=1):
            instr.then_inc(sems[sem], n)
            self.cnt[sem] += n
            return self.cnt[sem]

        def w(self, eng, sem, val=None):
            v = self.cnt[sem] if val is None else val
            if v > 0:
                eng.wait_ge(sems[sem], v)

    p = Px()

    with st:
        x_tm = st.enter_context(nc_.sbuf_tensor("x_tm", [128, NT, D], BF16))
        ident = st.enter_context(nc_.sbuf_tensor("ident", [128, 128], BF16))
        ones_cb = st.enter_context(nc_.sbuf_tensor("ones_cb", [128, 1], BF16))
        ones_cf = st.enter_context(nc_.sbuf_tensor("ones_cf", [128, 1], F32))
        ones_rf = st.enter_context(nc_.sbuf_tensor("ones_rf", [1, 128], F32))
        cst_eps = st.enter_context(nc_.sbuf_tensor("cst_eps", [128, 1], F32))
        nc_.const_aps.register(F32, LN_EPS, cst_eps[:, :])
        block = st.enter_context(nc_.Block())

        ENGLOG = {e: [] for e in ("tensor", "vector", "scalar", "gpsimd", "sync")}

        class _IP:
            __slots__ = ("incs",)
            def __init__(self):
                self.incs = []
            def then_inc(self, sem, n=1):
                self.incs.append((sem, n))
                return self

        class _EP:
            __slots__ = ("log",)
            def __init__(self, log):
                self.log = log
            def __getattr__(self, meth):
                log = self.log
                def call(*a, **k):
                    ip = _IP()
                    log.append((meth, a, k, ip))
                    return ip
                return call

        def on(engine_name, fn):
            fn(_EP(ENGLOG[engine_name]))

        def flush():
            for name in ("sync", "gpsimd", "tensor", "vector", "scalar"):
                log = ENGLOG[name]
                if not log:
                    continue
                ENGLOG[name] = []
                def _replay(eng, log=log):
                    for meth, a, k, ip in log:
                        r = getattr(eng, meth)(*a, **k)
                        for (sem, n) in ip.incs:
                            r.then_inc(sem, n)
                getattr(block, name)(_replay)

        def barrier():
            snap = {k: v for k, v in p.cnt.items() if k not in ("GPC", "CC")}
            for en in ["tensor", "vector", "scalar", "gpsimd", "sync"]:
                def _f(e, snap=snap):
                    for sn, v in snap.items():
                        if v > 0:
                            e.wait_ge(sems[sn], v)
                on(en, _f)
            flush()

        def _init_sync(y):
            p.inc(y.dma_start(out=ident[:, :], in_=ident_in.ap()), "SY", 16)
        on("sync", _init_sync)

        def _ldx_all(g):
            p.inc(g.dma_start(out=x_tm[:, :, :],
                              in_=x_in.ap().rearrange("(t p) d -> p t d", p=128)), "GPX", 16)
        on("gpsimd", _ldx_all)

        def _init_vec(v):
            v.memset(ones_cb[:, :], 1.0)
            v.memset(ones_cf[:, :], 1.0)
            v.memset(cst_eps[:, :], LN_EPS)
            p.inc(v.memset(ones_rf[:, :], 1.0), "VE")
        on("vector", _init_vec)

        def _cast_attn_w(g):
            for wsrc, wdst in ((wqT_d, wq_l), (wkT_d, wk_l), (wvT_d, wv_l), (woT_d, wo_l)):
                p.inc(g.dma_start(out=wdst.ap(), in_=wsrc.ap()), "GPC", 16)
        on("gpsimd", _cast_attn_w)

        with contextlib.ExitStack() as dq:
            wnat = dq.enter_context(nc_.sbuf_tensor("wnat", [128, 4, 2048], F32))
            wdq = dq.enter_context(nc_.sbuf_tensor("wdq", [128, 4, 2048], BF16))
            q8w = dq.enter_context(nc_.sbuf_tensor("q8w", [128, 4, 2048], FP8))
            bmw = dq.enter_context(nc_.sbuf_tensor("bmw", [128, 4, 64], F32))
            r2w = dq.enter_context(nc_.sbuf_tensor("r2w", [128, 4, 64], F32))
            s2w = dq.enter_context(nc_.sbuf_tensor("s2w", [128, 4, 64], F32))
            trs = dq.enter_context(nc_.sbuf_tensor("trs", [128, 2, 16, 512], BF16))
            dps = [dq.enter_context(nc_.psum_tensor(f"dps{i}", [128, 1024], BF16))
                   for i in range(2)]
            NBU = 2048 // BLK
            dq_state = {"i": 0,
                        "ve_mark": {b: 0 for b in range(4)},
                        "ge_mark": {b: 0 for b in range(4)},
                        "tr_mark": {b: 0 for b in range(4)},
                        "pending_dq2": None}

            def dequant_unit(in_ap, batch_out):
                b = dq_state["i"] % 4
                w = (dq_state["i"] // 4) % 2
                dq_state["i"] += 1

                def _load(y, b=b, in_ap=in_ap):
                    p.w(y, "VE", dq_state["ve_mark"][b])
                    p.w(y, "GE", dq_state["ge_mark"][b])
                    p.inc(y.dma_start(out=wnat[:, b, :], in_=in_ap), f"SYDQ{b}", 16)
                on("sync", _load)
                sy_h = p.cnt[f"SYDQ{b}"]

                def _scales(v, sy_h=sy_h, b=b):
                    p.w(v, f"SYDQ{b}", sy_h)
                    xv = wnat[:, b, :].rearrange("p (j bb) -> p bb j", bb=NBU, j=BLK)
                    p.inc(v.tensor_reduce(bmw[:, b, :], xv, axis=AX.X, op=ALU.max,
                                          apply_absolute_value=True), "VE")
                    p.w(v, "VE")
                    p.inc(v.tensor_scalar(bmw[:, b, :], bmw[:, b, :], QCLAMP, None, ALU.max), "VE")
                    p.w(v, "VE")
                    p.inc(v.reciprocal(r2w[:, b, :], bmw[:, b, :]), "VE")
                    p.w(v, "VE")
                    p.inc(v.tensor_scalar(r2w[:, b, :], r2w[:, b, :], 224.0, None, ALU.mult), "VE")
                    p.inc(v.tensor_scalar(s2w[:, b, :], bmw[:, b, :], 1.0 / 224.0, None, ALU.mult), "VE")
                on("vector", _scales)
                ve_s = p.cnt["VE"]

                def _y(g, b=b, ve_s=ve_s):
                    p.w(g, "VE", ve_s)
                    rb = r2w[:, b, :].unsqueeze(1).broadcast_to([128, BLK, NBU])
                    p.inc(g.tensor_tensor(q8w[:, b, :].rearrange("p (j bb) -> p j bb", bb=NBU),
                                          wnat[:, b, :].rearrange("p (j bb) -> p j bb", bb=NBU),
                                          rb, op=ALU.mult), "GE")
                on("gpsimd", _y)
                dq_state["ge_mark"][b] = p.cnt["GE"]
                ge_y = p.cnt["GE"]

                def tail(b=b, w=w, ge_y=ge_y, batch_out=batch_out):
                    def _dq2(v, b=b, ge_y=ge_y):
                        p.w(v, "GE", ge_y)
                        if dq_state["tr_mark"][b] > 0:
                            p.w(v, "PE", dq_state["tr_mark"][b])
                        sb_ = s2w[:, b, :].unsqueeze(1).broadcast_to([128, BLK, NBU])
                        p.inc(v.tensor_tensor(wdq[:, b, :].rearrange("p (j bb) -> p j bb", bb=NBU),
                                              q8w[:, b, :].rearrange("p (j bb) -> p j bb", bb=NBU),
                                              sb_, op=ALU.mult), "VE")
                    on("vector", _dq2)
                    dq_state["ve_mark"][b] = p.cnt["VE"]
                    ve_dq = p.cnt["VE"]

                    pe0 = p.cnt["PE"]
                    se0 = p.cnt["SE"]

                    def _tr(t, ve_dq=ve_dq, se0=se0, b=b):
                        p.w(t, "SY")
                        p.w(t, "VE", ve_dq)
                        p.w(t, "SE")
                        for kc in range(16):
                            if kc >= 2:
                                t.wait_ge(sems["SE"], se0 + kc - 1)
                            p.inc(t.transpose(dps[kc % 2][:, :128],
                                              wdq[:, b, kc * 128:(kc + 1) * 128], ident[:, :]), "PE")
                    on("tensor", _tr)
                    dq_state["tr_mark"][b] = p.cnt["PE"]

                    def _ev(s, pe0=pe0, b=b, w=w):
                        p.w(s, f"SYO{w}")
                        for kc in range(16):
                            s.wait_ge(sems["PE"], pe0 + kc + 1)
                            p.inc(s.copy(trs[:, w, kc, b * 128:(b + 1) * 128], dps[kc % 2][:, :128]), "SE")
                    on("scalar", _ev)

                    if b == 3:
                        se_e = p.cnt["SE"]

                        def _out(s, w=w, se_e=se_e, batch_out=batch_out):
                            s.wait_ge(sems["SE"], se_e)
                            for dst_ap, src_ap in batch_out(w):
                                p.inc(s.dma_start(out=dst_ap, in_=src_ap), f"SYO{w}", 16)
                        on("scalar", _out)

                if dq_state["pending_dq2"] is not None:
                    dq_state["pending_dq2"]()
                dq_state["pending_dq2"] = tail

            def flush_tail():
                if dq_state["pending_dq2"] is not None:
                    dq_state["pending_dq2"]()
                    dq_state["pending_dq2"] = None

            def _emit_gather(ins_t, outs_t, wv, pre=None):
                if NC <= 1:
                    return

                def _g(g, ins_t=ins_t, outs_t=outs_t, wv=wv, pre=pre):
                    p.w(g, "SYO0", wv[0])
                    p.w(g, "SYO1", wv[1])
                    if pre is not None:
                        pre(g)
                    p.inc(g.collective_compute("AllGather", ALU.bypass,
                                               replica_groups=[list(range(NC))],
                                               ins=[ins_t.ap().opt()], outs=[outs_t.ap().opt()]), "CC")
                on("gpsimd", _g)

            syo_after = {}
            for src_t, dst in ((gsh, gu_g_l), (ush, gu_u_l)):
                for rt in range(R1 // 128):
                    bo = None
                    if rt % 4 == 3:
                        def bo(w, rt0=rt - 3, dst=dst):
                            return [(bass.AP(dst, rt0 * 128 + q * 4 * 128 * R1,
                                             [[R1, 128], [128 * R1, 4], [1, 512]]),
                                     trs[:, w, q * 4:(q + 1) * 4, :]) for q in range(4)]
                    dequant_unit(src_t.ap()[rt * 128:(rt + 1) * 128, :], bo)
                flush_tail()
                syo_after[dst.name] = (p.cnt["SYO0"], p.cnt["SYO1"])

            _emit_gather(gu_g_l, gudqT_g, syo_after[gu_g_l.name])

            NRD = R2 // 128
            for rt in range(NRD):
                for s4 in range(4):
                    bo = None
                    if s4 == 3:
                        def bo(w, rt=rt):
                            return [(bass.AP(ddqT_l, rt * 128 + q4 * (16 * 128 * R2),
                                             [[R2, 128], [128 * R2, 16], [1, 128]]),
                                     trs[:, w, :, q4 * 128:(q4 + 1) * 128]) for q4 in range(4)]
                    dequant_unit(bass.AP(dsh, (rt * 128) * I + s4 * 2048, [[I, 128], [1, 2048]]), bo)
                if rt == (NRD - 1) // 2:
                    _emit_gather(gu_u_l, gudqT_u, syo_after[gu_u_l.name])
            flush_tail()
            _emit_gather(ddqT_l, ddqT, (p.cnt["SYO0"], p.cnt["SYO1"]))

            def _pre_w(g):
                p.w(g, "GPC", 64)
            for lt, gt in ((wq_l, wq_bf), (wk_l, wk_bf), (wv_l, wv_bf), (wo_l, wo_bf)):
                _emit_gather(lt, gt, (0, 0), pre=_pre_w if lt is wq_l else None)
            flush()
        barrier()

        def ln_stats(v, src_ap, scr_ap, nst):
            p.inc(v.tensor_reduce(nst[:, 0:1], src_ap, axis=AX.X, op=ALU.add, negate=True), "VE")
            p.inc(v.scalar_tensor_tensor(scr_ap, src_ap, 1.0, src_ap,
                                         op0=ALU.mult, op1=ALU.mult, accum_out=nst[:, 1:2]), "VE")
            p.w(v, "VE")
            p.inc(v.tensor_scalar(nst[:, 0:1], nst[:, 0:1], 1.0 / D, None, ALU.mult), "VE")
            p.w(v, "VE")
            p.inc(v.tensor_tensor(nst[:, 2:3], nst[:, 0:1], nst[:, 0:1], op=ALU.mult), "VE")
            p.w(v, "VE")
            p.inc(v.scalar_tensor_tensor(nst[:, 3:4], nst[:, 1:2], 1.0 / D, nst[:, 2:3],
                                         op0=ALU.mult, op1=ALU.subtract), "VE")

        def ln_apply(src_ap, dst_ap, nst):
            ve_v = p.cnt["VE"]

            def _sq(s, ve_v=ve_v):
                p.w(s, "VE", ve_v)
                p.inc(s.activation(nst[:, 4:5], nst[:, 3:4], AF.Sqrt, bias=LN_EPS), "SE")
            on("scalar", _sq)
            se_s = p.cnt["SE"]

            def _xn(v, se_s=se_s):
                p.w(v, "SE", se_s)
                p.inc(v.reciprocal(nst[:, 5:6], nst[:, 4:5]), "VE")
                p.w(v, "VE")
                p.inc(v.tensor_scalar(dst_ap, src_ap, nst[:, 0:1], nst[:, 5:6],
                                      op0=ALU.add, op1=ALU.mult), "VE")
                p.w(v, "VE")
            on("vector", _xn)

        def ffn_stage(first_ffn, write_out):
            sfx = "_f1" if first_ffn else "_f3"
            with contextlib.ExitStack() as fs:
                xn = fs.enter_context(nc_.sbuf_tensor("xn" + sfx, [128, D], F32))
                nst = fs.enter_context(nc_.sbuf_tensor("nst" + sfx, [128, 8], F32))
                q8x = fs.enter_context(nc_.sbuf_tensor("q8x" + sfx, [128, D], FP8))
                xq_tm = fs.enter_context(nc_.sbuf_tensor("xq_tm" + sfx, [128, TTG, D], BF16))
                xqT = fs.enter_context(nc_.sbuf_tensor("xqT" + sfx, [128, TTG, 16, 128], BF16))
                bmx = fs.enter_context(nc_.sbuf_tensor("bmx" + sfx, [128, NBI], F32))
                r2x = fs.enter_context(nc_.sbuf_tensor("r2x" + sfx, [128, NBI], F32))
                s2x = fs.enter_context(nc_.sbuf_tensor("s2x" + sfx, [128, NBI], F32))
                hbuf = fs.enter_context(nc_.sbuf_tensor("hbuf" + sfx, [128, TTG, I], BF16))
                q8h = fs.enter_context(nc_.sbuf_tensor("q8h" + sfx, [128, I], FP8))
                guw = fs.enter_context(nc_.sbuf_tensor("guw" + sfx, [128, 4, 2, 4, 512], BF16))
                dnw = fs.enter_context(nc_.sbuf_tensor("dnw" + sfx, [128, 4, 1024], BF16))
                outs = fs.enter_context(nc_.sbuf_tensor("outs" + sfx, [128, D], F32))
                sg_ev = fs.enter_context(nc_.sbuf_tensor("sg_ev" + sfx, [128, 512], F32))
                ps_gu = [fs.enter_context(nc_.psum_tensor(f"psg{i}" + sfx, [128, 512], F32))
                         for i in range(8)]
                psA = ps_gu[6][:, :].bitcast(BF16)
                psB = ps_gu[7][:, :].bitcast(BF16)

                def quant_rowblock(src_ap, qdst_ap, dqdst_ap, nblk, feat):
                    nch = feat // 2048

                    def _st(v):
                        p.w(v, "GE")
                        xv = src_ap.rearrange("p (c j b) -> p c b j", c=nch, j=BLK, b=64)
                        bmv = bmx[:, :nblk].rearrange("p (c b) -> p c b", c=nch)
                        p.inc(v.tensor_reduce(bmv, xv, axis=AX.X, op=ALU.max,
                                              apply_absolute_value=True), "VE")
                        p.w(v, "VE")
                        p.inc(v.tensor_scalar(bmx[:, :nblk], bmx[:, :nblk], QCLAMP, None, ALU.max), "VE")
                        p.w(v, "VE")
                        p.inc(v.reciprocal(r2x[:, :nblk], bmx[:, :nblk]), "VE")
                        p.w(v, "VE")
                        p.inc(v.tensor_scalar(r2x[:, :nblk], r2x[:, :nblk], 224.0, None, ALU.mult), "VE")
                        p.inc(v.tensor_scalar(s2x[:, :nblk], bmx[:, :nblk], 1.0 / 224.0, None, ALU.mult), "VE")
                    on("vector", _st)
                    ve_s = p.cnt["VE"]

                    def _yd(v, ve_s=ve_s):
                        p.w(v, "VE", ve_s)
                        rb = (r2x[:, :nblk].rearrange("p (c b) -> p c b", c=nch)
                              .unsqueeze(2).broadcast_to([128, nch, BLK, 64]))
                        sb_ = (s2x[:, :nblk].rearrange("p (c b) -> p c b", c=nch)
                               .unsqueeze(2).broadcast_to([128, nch, BLK, 64]))
                        p.inc(v.tensor_tensor(qdst_ap.rearrange("p (c j b) -> p c j b", c=nch, j=BLK, b=64),
                                              src_ap.rearrange("p (c j b) -> p c j b", c=nch, j=BLK, b=64),
                                              rb, op=ALU.mult), "VE")
                        p.w(v, "VE")
                        p.inc(v.tensor_tensor(dqdst_ap.rearrange("p (c j b) -> p c j b", c=nch, j=BLK, b=64),
                                              qdst_ap.rearrange("p (c j b) -> p c j b", c=nch, j=BLK, b=64),
                                              sb_, op=ALU.mult), "VE")
                    on("vector", _yd)

                def chunk_transpose(src_ap, dst_fn, nkc):
                    pe0 = p.cnt["PE"]
                    se0 = p.cnt["SE"]

                    def _tr(t, se0=se0):
                        p.w(t, "VE")
                        p.w(t, "SE")
                        for kc in range(nkc):
                            if kc >= 2:
                                t.wait_ge(sems["SE"], se0 + kc - 1)
                            pst = (psA, psB)[kc % 2]
                            p.inc(t.transpose(pst[:, :128], src_ap[:, kc * 128:(kc + 1) * 128], ident[:, :]), "PE")
                    on("tensor", _tr)

                    def _ev(s):
                        for kc in range(nkc):
                            pst = (psA, psB)[kc % 2]
                            s.wait_ge(sems["PE"], pe0 + kc + 1)
                            p.inc(s.copy(dst_fn(kc), pst[:, :128]), "SE")
                    on("scalar", _ev)

                se_xqT_by_group = {}
                prep_pe_mark = [0]

                def prep_ve(ttg0):
                    for i in range(TTG):
                        tt = ttg0 + i
                        if first_ffn:
                            def _wx(v, tt=tt):
                                p.w(v, "GPX", 16)
                            on("vector", _wx)

                        def _ln(v, tt=tt):
                            p.w(v, "SY2")
                            if prep_pe_mark[0] > 0:
                                p.w(v, "PE", prep_pe_mark[0])
                            ln_stats(v, x_tm[:, tt, :], xn[:, :], nst)
                        on("vector", _ln)
                        ln_apply(x_tm[:, tt, :], xn[:, :], nst)
                        quant_rowblock(xn[:, :], q8x[:, :], xq_tm[:, i, :], NBD, D)

                def prep_tr(ttg0):
                    for i in range(TTG):
                        chunk_transpose(xq_tm[:, i, :], lambda kc, i=i: xqT[:, i, kc, :], 16)
                    prep_pe_mark[0] = p.cnt["PE"]
                    se_xqT_by_group[ttg0] = p.cnt["SE"]

                ngroups = NT // TTG
                prep_ve(0)
                prep_tr(0)
                for ttg0 in range(0, NT, TTG):
                    g_n = TTG
                    se_xqT = se_xqT_by_group[ttg0]

                    NCCG = I // 512
                    pe_after = {}
                    ve_evac = {0: 0, 1: 0}
                    pe_mm_h = {0: 0, 1: 0}
                    for ccg in range(NCCG):
                        rank = (ccg * 512) // R1
                        coff = (ccg * 512) % R1
                        for wi in range(2):
                            wt = gudqT_g if wi == 0 else gudqT_u
                            for kg in range(4):
                                def _ldw(y, rank=rank, coff=coff, kg=kg, ccg=ccg, wi=wi, wt=wt):
                                    if NC > 1:
                                        p.w(y, "CC", 1 + wi)
                                    key = (wi, ccg - 1, kg)
                                    if key in pe_after:
                                        p.w(y, "PE", pe_after[key])
                                    src_ap = bass.AP(wt, (rank * D + kg * 4 * 128) * R1 + coff,
                                                     [[R1, 128], [128 * R1, 4], [1, 512]])
                                    p.inc(y.dma_start(out=guw[:, kg, wi, :, :], in_=src_ap),
                                          f"SYW{kg + 4 * wi}", 16)
                                on("sync", _ldw)
                                sy_w = (f"SYW{kg + 4 * wi}", p.cnt[f"SYW{kg + 4 * wi}"])

                                for half in range(2):
                                    hb = half * 4
                                    tts = list(range(half * HALF, half * HALF + HALF))

                                    def _mm(t, kg=kg, sy_w=sy_w, ccg=ccg, tts=tts, hb=hb,
                                            half=half, wi=wi):
                                        p.w(t, sy_w[0], sy_w[1])
                                        p.w(t, "SE", se_xqT)
                                        if NC > 1:
                                            p.w(t, "CC", 1 + wi)
                                        if ccg == 0 and wi == 0 and kg == 0:
                                            p.w(t, "VE")
                                        if ccg > 0 and wi == 0 and kg == 0:
                                            p.w(t, "VE", ve_evac[half])
                                        mm = None
                                        for k4 in range(4):
                                            k = kg * 4 + k4
                                            for i_ in tts:
                                                mm = t.matmul(ps_gu[hb + 2 * (i_ - tts[0]) + wi][:, :],
                                                              xqT[:, i_, k, :],
                                                              guw[:, kg, wi, k4, :],
                                                              start=(k == 0), stop=(k == 15))
                                        p.inc(mm, "PE")
                                    on("tensor", _mm)
                                    if wi == 1 and kg == 3:
                                        pe_mm_h[half] = p.cnt["PE"]
                                pe_after[(wi, ccg, kg)] = p.cnt["PE"]
                        pe_mm = p.cnt["PE"]

                        for half in range(2):
                            hb = half * 4
                            for j, i_ in enumerate(range(half * HALF, half * HALF + HALF)):
                                def _si(s, hb=hb, j=j, pe_mm=pe_mm_h[half]):
                                    p.w(s, "PE", pe_mm)
                                    p.w(s, "VE")
                                    p.inc(s.activation(sg_ev[:, :], ps_gu[hb + 2 * j][:, :], AF.Silu), "SE")
                                on("scalar", _si)
                                se_g = p.cnt["SE"]

                                def _h(v, hb=hb, j=j, i_=i_, ccg=ccg, se_g=se_g):
                                    p.w(v, "SE", se_g)
                                    p.inc(v.tensor_tensor(hbuf[:, i_, ccg * 512:(ccg + 1) * 512],
                                                          sg_ev[:, :], ps_gu[hb + 2 * j + 1][:, :],
                                                          op=ALU.mult), "VE")
                                on("vector", _h)
                            ve_evac[half] = p.cnt["VE"]

                    for i_ in range(g_n):
                        quant_rowblock(hbuf[:, i_, :], q8h[:, :], hbuf[:, i_, :], NBI, I)
                        chunk_transpose(hbuf[:, i_, :],
                                        lambda kc, i_=i_: hbuf[:, i_, kc * 128:(kc + 1) * 128], 64)

                    dn_ve = {0: p.cnt["VE"], 1: 0}
                    dn_se = {0: p.cnt["SE"], 1: p.cnt["SE"]}
                    if ttg0 + TTG < NT:
                        prep_ve(ttg0 + TTG)

                    pe_after_d = {}
                    for fh in range(2):
                        for kc in range(64):
                            buf = kc % 4

                            def _ldd(y, fh=fh, kc=kc, buf=buf):
                                if NC > 1:
                                    p.w(y, "CC", 3)
                                key = fh * 64 + kc - 4
                                if key >= 0:
                                    p.w(y, "PE", pe_after_d[key])
                                c0 = fh * 1024
                                if R2 >= 1024:
                                    src_ap = bass.AP(ddqT, ((c0 // R2) * I + kc * 128) * R2 + (c0 % R2),
                                                     [[R2, 128], [1, 1024]])
                                    dst_ap = dnw[:, buf, :]
                                else:
                                    nr = 1024 // R2
                                    src_ap = bass.AP(ddqT, ((c0 // R2) * I + kc * 128) * R2,
                                                     [[R2, 128], [I * R2, nr], [1, R2]])
                                    dst_ap = dnw[:, buf, :].rearrange("p (n r) -> p n r", n=nr)
                                p.inc(y.dma_start(out=dst_ap, in_=src_ap), f"SYD{buf}", 16)
                            on("sync", _ldd)
                            sy_d = (f"SYD{buf}", p.cnt[f"SYD{buf}"])

                            def _dmm(t, kc=kc, buf=buf, sy_d=sy_d, fh=fh):
                                p.w(t, sy_d[0], sy_d[1])
                                if NC > 1:
                                    p.w(t, "CC", 3)
                                if kc == 0:
                                    p.w(t, "SE", dn_se[fh])
                                    p.w(t, "VE", dn_ve[fh])
                                mm = None
                                for i_ in range(g_n):
                                    mm = t.matmul(ps_gu[i_ * 2][:, :], hbuf[:, i_, kc * 128:(kc + 1) * 128],
                                                  dnw[:, buf, 0:512], start=(kc == 0), stop=(kc == 63))
                                    mm = t.matmul(ps_gu[i_ * 2 + 1][:, :], hbuf[:, i_, kc * 128:(kc + 1) * 128],
                                                  dnw[:, buf, 512:1024], start=(kc == 0), stop=(kc == 63))
                                p.inc(mm, "PE")
                            on("tensor", _dmm)
                            pe_after_d[fh * 64 + kc] = p.cnt["PE"]
                        pe_dn = p.cnt["PE"]

                        for i_ in range(g_n):
                            tt = ttg0 + i_

                            def _res(v, i_=i_, tt=tt, fh=fh, pe_dn=pe_dn):
                                p.w(v, "PE", pe_dn)
                                if write_out:
                                    p.w(v, "SY2")
                                for cb in range(2):
                                    c0 = fh * 1024 + cb * 512
                                    dst = outs[:, c0:c0 + 512] if write_out else x_tm[:, tt, c0:c0 + 512]
                                    p.inc(v.scalar_tensor_tensor(dst, ps_gu[i_ * 2 + cb][:, :], 0.5,
                                                                 x_tm[:, tt, c0:c0 + 512],
                                                                 op0=ALU.mult, op1=ALU.add), "VE")
                            on("vector", _res)

                            if write_out:
                                ve_o = p.cnt["VE"]

                                def _sto(y, tt=tt, fh=fh, ve_o=ve_o):
                                    p.w(y, "VE", ve_o)
                                    c0 = fh * 1024
                                    p.inc(y.dma_start(out=out_t.ap()[tt * 128:(tt + 1) * 128, c0:c0 + 1024],
                                                      in_=outs[:, c0:c0 + 1024]), "SY2", 16)
                                on("sync", _sto)

                        if fh == 0:
                            dn_ve[1] = p.cnt["VE"]
                            if ttg0 + TTG < NT:
                                prep_tr(ttg0 + TTG)
                                dn_se[1] = p.cnt["SE"]
                flush()

        TTA = min(4, NT)

        def attn_stage():
            with contextlib.ExitStack() as ats:
                QT = ats.enter_context(nc_.sbuf_tensor("QT", [128, NH, L], BF16))
                attno = ats.enter_context(nc_.sbuf_tensor("attno", [128, NH, L], BF16))

                with contextlib.ExitStack() as qs:
                    xnb = qs.enter_context(nc_.sbuf_tensor("xnb", [128, D], BF16))
                    xnT = qs.enter_context(nc_.sbuf_tensor("xnT", [128, NT, 16, 128], BF16))
                    sq2 = qs.enter_context(nc_.sbuf_tensor("sq2", [128, 2, D], F32))
                    nst = qs.enter_context(nc_.sbuf_tensor("nst2", [128, 8], F32))
                    cgq = qs.enter_context(nc_.sbuf_tensor("cgq", [128, D], BF16))
                    sgq = qs.enter_context(nc_.sbuf_tensor("sgq", [128, D], BF16))
                    cgk = qs.enter_context(nc_.sbuf_tensor("cgk", [128, D], BF16))
                    sgk = qs.enter_context(nc_.sbuf_tensor("sgk", [128, D], BF16))
                    wqk = qs.enter_context(nc_.sbuf_tensor("wqk", [128, 8, 2, 512], BF16))
                    msq = qs.enter_context(nc_.sbuf_tensor("msq", [128, 2, 4], F32))
                    rstd = qs.enter_context(nc_.sbuf_tensor("rstd", [128, 2, 4], F32))
                    qtm = qs.enter_context(nc_.sbuf_tensor("qtm", [128, 512], BF16))
                    ktm = qs.enter_context(nc_.sbuf_tensor("ktm", [128, 512], BF16))
                    asc = qs.enter_context(nc_.sbuf_tensor("asc", [128, 2, 512], BF16))
                    bsc = qs.enter_context(nc_.sbuf_tensor("bsc", [128, 2, 512], BF16))
                    kst = qs.enter_context(nc_.sbuf_tensor("kst", [128, 4, 128], BF16))
                    vstg = qs.enter_context(nc_.sbuf_tensor("vstg", [128, 512], BF16))
                    ps = [qs.enter_context(nc_.psum_tensor(f"aps{i}", [128, 512], F32))
                          for i in range(8)]

                    for ri, (rin, dstt) in enumerate(((cg_q_in, cgq), (sg_q_in, sgq),
                                                      (cg_k_in, cgk), (sg_k_in, sgk))):
                        def _lr(y, rin=rin):
                            p.w(y, "PE")
                            p.inc(y.dma_start(out=sq2[0:1, 0, :], in_=rin.ap()), "SYX", 16)
                        on("sync", _lr)
                        sy_r = p.cnt["SYX"]

                        def _b1(t, sy_r=sy_r):
                            p.w(t, "SYX", sy_r)
                            p.w(t, "VE")
                            p.w(t, "SE")
                            mm = None
                            for j in range(4):
                                mm = t.matmul(ps[j][:, :], ones_rf[:, :], sq2[0:1, 0, j * 512:(j + 1) * 512],
                                              start=True, stop=True)
                            p.inc(mm, "PE")
                        on("tensor", _b1)
                        pe_r = p.cnt["PE"]

                        def _c1(s, dstt=dstt, pe_r=pe_r):
                            p.w(s, "PE", pe_r)
                            for j in range(4):
                                p.inc(s.copy(dstt[:, j * 512:(j + 1) * 512], ps[j][:, :]), "SE")
                        on("scalar", _c1)
                    se_rope = p.cnt["SE"]

                    psA2 = ps[6][:, :].bitcast(BF16)
                    psB2 = ps[7][:, :].bitcast(BF16)
                    pe_buf = {i: 0 for i in range(8)}
                    ms_mark = {0: 0, 1: 0}
                    sqr_mark = {0: 0, 1: 0}
                    trq_mark = {0: 0, 1: 0}

                    for tt in range(NT):
                        def _ln2(v, tt=tt):
                            p.w(v, "SE")
                            ln_stats(v, x_tm[:, tt, :], sq2[:, 0, :], nst)
                        on("vector", _ln2)
                        ln_apply(x_tm[:, tt, :], xnb[:, :], nst)

                        pe0 = p.cnt["PE"]
                        se0 = p.cnt["SE"]

                        def _trx(t, se0=se0):
                            p.w(t, "VE")
                            p.w(t, "SE")
                            for kc in range(16):
                                if kc >= 2:
                                    t.wait_ge(sems["SE"], se0 + kc - 1)
                                pst = (psA2, psB2)[kc % 2]
                                p.inc(t.transpose(pst[:, :128], xnb[:, kc * 128:(kc + 1) * 128],
                                                  ident[:, :]), "PE")
                        on("tensor", _trx)

                        def _evx(s, pe0=pe0, tt=tt):
                            for kc in range(16):
                                pst = (psA2, psB2)[kc % 2]
                                s.wait_ge(sems["PE"], pe0 + kc + 1)
                                p.inc(s.copy(xnT[:, tt, kc, :], pst[:, :128]), "SE")
                        on("scalar", _evx)
                    se_xnT = p.cnt["SE"]

                    for hfh in range(2):
                        for dqg in (2 * hfh, 2 * hfh + 1):
                          for g0 in range(0, NT, TTA):
                            for k in range(16):
                                buf = k % 8

                                def _lw(y, k=k, buf=buf, dqg=dqg):
                                    if NC > 1:
                                        p.w(y, "CC", 5)
                                    else:
                                        p.w(y, "GPC", 64)
                                    p.w(y, "PE", pe_buf[buf])
                                    for wi, wt in enumerate((wq_bf, wk_bf)):
                                        sap = bass.AP(wt, k * 128 * D + dqg * 512, [[D, 128], [1, 512]])
                                        p.inc(y.dma_start(out=wqk[:, buf, wi, :], in_=sap), f"GW{buf}", 16)
                                on("sync", _lw)
                                gw_c = (f"GW{buf}", p.cnt[f"GW{buf}"])

                                def _qkm(t, k=k, buf=buf, gw_c=gw_c, dqg=dqg, g0=g0):
                                    p.w(t, gw_c[0], gw_c[1])
                                    p.w(t, "SE", se_xnT)
                                    if k == 0:
                                        p.w(t, "VE")
                                        p.w(t, "SE")
                                    mm = None
                                    for i in range(TTA):
                                        mm = t.matmul(ps[i][:, :], xnT[:, g0 + i, k, :], wqk[:, buf, 0, :],
                                                      start=(k == 0), stop=(k == 15))
                                        mm = t.matmul(ps[4 + i][:, :], xnT[:, g0 + i, k, :], wqk[:, buf, 1, :],
                                                      start=(k == 0), stop=(k == 15))
                                    p.inc(mm, "PE")
                                on("tensor", _qkm)
                                pe_buf[buf] = p.cnt["PE"]
                            pe_qk = p.cnt["PE"]

                            for i in range(TTA):
                                tt = g0 + i
                                for which in range(2):
                                    psu = ps[i] if which == 0 else ps[4 + i]
                                    cg_t = cgq if which == 0 else cgk
                                    sg_t = sgq if which == 0 else sgk
                                    dtm = qtm if which == 0 else ktm
                                    ch = which

                                    def _sq2(s, psu=psu, ch=ch):
                                        p.w(s, "PE", pe_qk)
                                        p.w(s, "SE", se_rope)
                                        p.w(s, "VE", ms_mark[ch])
                                        p.inc(s.activation(sq2[:, ch, :512], psu[:, :], AF.Square), "SE")
                                    on("scalar", _sq2)
                                    se_q2 = p.cnt["SE"]

                                    def _ms(v, se_q2=se_q2, ch=ch):
                                        p.w(v, "SE", se_q2)
                                        p.w(v, "SE", sqr_mark[ch])
                                        sv = sq2[:, ch, :512].rearrange("p (h d) -> p h d", h=4)
                                        p.inc(v.tensor_reduce(msq[:, ch, :], sv, axis=AX.X, op=ALU.add), "VE")
                                        p.w(v, "VE")
                                        p.inc(v.tensor_scalar(msq[:, ch, :], msq[:, ch, :], 1.0 / DH, RMS_EPS,
                                                              op0=ALU.mult, op1=ALU.add), "VE")
                                    on("vector", _ms)
                                    ms_mark[ch] = p.cnt["VE"]
                                    ve_m = p.cnt["VE"]

                                    def _sqr(s, ve_m=ve_m, ch=ch):
                                        p.w(s, "VE", ve_m)
                                        p.inc(s.activation(rstd[:, ch, :], msq[:, ch, :], AF.Sqrt), "SE")
                                    on("scalar", _sqr)
                                    sqr_mark[ch] = p.cnt["SE"]
                                    se_r2 = p.cnt["SE"]

                                    def _rr(v, se_r2=se_r2, psu=psu, cg_t=cg_t, sg_t=sg_t,
                                            dtm=dtm, dqg=dqg, ch=ch):
                                        p.w(v, "SE", se_r2)
                                        p.w(v, "PE", trq_mark[ch])
                                        p.inc(v.reciprocal(rstd[:, ch, :], rstd[:, ch, :]), "VE")
                                        p.w(v, "VE")
                                        sl = slice(dqg * 512, (dqg + 1) * 512)
                                        p.inc(v.tensor_tensor(asc[:, ch, :], psu[:, :], cg_t[:, sl], op=ALU.mult), "VE")
                                        swap = bass.AP(psu.handle if hasattr(psu, "handle") else psu,
                                                       1, [[512, 128], [2, 256], [-1, 2]])
                                        p.inc(v.scalar_tensor_tensor(
                                            bsc[:, ch, :].rearrange("p (a t) -> p a t", t=2), swap, 1.0,
                                            sg_t[:, sl].rearrange("p (a t) -> p a t", t=2),
                                            op0=ALU.mult, op1=ALU.mult), "VE")
                                        p.w(v, "VE")
                                        p.inc(v.tensor_tensor(asc[:, ch, :], asc[:, ch, :], bsc[:, ch, :], op=ALU.add), "VE")
                                        p.w(v, "VE")
                                        for h4 in range(4):
                                            p.inc(v.tensor_scalar(dtm[:, h4 * 128:(h4 + 1) * 128],
                                                                  asc[:, ch, h4 * 128:(h4 + 1) * 128],
                                                                  rstd[:, ch, h4:h4 + 1], None, ALU.mult), "VE")
                                    on("vector", _rr)
                                ve_rope = p.cnt["VE"]

                                scrA = ps[i][:, :].bitcast(BF16)
                                scrB = ps[4 + i][:, :].bitcast(BF16)
                                pe1 = p.cnt["PE"]
                                se1 = p.cnt["SE"]

                                def _trq(t, ve_rope=ve_rope, scrA=scrA, scrB=scrB, se1=se1):
                                    p.w(t, "VE", ve_rope)
                                    p.w(t, "SE")
                                    for j in range(8):
                                        if j >= 2:
                                            t.wait_ge(sems["SE"], se1 + j - 1)
                                        srcb = (qtm if j < 4 else ktm)[:, (j % 4) * 128:(j % 4 + 1) * 128]
                                        p.inc(t.transpose((scrA, scrB)[j % 2][:, :128], srcb, ident[:, :]), "PE")
                                on("tensor", _trq)
                                trq_mark[0] = p.cnt["PE"]
                                trq_mark[1] = p.cnt["PE"]

                                def _evq(s, pe1=pe1, i=i, tt=tt, dqg=dqg, scrA=scrA, scrB=scrB):
                                    p.w(s, "SY2")
                                    for j in range(8):
                                        s.wait_ge(sems["PE"], pe1 + j + 1)
                                        if j < 4:
                                            dst = QT[:, dqg * 4 + j, tt * 128:(tt + 1) * 128]
                                        else:
                                            dst = kst[:, j - 4, :]
                                        p.inc(s.copy(dst, (scrA, scrB)[j % 2][:, :128]), "SE")
                                on("scalar", _evq)
                                se_k = p.cnt["SE"]

                                def _ko(y, tt=tt, dqg=dqg, se_k=se_k):
                                    p.w(y, "SE", se_k)
                                    dst_ap = bass.AP(kT_lh[dqg // 2], ((dqg % 2) * 512) * L + tt * 128,
                                                     [[L, 128], [128 * L, 4], [1, 128]])
                                    p.inc(y.dma_start(out=dst_ap, in_=kst[:, :, :]), "SY2", 16)
                                on("sync", _ko)

                        for dqg in (2 * hfh, 2 * hfh + 1):
                          for g0 in range(0, NT, TTA):
                            for k in range(16):
                                buf = k % 8

                                def _lv(y, k=k, buf=buf, dqg=dqg):
                                    if NC > 1:
                                        p.w(y, "CC", 6)
                                    else:
                                        p.w(y, "GPC", 64)
                                    p.w(y, "PE", pe_buf[buf])
                                    sap = bass.AP(wv_bf, k * 128 * D + dqg * 512, [[D, 128], [1, 512]])
                                    p.inc(y.dma_start(out=wqk[:, buf, 0, :], in_=sap), f"GW{buf}", 16)
                                on("sync", _lv)
                                gw_c = (f"GW{buf}", p.cnt[f"GW{buf}"])

                                def _vmm(t, k=k, buf=buf, gw_c=gw_c, g0=g0):
                                    p.w(t, gw_c[0], gw_c[1])
                                    p.w(t, "SE", se_xnT)
                                    if k == 0:
                                        p.w(t, "VE")
                                        p.w(t, "SE")
                                    mm = None
                                    for i in range(TTA):
                                        mm = t.matmul(ps[i][:, :], xnT[:, g0 + i, k, :], wqk[:, buf, 0, :],
                                                      start=(k == 0), stop=(k == 15))
                                    p.inc(mm, "PE")
                                on("tensor", _vmm)
                                pe_buf[buf] = p.cnt["PE"]
                            pe_v = p.cnt["PE"]

                            for i in range(TTA):
                                tt = g0 + i

                                def _vev(s, i=i, pe_v=pe_v):
                                    p.w(s, "PE", pe_v)
                                    p.w(s, "SY2")
                                    p.inc(s.copy(vstg[:, :], ps[i][:, :]), "SE")
                                on("scalar", _vev)
                                se_v = p.cnt["SE"]

                                def _vo(y, tt=tt, dqg=dqg, se_v=se_v):
                                    p.w(y, "SE", se_v)
                                    hf = dqg // 2
                                    p.inc(y.dma_start(out=v_lh[hf].ap()[tt * 128:(tt + 1) * 128,
                                                                        (dqg % 2) * 512:(dqg % 2 + 1) * 512],
                                                      in_=vstg[:, :]), "SY2", 16)
                                on("sync", _vo)

                        if NC > 1:
                            def _gkv(g, hfh=hfh):
                                p.w(g, "SY2")
                                rg = [list(range(NC))]
                                p.inc(g.collective_compute("AllGather", ALU.bypass, replica_groups=rg,
                                                           ins=[kT_lh[hfh].ap().opt()],
                                                           outs=[kT_agh[hfh].ap().opt()]), "CC")
                                p.inc(g.collective_compute("AllGather", ALU.bypass, replica_groups=rg,
                                                           ins=[v_lh[hfh].ap().opt()],
                                                           outs=[v_agh[hfh].ap().opt()]), "CC")
                            on("gpsimd", _gkv)
                    flush()
                barrier()

                with contextlib.ExitStack() as fl:
                    kbuf = fl.enter_context(nc_.sbuf_tensor("kbuf", [128, 3, L], BF16))
                    vbuf = fl.enter_context(nc_.sbuf_tensor("vbuf", [128, 3, NT, 128], BF16))
                    pbuf = fl.enter_context(nc_.sbuf_tensor("pbuf", [128, 3, L], BF16))
                    tbuf = fl.enter_context(nc_.sbuf_tensor("tbuf", [128, 3, L], F32))
                    paccum = fl.enter_context(nc_.sbuf_tensor("paccum", [128, 2, L], F32))
                    cmask = fl.enter_context(nc_.sbuf_tensor("cmask", [128, NTG, 128], BF16))
                    lr_sb = fl.enter_context(nc_.sbuf_tensor("lr_sb", [1, L], F32))
                    ao_raw = fl.enter_context(nc_.sbuf_tensor("ao_raw", [128, L], BF16))
                    ps_sT = [fl.enter_context(nc_.psum_tensor(f"ps_sT{i}", [128, L], F32)) for i in range(2)]
                    ps_oT = [fl.enter_context(nc_.psum_tensor(f"ps_oT{i}", [128, L], F32)) for i in range(2)]

                    def _ldm(y):
                        p.inc(y.dma_start(out=cmask[:, :, :],
                                          in_=cmask_in.ap().rearrange("p (g q) -> p g q", g=NTG)), "SY", 16)
                    on("sync", _ldm)
                    sy_m = p.cnt["SY"]

                    sy_kv = {}
                    se_tanh = {}
                    se_exp = {}
                    pe_av = {}
                    pe_av_kv = {}
                    ve_pacc = {}
                    last_g = NTG - 1
                    pidx = 0
                    av_q = []
                    pending_epi1 = None
                    epi_q = []
                    ve_gate = 0
                    se_gate = 0
                    se_ao_cur = 0
                    pe_lmm_mark = {0: 0, 1: 0}
                    sea_mark = {0: 0, 1: 0}
                    veg_mark = {0: 0, 1: 0}

                    NT2 = NT // 2
                    for h in range(NH):
                        for r in range(NC):
                            kvidx = h * NC + r
                            kvp = kvidx % 3

                            def _lkv(y, h=h, r=r, kvidx=kvidx, kvp=kvp):
                                hf, hh = h // 8, h % 8
                                if NC > 1:
                                    p.w(y, "CC", 9 + 2 * hf)
                                if kvidx >= 3:
                                    p.w(y, "PE", pe_av_kv[kvidx - 3])
                                ksrc = kT_agh[hf].ap()[r * HD2 + hh * 128:r * HD2 + (hh + 1) * 128, :]
                                p.inc(y.dma_start(out=kbuf[:, kvp, :], in_=ksrc), f"SYK{kvp}", 16)
                                for vh in range(2):
                                    vsrc = bass.AP(v_agh[hf],
                                                   (r * L + vh * NT2 * 128) * HD2 + hh * 128,
                                                   [[HD2, 128], [128 * HD2, NT2], [1, 128]])
                                    p.inc(y.dma_start(out=vbuf[:, kvp, vh * NT2:(vh + 1) * NT2, :],
                                                      in_=vsrc), f"SYK{kvp}", 16)
                            on("sync", _lkv)
                            sy_kv[kvidx] = (f"SYK{kvp}", p.cnt[f"SYK{kvp}"])

                            for s_ in range(NT):
                                g = s_ * NC + r
                                i0 = _i0_of(g, NC)
                                lo = i0 * 128
                                par = pidx % 2
                                pa3 = pidx % 3
                                hp = h % 2

                                def _qkm(t, r=r, s_=s_, h=h, lo=lo, par=par, kvidx=kvidx,
                                         pidx=pidx, kvp=kvp, g=g):
                                    p.w(t, sy_kv[kvidx][0], sy_kv[kvidx][1])
                                    p.w(t, "SY", sy_m)
                                    if pidx - 2 >= 0 and (pidx - 2) in se_tanh:
                                        p.w(t, "SE", se_tanh[pidx - 2])
                                    mm = None
                                    for (c0, c1) in _chunks512(lo, L):
                                        mm = t.matmul(ps_sT[par][:, c0:c1],
                                                      kbuf[:, kvp, s_ * 128:(s_ + 1) * 128],
                                                      QT[:, h, c0:c1], start=True, stop=True,
                                                      skip_group_check=True)
                                    mm = t.matmul(ps_sT[par][:, lo:lo + 128], ident[:, :],
                                                  cmask[:, g, :], start=False, stop=True,
                                                  skip_group_check=True)
                                    p.inc(mm, "PE")
                                on("tensor", _qkm)
                                pe_here = p.cnt["PE"]

                                if pending_epi1 is None and epi_q:
                                    epi_q.pop(0)()
                                while len(av_q) >= 2:
                                    av_q.pop(0)()

                                def _th(s, par=par, pa3=pa3, lo=lo, pe_here=pe_here, pidx=pidx):
                                    p.w(s, "PE", pe_here)
                                    p.inc(s.activation(tbuf[:, pa3, lo:L], ps_sT[par][:, lo:L],
                                                       AF.Tanh, scale=TANH_SCALE), "SE")
                                on("scalar", _th)
                                se_tanh[pidx] = p.cnt["SE"]
                                se_t = p.cnt["SE"]

                                def _ex(s, pa3=pa3, lo=lo, se_t=se_t, pidx=pidx):
                                    p.w(s, "SE", se_t)
                                    if pidx - 3 >= 0 and (pidx - 3) in pe_av:
                                        p.w(s, "PE", pe_av[pidx - 3])
                                    if pidx - 3 >= 0 and (pidx - 3) in ve_pacc:
                                        p.w(s, "VE", ve_pacc[pidx - 3])
                                    p.inc(s.activation(pbuf[:, pa3, lo:L], tbuf[:, pa3, lo:L],
                                                       AF.Exp, scale=50.0), "SE")
                                on("scalar", _ex)
                                se_exp[pidx] = p.cnt["SE"]
                                se_e = p.cnt["SE"]

                                if pending_epi1 is not None:
                                    pending_epi1()
                                    pending_epi1 = None

                                def _pa(v, g=g, lo=lo, pa3=pa3, se_e=se_e, pidx=pidx, hp=hp):
                                    p.w(v, "SE", se_e)
                                    if pidx - 1 >= 0 and (pidx - 1) in ve_pacc:
                                        p.w(v, "VE", ve_pacc[pidx - 1])
                                    if g == 0 and pe_lmm_mark[hp] > 0:
                                        p.w(v, "PE", pe_lmm_mark[hp])
                                    if g == 0:
                                        p.inc(v.tensor_copy(paccum[:, hp, :], pbuf[:, pa3, :]), "VE")
                                    else:
                                        p.inc(v.tensor_tensor(paccum[:, hp, lo:L], paccum[:, hp, lo:L],
                                                              pbuf[:, pa3, lo:L], op=ALU.add), "VE")
                                on("vector", _pa)
                                ve_pacc[pidx] = p.cnt["VE"]

                                def make_av(r=r, s_=s_, g=g, lo=lo, pa3=pa3, se_e=se_e, hp=hp,
                                            pidx=pidx, kvidx=kvidx, kvp=kvp, first_of_head=(g == 0)):
                                    def _av(t):
                                        p.w(t, "SE", se_e)
                                        if first_of_head:
                                            if sea_mark[hp] > 0:
                                                p.w(t, "SE", sea_mark[hp])
                                            if veg_mark[hp] > 0:
                                                p.w(t, "VE", veg_mark[hp])
                                        mm = None
                                        for (c0, c1) in _chunks512(lo, L):
                                            mm = t.matmul(ps_oT[hp][:, c0:c1], vbuf[:, kvp, s_, :],
                                                          pbuf[:, pa3, c0:c1],
                                                          start=(g == 0), stop=(g == last_g),
                                                          skip_group_check=True)
                                        p.inc(mm, "PE")
                                        pe_av[pidx] = p.cnt["PE"]
                                        pe_av_kv[kvidx] = p.cnt["PE"]
                                    def emit():
                                        on("tensor", _av)
                                    return emit
                                av_q.append(make_av())
                                pidx += 1

                        for _e in av_q:
                            _e()
                        av_q = []
                        pe_hd0 = p.cnt["PE"]
                        ve_pl = ve_pacc[pidx - 1]

                        def make_epi1(pe_hd0=pe_hd0, hp=hp):
                            def emit():
                                nonlocal se_ao_cur

                                def _aoe(s, pe_hd0=pe_hd0, hp=hp):
                                    p.w(s, "PE", pe_hd0)
                                    if ve_gate > 0:
                                        p.w(s, "VE", ve_gate)
                                    p.inc(s.copy(ao_raw[:, :], ps_oT[hp][:, :]), "SE")
                                on("scalar", _aoe)
                                se_ao_cur = p.cnt["SE"]
                                sea_mark[hp] = p.cnt["SE"]
                            return emit

                        epi_state = {}

                        def make_epi2a(hp=hp, ve_pl=ve_pl, epi_state=epi_state):
                            def emit():
                                se_ao = se_ao_cur
                                epi_state["se_ao"] = se_ao

                                def _lmm(t, ve_pl=ve_pl, se_ao=se_ao, hp=hp):
                                    p.w(t, "SE", se_ao)
                                    p.w(t, "VE", ve_pl)
                                    mm = None
                                    for (c0, c1) in _chunks512(0, L):
                                        mm = t.matmul(ps_oT[hp][0:1, c0:c1], ones_cf[:, :],
                                                      paccum[:, hp, c0:c1],
                                                      start=True, stop=True, skip_group_check=True)
                                    p.inc(mm, "PE")
                                on("tensor", _lmm)
                                pe_lmm_mark[hp] = p.cnt["PE"]
                                pe_lmm = p.cnt["PE"]

                                def _lcp(s, pe_lmm=pe_lmm, hp=hp):
                                    p.w(s, "PE", pe_lmm)
                                    p.w(s, "VE", ve_gate)
                                    p.inc(s.copy(lr_sb[:, :], ps_oT[hp][0:1, :]), "SE")
                                on("scalar", _lcp)
                                se_lcp = p.cnt["SE"]

                                def _lev(v, se_lcp=se_lcp):
                                    p.w(v, "SE", se_lcp)
                                    p.inc(v.reciprocal(lr_sb[:, :], lr_sb[:, :]), "VE")
                                on("vector", _lev)
                                epi_state["ve_l"] = p.cnt["VE"]
                            return emit

                        def make_epi2b(h=h, hp=hp, epi_state=epi_state):
                            def emit():
                                nonlocal ve_gate, se_gate
                                se_ao = epi_state["se_ao"]
                                ve_l = epi_state["ve_l"]

                                def _lbc(t, ve_l=ve_l, hp=hp):
                                    p.w(t, "VE", ve_l)
                                    mm = None
                                    for (c0, c1) in _chunks512(0, L):
                                        mm = t.matmul(ps_oT[hp][:, c0:c1], ones_rf[:, :], lr_sb[:, c0:c1],
                                                      start=True, stop=True, skip_group_check=True)
                                    p.inc(mm, "PE")
                                on("tensor", _lbc)
                                pe_lb = p.cnt["PE"]

                                def _aom(v, h=h, se_ao=se_ao, pe_lb=pe_lb, hp=hp):
                                    p.w(v, "SE", se_ao)
                                    p.w(v, "PE", pe_lb)
                                    p.inc(v.tensor_tensor(attno[:, h, :], ao_raw[:, :],
                                                          ps_oT[hp][:, :], op=ALU.mult), "VE")
                                on("vector", _aom)
                                ve_gate = p.cnt["VE"]
                                se_gate = p.cnt["SE"]
                                veg_mark[hp] = p.cnt["VE"]
                            return emit
                        pending_epi1 = make_epi1()
                        epi_q.append(make_epi2a())
                        epi_q.append(make_epi2b())
                    if pending_epi1 is not None:
                        pending_epi1()
                        pending_epi1 = None
                    while epi_q:
                        epi_q.pop(0)()
                    flush()
                barrier()

                with contextlib.ExitStack() as os_:
                    wos = os_.enter_context(nc_.sbuf_tensor("wos", [128, 2, 1024], BF16))
                    ps_o = [os_.enter_context(nc_.psum_tensor(f"ops{i}", [128, 512], F32))
                            for i in range(8)]
                    TTO = min(4, NT)

                    pe_bufo = {0: 0, 1: 0}
                    for fh in range(2):
                        for g0 in range(0, NT, TTO):
                            for kc in range(16):
                                buf = kc % 2

                                def _lwo(y, kc=kc, buf=buf, fh=fh):
                                    if NC > 1:
                                        p.w(y, "CC", 7)
                                    else:
                                        p.w(y, "GPC", 64)
                                    if pe_bufo[buf] == 0:
                                        p.w(y, "PE")
                                    else:
                                        p.w(y, "PE", pe_bufo[buf])
                                    sap = bass.AP(wo_bf, kc * 128 * D + fh * 1024, [[D, 128], [1, 1024]])
                                    p.inc(y.dma_start(out=wos[:, buf, :], in_=sap), f"GW{buf}", 16)
                                on("sync", _lwo)
                                gw_c = (f"GW{buf}", p.cnt[f"GW{buf}"])

                                def _om(t, kc=kc, buf=buf, gw_c=gw_c, g0=g0):
                                    p.w(t, gw_c[0], gw_c[1])
                                    if kc == 0:
                                        p.w(t, "VE")
                                    mm = None
                                    for i in range(TTO):
                                        tt = g0 + i
                                        mm = t.matmul(ps_o[2 * i][:, :], attno[:, kc, tt * 128:(tt + 1) * 128],
                                                      wos[:, buf, 0:512], start=(kc == 0), stop=(kc == 15))
                                        mm = t.matmul(ps_o[2 * i + 1][:, :], attno[:, kc, tt * 128:(tt + 1) * 128],
                                                      wos[:, buf, 512:1024], start=(kc == 0), stop=(kc == 15))
                                    p.inc(mm, "PE")
                                on("tensor", _om)
                                pe_bufo[buf] = p.cnt["PE"]
                            pe_o = p.cnt["PE"]

                            for i in range(TTO):
                                tt = g0 + i

                                def _ores(v, i=i, tt=tt, fh=fh, pe_o=pe_o):
                                    p.w(v, "PE", pe_o)
                                    for cb in range(2):
                                        c0 = fh * 1024 + cb * 512
                                        p.inc(v.scalar_tensor_tensor(x_tm[:, tt, c0:c0 + 512],
                                                                     ps_o[2 * i + cb][:, :], 1.0,
                                                                     x_tm[:, tt, c0:c0 + 512],
                                                                     op0=ALU.mult, op1=ALU.add), "VE")
                                on("vector", _ores)
                    flush()

        def write_x_out():
            with contextlib.ExitStack() as ws:
                stg = ws.enter_context(nc_.sbuf_tensor("stg", [128, D], F32))
                for tt in range(NT):
                    def _c(v, tt=tt):
                        p.w(v, "SY2")
                        p.inc(v.tensor_copy(stg[:, :], x_tm[:, tt, :]), "VE")
                    on("vector", _c)
                    ve_c = p.cnt["VE"]

                    def _o(y, tt=tt, ve_c=ve_c):
                        p.w(y, "VE", ve_c)
                        p.inc(y.dma_start(out=out_t.ap()[tt * 128:(tt + 1) * 128, :], in_=stg[:, :]), "SY2", 16)
                    on("sync", _o)
                flush()

        ffn_stage(first_ffn=True, write_out=False)
        barrier()
        if out_stage >= 2:
            attn_stage()
            barrier()
        if out_stage >= 3:
            ffn_stage(first_ffn=False, write_out=True)
        else:
            write_x_out()
        barrier()

        def _fin(y):
            p.w(y, "SY2")
            p.w(y, "SY")
            p.w(y, "GP")
        on("sync", _fin)
        flush()

    build_program.last_cnt = dict(p.cnt)
    return nc_


def _host_prep(inputs, T, NCORES):
    pi = _pi_src(); rho = _rho_src()
    x = np.asarray(inputs["L0_attn_x"]).reshape(-1, D)[:T]
    wq = np.asarray(inputs["L0_attn_wq"]); wk = np.asarray(inputs["L0_attn_wk"])
    wv = np.asarray(inputs["L0_attn_wv"]); wo = np.asarray(inputs["L0_attn_wo"])
    qnw = np.asarray(inputs["L0_attn_q_norm_w"]); knw = np.asarray(inputs["L0_attn_k_norm_w"])
    cos = np.asarray(inputs["L0_attn_rope_cos"]); sin = np.asarray(inputs["L0_attn_rope_sin"])
    gw = np.asarray(inputs["L0_ffn_gate_w"]); uw = np.asarray(inputs["L0_ffn_up_w"])
    dw = np.asarray(inputs["L0_ffn_down_w"])

    NTG = T // 128
    tiles = [_tiles_of(c, NTG, NCORES) for c in range(NCORES)]
    xp = np.ascontiguousarray(x[:, pi])
    wqT = np.ascontiguousarray(wq[:, pi].T); wkT = np.ascontiguousarray(wk[:, pi].T)
    wvT = np.ascontiguousarray(wv[:, pi].T)
    woT = np.ascontiguousarray(wo[pi, :].T)
    g_p = np.ascontiguousarray(gw[rho][:, pi]); u_p = np.ascontiguousarray(uw[rho][:, pi])
    d_p = np.ascontiguousarray(dw[pi][:, rho])
    R1 = I // NCORES; R2 = D // NCORES

    c16 = cos[:NH]; s16 = sin[:NH]
    crep = np.repeat(c16, 2, axis=1).reshape(-1)
    srep = np.repeat(s16, 2, axis=1).reshape(-1)
    sign = np.tile([-1.0, 1.0], D // 2)

    def mkrows(gv):
        gt = np.tile(gv, NH)
        gswap = gt.reshape(-1, 2)[:, ::-1].reshape(-1)
        cgr = (gt * crep).astype(np.float32)[None]
        sgr = (gswap * srep * sign).astype(np.float32)[None]
        return np.ascontiguousarray(cgr), np.ascontiguousarray(sgr)

    cg_q, sg_q = mkrows(qnw); cg_k, sg_k = mkrows(knw)
    ident = np.eye(128, dtype=BF)

    kk = np.arange(128)[:, None]; qq = np.arange(128)[None, :]
    diag = np.where(kk <= qq, 0.0, -3e9)
    keep = np.zeros((128, 128))
    full = np.full((128, 128), -3e9)

    in_maps = []
    for c in range(NCORES):
        rows = np.concatenate([np.arange(t * 128, (t + 1) * 128) for t in tiles[c]])
        cm = np.empty((128, NTG, 128), dtype=np.float32)
        for g in range(NTG):
            i0 = _i0_of(g, NCORES)
            G = NCORES * i0 + c
            cm[:, g, :] = keep if G > g else (diag if G == g else full)
        RS = D // NCORES
        in_maps.append({
            "x_loc": np.ascontiguousarray(xp[rows]),
            "wqT": np.ascontiguousarray(wqT[c * RS:(c + 1) * RS]),
            "wkT": np.ascontiguousarray(wkT[c * RS:(c + 1) * RS]),
            "wvT": np.ascontiguousarray(wvT[c * RS:(c + 1) * RS]),
            "woT": np.ascontiguousarray(woT[c * RS:(c + 1) * RS]),
            "gsh": np.ascontiguousarray(g_p[c * R1:(c + 1) * R1]),
            "ush": np.ascontiguousarray(u_p[c * R1:(c + 1) * R1]),
            "dsh": np.ascontiguousarray(d_p[c * R2:(c + 1) * R2]),
            "cg_q": cg_q, "sg_q": sg_q, "cg_k": cg_k, "sg_k": sg_k,
            "ident_in": ident,
            "cmask_in": np.ascontiguousarray(cm.reshape(128, NTG * 128).astype(BF)),
        })
    return in_maps, tiles, pi


def _host_finish(results, tiles, pi, T, NCORES):
    out = np.empty((T, D), dtype=np.float32)
    for c in range(NCORES):
        rows = np.concatenate([np.arange(t * 128, (t + 1) * 128) for t in tiles[c]])
        out[rows] = results[c]["out_loc"]
    inv = np.empty(D, dtype=np.int64); inv[pi] = np.arange(D)
    return np.ascontiguousarray(out[:, inv])[None]


def kernel(**inputs):
    T, NCORES = 8192, 8
    in_maps, tiles, pi = _host_prep(inputs, T, NCORES)
    nc_ = build_program(T, NCORES, out_stage=3)
    res = run_bass_kernel_spmd(nc_, in_maps, core_ids=list(range(NCORES)))
    return _host_finish(res.results, tiles, pi, T, NCORES)


# version 32
# speedup vs baseline: 1.0266x; 1.0108x over previous
import contextlib
import numpy as np
import ml_dtypes
import concourse.bass as bass
import concourse.mybir as mybir
from concourse.bass_utils import run_bass_kernel_spmd

F32 = mybir.dt.float32
BF16 = mybir.dt.bfloat16
FP8 = mybir.dt.float8e4
AF = mybir.ActivationFunctionType
ALU = mybir.AluOpType
AX = mybir.AxisListType
BF = ml_dtypes.bfloat16

D = 2048
NH, DH = 16, 128
I = 8192
BLK = 32
NBD = D // BLK
NBI = I // BLK
QCLAMP = 448.0 * 1e-30
LN_EPS = 1e-5
RMS_EPS = 1e-6
TANH_SCALE = float((DH ** (-0.5)) / 50.0)


def _pi_src():
    pos = np.arange(D)
    return (32 * (pos % NBD) + pos // NBD).astype(np.int64)


def _rho_src():
    pos = np.arange(I)
    lf = pos % 2048
    return ((pos // 2048) * 2048 + 32 * (lf % 64) + lf // 64).astype(np.int64)


def _tiles_of(c, ntg, nc):
    return list(range(c, ntg, nc))


def _i0_of(g, nc):
    return max(0, -((-(g - (nc - 1))) // nc))


def _chunks512(lo, hi):
    out = []
    while lo < hi:
        nxt = min(hi, (lo // 512 + 1) * 512)
        out.append((lo, nxt))
        lo = nxt
    return out


def build_program(T, NC, out_stage=3):
    L = T // NC
    NT = L // 128
    NTG = T // 128
    assert T % (128 * NC) == 0 and NT >= 2 and NT % 2 == 0
    R1 = I // NC
    R2 = D // NC
    assert R1 % 128 == 0 and R2 % 128 == 0
    TTG = min(4, NT)
    HALF = TTG // 2

    nc_ = bass.Bass("TRN2", target_bir_lowering=False)

    def din(name, shape, dt=F32):
        return nc_.dram_tensor(name, list(shape), dt, kind="ExternalInput")

    x_in = din("x_loc", [L, D])
    wqT_d = din("wqT", [D // NC, D]); wkT_d = din("wkT", [D // NC, D])
    wvT_d = din("wvT", [D // NC, D])
    woT_d = din("woT", [D // NC, D])
    gsh = din("gsh", [R1, D]); ush = din("ush", [R1, D]); dsh = din("dsh", [R2, I])
    cg_q_in = din("cg_q", [1, D]); sg_q_in = din("sg_q", [1, D])
    cg_k_in = din("cg_k", [1, D]); sg_k_in = din("sg_k", [1, D])
    ident_in = din("ident_in", [128, 128], BF16)
    cmask_in = din("cmask_in", [128, NTG * 128], BF16)
    out_t = nc_.dram_tensor("out_loc", [L, D], F32, kind="ExternalOutput")

    wq_l = nc_.dram_tensor("wq_l", [D // NC, D], BF16)
    wk_l = nc_.dram_tensor("wk_l", [D // NC, D], BF16)
    wv_l = nc_.dram_tensor("wv_l", [D // NC, D], BF16)
    wo_l = nc_.dram_tensor("wo_l", [D // NC, D], BF16)
    gu_g_l = nc_.dram_tensor("gu_g_l", [D, R1], BF16)
    gu_u_l = nc_.dram_tensor("gu_u_l", [D, R1], BF16)
    ddqT_l = nc_.dram_tensor("ddqT_l", [I, R2], BF16)
    HD2 = D // 2
    kT_lh = [nc_.dram_tensor(f"kT_l{a}", [HD2, L], BF16) for a in range(2)]
    v_lh = [nc_.dram_tensor(f"v_l{a}", [L, HD2], BF16) for a in range(2)]
    _AS = "Shared" if NC > 4 else "Local"
    if NC > 1:
        gudqT_g = nc_.dram_tensor("gudqT_g", [NC * D, R1], BF16, addr_space=_AS)
        gudqT_u = nc_.dram_tensor("gudqT_u", [NC * D, R1], BF16, addr_space=_AS)
        ddqT = nc_.dram_tensor("ddqT", [NC * I, R2], BF16, addr_space=_AS)
        wq_bf = nc_.dram_tensor("wq_bf", [D, D], BF16, addr_space=_AS)
        wk_bf = nc_.dram_tensor("wk_bf", [D, D], BF16, addr_space=_AS)
        wv_bf = nc_.dram_tensor("wv_bf", [D, D], BF16, addr_space=_AS)
        wo_bf = nc_.dram_tensor("wo_bf", [D, D], BF16, addr_space=_AS)
        kT_agh = [nc_.dram_tensor(f"kT_ag{a}", [NC * HD2, L], BF16, addr_space=_AS)
                  for a in range(2)]
        v_agh = [nc_.dram_tensor(f"v_ag{a}", [NC * L, HD2], BF16, addr_space=_AS)
                 for a in range(2)]
    else:
        gudqT_g, gudqT_u, ddqT = gu_g_l, gu_u_l, ddqT_l
        wq_bf, wk_bf, wv_bf, wo_bf = wq_l, wk_l, wv_l, wo_l
        kT_agh, v_agh = kT_lh, v_lh

    st = contextlib.ExitStack()
    sems = {}
    for s in ["SY", "GP", "CC", "PE", "VE", "SE", "SY2", "SYX",
              "SYW0", "SYW1", "SYW2", "SYW3", "SYW4", "SYW5", "SYW6", "SYW7",
              "SYD0", "SYD1", "SYD2", "SYD3", "SYD4", "SYD5", "SYD6", "SYD7", "SYK0", "SYK1", "SYK2",
              "GW0", "GW1", "GW2", "GW3", "GW4", "GW5", "GW6", "GW7", "GE", "SYDQ0", "SYDQ1", "SYDQ2", "SYDQ3",
              "SYO0", "SYO1", "GPC", "GPX"]:
        sems[s] = st.enter_context(nc_.semaphore(s))

    class Px:
        def __init__(self):
            self.cnt = {k: 0 for k in sems}

        def inc(self, instr, sem, n=1):
            instr.then_inc(sems[sem], n)
            self.cnt[sem] += n
            return self.cnt[sem]

        def w(self, eng, sem, val=None):
            v = self.cnt[sem] if val is None else val
            if v > 0:
                eng.wait_ge(sems[sem], v)

    p = Px()

    with st:
        x_tm = st.enter_context(nc_.sbuf_tensor("x_tm", [128, NT, D], BF16))
        ident = st.enter_context(nc_.sbuf_tensor("ident", [128, 128], BF16))
        ones_cb = st.enter_context(nc_.sbuf_tensor("ones_cb", [128, 1], BF16))
        ones_cf = st.enter_context(nc_.sbuf_tensor("ones_cf", [128, 1], F32))
        ones_rf = st.enter_context(nc_.sbuf_tensor("ones_rf", [1, 128], F32))
        cst_eps = st.enter_context(nc_.sbuf_tensor("cst_eps", [128, 1], F32))
        nc_.const_aps.register(F32, LN_EPS, cst_eps[:, :])
        block = st.enter_context(nc_.Block())

        ENGLOG = {e: [] for e in ("tensor", "vector", "scalar", "gpsimd", "sync")}

        class _IP:
            __slots__ = ("incs",)
            def __init__(self):
                self.incs = []
            def then_inc(self, sem, n=1):
                self.incs.append((sem, n))
                return self

        class _EP:
            __slots__ = ("log",)
            def __init__(self, log):
                self.log = log
            def __getattr__(self, meth):
                log = self.log
                def call(*a, **k):
                    ip = _IP()
                    log.append((meth, a, k, ip))
                    return ip
                return call

        def on(engine_name, fn):
            fn(_EP(ENGLOG[engine_name]))

        def flush():
            for name in ("sync", "gpsimd", "tensor", "vector", "scalar"):
                log = ENGLOG[name]
                if not log:
                    continue
                ENGLOG[name] = []
                def _replay(eng, log=log):
                    for meth, a, k, ip in log:
                        r = getattr(eng, meth)(*a, **k)
                        for (sem, n) in ip.incs:
                            r.then_inc(sem, n)
                getattr(block, name)(_replay)

        def barrier():
            snap = {k: v for k, v in p.cnt.items() if k not in ("GPC", "CC")}
            for en in ["tensor", "vector", "scalar", "gpsimd", "sync"]:
                def _f(e, snap=snap):
                    for sn, v in snap.items():
                        if v > 0:
                            e.wait_ge(sems[sn], v)
                on(en, _f)
            flush()

        def _init_sync(y):
            p.inc(y.dma_start(out=ident[:, :], in_=ident_in.ap()), "SY", 16)
        on("sync", _init_sync)

        def _ldx_all(g):
            p.inc(g.dma_start(out=x_tm[:, :, :],
                              in_=x_in.ap().rearrange("(t p) d -> p t d", p=128)), "GPX", 16)
        on("gpsimd", _ldx_all)

        def _init_vec(v):
            v.memset(ones_cb[:, :], 1.0)
            v.memset(ones_cf[:, :], 1.0)
            v.memset(cst_eps[:, :], LN_EPS)
            p.inc(v.memset(ones_rf[:, :], 1.0), "VE")
        on("vector", _init_vec)

        def _cast_attn_w(g):
            for wsrc, wdst in ((wqT_d, wq_l), (wkT_d, wk_l), (wvT_d, wv_l), (woT_d, wo_l)):
                p.inc(g.dma_start(out=wdst.ap(), in_=wsrc.ap()), "GPC", 16)
        on("gpsimd", _cast_attn_w)

        with contextlib.ExitStack() as dq:
            wnat = dq.enter_context(nc_.sbuf_tensor("wnat", [128, 4, 2048], F32))
            wdq = dq.enter_context(nc_.sbuf_tensor("wdq", [128, 4, 2048], BF16))
            q8w = dq.enter_context(nc_.sbuf_tensor("q8w", [128, 4, 2048], FP8))
            bmw = dq.enter_context(nc_.sbuf_tensor("bmw", [128, 4, 64], F32))
            r2w = dq.enter_context(nc_.sbuf_tensor("r2w", [128, 4, 64], F32))
            s2w = dq.enter_context(nc_.sbuf_tensor("s2w", [128, 4, 64], F32))
            trs = dq.enter_context(nc_.sbuf_tensor("trs", [128, 2, 16, 512], BF16))
            dps = [dq.enter_context(nc_.psum_tensor(f"dps{i}", [128, 1024], BF16))
                   for i in range(2)]
            NBU = 2048 // BLK
            dq_state = {"i": 0,
                        "ve_mark": {b: 0 for b in range(4)},
                        "ge_mark": {b: 0 for b in range(4)},
                        "tr_mark": {b: 0 for b in range(4)},
                        "pending_dq2": None}

            def dequant_unit(in_ap, batch_out):
                b = dq_state["i"] % 4
                w = (dq_state["i"] // 4) % 2
                dq_state["i"] += 1

                def _load(y, b=b, in_ap=in_ap):
                    p.w(y, "VE", dq_state["ve_mark"][b])
                    p.w(y, "GE", dq_state["ge_mark"][b])
                    p.inc(y.dma_start(out=wnat[:, b, :], in_=in_ap), f"SYDQ{b}", 16)
                on("sync", _load)
                sy_h = p.cnt[f"SYDQ{b}"]

                def _scales(v, sy_h=sy_h, b=b):
                    p.w(v, f"SYDQ{b}", sy_h)
                    xv = wnat[:, b, :].rearrange("p (j bb) -> p bb j", bb=NBU, j=BLK)
                    p.inc(v.tensor_reduce(bmw[:, b, :], xv, axis=AX.X, op=ALU.max,
                                          apply_absolute_value=True), "VE")
                    p.w(v, "VE")
                    p.inc(v.tensor_scalar(bmw[:, b, :], bmw[:, b, :], QCLAMP, None, ALU.max), "VE")
                    p.w(v, "VE")
                    p.inc(v.reciprocal(r2w[:, b, :], bmw[:, b, :]), "VE")
                    p.w(v, "VE")
                    p.inc(v.tensor_scalar(r2w[:, b, :], r2w[:, b, :], 224.0, None, ALU.mult), "VE")
                    p.inc(v.tensor_scalar(s2w[:, b, :], bmw[:, b, :], 1.0 / 224.0, None, ALU.mult), "VE")
                on("vector", _scales)
                ve_s = p.cnt["VE"]

                def _y(g, b=b, ve_s=ve_s):
                    p.w(g, "VE", ve_s)
                    rb = r2w[:, b, :].unsqueeze(1).broadcast_to([128, BLK, NBU])
                    p.inc(g.tensor_tensor(q8w[:, b, :].rearrange("p (j bb) -> p j bb", bb=NBU),
                                          wnat[:, b, :].rearrange("p (j bb) -> p j bb", bb=NBU),
                                          rb, op=ALU.mult), "GE")
                on("gpsimd", _y)
                dq_state["ge_mark"][b] = p.cnt["GE"]
                ge_y = p.cnt["GE"]

                def tail(b=b, w=w, ge_y=ge_y, batch_out=batch_out):
                    def _dq2(v, b=b, ge_y=ge_y):
                        p.w(v, "GE", ge_y)
                        if dq_state["tr_mark"][b] > 0:
                            p.w(v, "PE", dq_state["tr_mark"][b])
                        sb_ = s2w[:, b, :].unsqueeze(1).broadcast_to([128, BLK, NBU])
                        p.inc(v.tensor_tensor(wdq[:, b, :].rearrange("p (j bb) -> p j bb", bb=NBU),
                                              q8w[:, b, :].rearrange("p (j bb) -> p j bb", bb=NBU),
                                              sb_, op=ALU.mult), "VE")
                    on("vector", _dq2)
                    dq_state["ve_mark"][b] = p.cnt["VE"]
                    ve_dq = p.cnt["VE"]

                    pe0 = p.cnt["PE"]
                    se0 = p.cnt["SE"]

                    def _tr(t, ve_dq=ve_dq, se0=se0, b=b):
                        p.w(t, "SY")
                        p.w(t, "VE", ve_dq)
                        p.w(t, "SE")
                        for kc in range(16):
                            if kc >= 2:
                                t.wait_ge(sems["SE"], se0 + kc - 1)
                            p.inc(t.transpose(dps[kc % 2][:, :128],
                                              wdq[:, b, kc * 128:(kc + 1) * 128], ident[:, :]), "PE")
                    on("tensor", _tr)
                    dq_state["tr_mark"][b] = p.cnt["PE"]

                    def _ev(s, pe0=pe0, b=b, w=w):
                        p.w(s, f"SYO{w}")
                        for kc in range(16):
                            s.wait_ge(sems["PE"], pe0 + kc + 1)
                            p.inc(s.copy(trs[:, w, kc, b * 128:(b + 1) * 128], dps[kc % 2][:, :128]), "SE")
                    on("scalar", _ev)

                    if b == 3:
                        se_e = p.cnt["SE"]

                        def _out(s, w=w, se_e=se_e, batch_out=batch_out):
                            s.wait_ge(sems["SE"], se_e)
                            for dst_ap, src_ap in batch_out(w):
                                p.inc(s.dma_start(out=dst_ap, in_=src_ap), f"SYO{w}", 16)
                        on("scalar", _out)

                if dq_state["pending_dq2"] is not None:
                    dq_state["pending_dq2"]()
                dq_state["pending_dq2"] = tail

            def flush_tail():
                if dq_state["pending_dq2"] is not None:
                    dq_state["pending_dq2"]()
                    dq_state["pending_dq2"] = None

            def _emit_gather(ins_t, outs_t, wv, pre=None):
                if NC <= 1:
                    return

                def _g(g, ins_t=ins_t, outs_t=outs_t, wv=wv, pre=pre):
                    p.w(g, "SYO0", wv[0])
                    p.w(g, "SYO1", wv[1])
                    if pre is not None:
                        pre(g)
                    p.inc(g.collective_compute("AllGather", ALU.bypass,
                                               replica_groups=[list(range(NC))],
                                               ins=[ins_t.ap().opt()], outs=[outs_t.ap().opt()]), "CC")
                on("gpsimd", _g)

            syo_after = {}
            for src_t, dst in ((gsh, gu_g_l), (ush, gu_u_l)):
                for rt in range(R1 // 128):
                    bo = None
                    if rt % 4 == 3:
                        def bo(w, rt0=rt - 3, dst=dst):
                            return [(bass.AP(dst, rt0 * 128 + q * 4 * 128 * R1,
                                             [[R1, 128], [128 * R1, 4], [1, 512]]),
                                     trs[:, w, q * 4:(q + 1) * 4, :]) for q in range(4)]
                    dequant_unit(src_t.ap()[rt * 128:(rt + 1) * 128, :], bo)
                flush_tail()
                syo_after[dst.name] = (p.cnt["SYO0"], p.cnt["SYO1"])

            _emit_gather(gu_g_l, gudqT_g, syo_after[gu_g_l.name])

            NRD = R2 // 128
            for rt in range(NRD):
                for s4 in range(4):
                    bo = None
                    if s4 == 3:
                        def bo(w, rt=rt):
                            return [(bass.AP(ddqT_l, rt * 128 + q4 * (16 * 128 * R2),
                                             [[R2, 128], [128 * R2, 16], [1, 128]]),
                                     trs[:, w, :, q4 * 128:(q4 + 1) * 128]) for q4 in range(4)]
                    dequant_unit(bass.AP(dsh, (rt * 128) * I + s4 * 2048, [[I, 128], [1, 2048]]), bo)
                if rt == (NRD - 1) // 2:
                    _emit_gather(gu_u_l, gudqT_u, syo_after[gu_u_l.name])
            flush_tail()
            _emit_gather(ddqT_l, ddqT, (p.cnt["SYO0"], p.cnt["SYO1"]))

            def _pre_w(g):
                p.w(g, "GPC", 64)
            for lt, gt in ((wq_l, wq_bf), (wk_l, wk_bf), (wv_l, wv_bf), (wo_l, wo_bf)):
                _emit_gather(lt, gt, (0, 0), pre=_pre_w if lt is wq_l else None)
            flush()
        barrier()

        def ln_stats(v, src_ap, scr_ap, nst):
            p.inc(v.tensor_reduce(nst[:, 0:1], src_ap, axis=AX.X, op=ALU.add, negate=True), "VE")
            p.inc(v.scalar_tensor_tensor(scr_ap, src_ap, 1.0, src_ap,
                                         op0=ALU.mult, op1=ALU.mult, accum_out=nst[:, 1:2]), "VE")
            p.w(v, "VE")
            p.inc(v.tensor_scalar(nst[:, 0:1], nst[:, 0:1], 1.0 / D, None, ALU.mult), "VE")
            p.w(v, "VE")
            p.inc(v.tensor_tensor(nst[:, 2:3], nst[:, 0:1], nst[:, 0:1], op=ALU.mult), "VE")
            p.w(v, "VE")
            p.inc(v.scalar_tensor_tensor(nst[:, 3:4], nst[:, 1:2], 1.0 / D, nst[:, 2:3],
                                         op0=ALU.mult, op1=ALU.subtract), "VE")

        def ln_apply(src_ap, dst_ap, nst):
            ve_v = p.cnt["VE"]

            def _sq(s, ve_v=ve_v):
                p.w(s, "VE", ve_v)
                p.inc(s.activation(nst[:, 4:5], nst[:, 3:4], AF.Sqrt, bias=LN_EPS), "SE")
            on("scalar", _sq)
            se_s = p.cnt["SE"]

            def _xn(v, se_s=se_s):
                p.w(v, "SE", se_s)
                p.inc(v.reciprocal(nst[:, 5:6], nst[:, 4:5]), "VE")
                p.w(v, "VE")
                p.inc(v.tensor_scalar(dst_ap, src_ap, nst[:, 0:1], nst[:, 5:6],
                                      op0=ALU.add, op1=ALU.mult), "VE")
                p.w(v, "VE")
            on("vector", _xn)

        def ffn_stage(first_ffn, write_out):
            sfx = "_f1" if first_ffn else "_f3"
            with contextlib.ExitStack() as fs:
                xn = fs.enter_context(nc_.sbuf_tensor("xn" + sfx, [128, D], F32))
                nst = fs.enter_context(nc_.sbuf_tensor("nst" + sfx, [128, 8], F32))
                q8x = fs.enter_context(nc_.sbuf_tensor("q8x" + sfx, [128, D], FP8))
                xq_tm = fs.enter_context(nc_.sbuf_tensor("xq_tm" + sfx, [128, TTG, D], BF16))
                xqT = fs.enter_context(nc_.sbuf_tensor("xqT" + sfx, [128, TTG, 16, 128], BF16))
                bmx = fs.enter_context(nc_.sbuf_tensor("bmx" + sfx, [128, NBI], F32))
                r2x = fs.enter_context(nc_.sbuf_tensor("r2x" + sfx, [128, NBI], F32))
                s2x = fs.enter_context(nc_.sbuf_tensor("s2x" + sfx, [128, NBI], F32))
                hbuf = fs.enter_context(nc_.sbuf_tensor("hbuf" + sfx, [128, TTG, I], BF16))
                q8h = fs.enter_context(nc_.sbuf_tensor("q8h" + sfx, [128, I], FP8))
                guw = fs.enter_context(nc_.sbuf_tensor("guw" + sfx, [128, 4, 2, 4, 512], BF16))
                dnw = fs.enter_context(nc_.sbuf_tensor("dnw" + sfx, [128, 4, 1024], BF16))
                outs = fs.enter_context(nc_.sbuf_tensor("outs" + sfx, [128, D], F32))
                sg_ev = fs.enter_context(nc_.sbuf_tensor("sg_ev" + sfx, [128, 512], F32))
                ps_gu = [fs.enter_context(nc_.psum_tensor(f"psg{i}" + sfx, [128, 512], F32))
                         for i in range(8)]
                psA = ps_gu[6][:, :].bitcast(BF16)
                psB = ps_gu[7][:, :].bitcast(BF16)

                def quant_rowblock(src_ap, qdst_ap, dqdst_ap, nblk, feat):
                    nch = feat // 2048

                    def _st(v):
                        p.w(v, "GE")
                        xv = src_ap.rearrange("p (c j b) -> p c b j", c=nch, j=BLK, b=64)
                        bmv = bmx[:, :nblk].rearrange("p (c b) -> p c b", c=nch)
                        p.inc(v.tensor_reduce(bmv, xv, axis=AX.X, op=ALU.max,
                                              apply_absolute_value=True), "VE")
                        p.w(v, "VE")
                        p.inc(v.tensor_scalar(bmx[:, :nblk], bmx[:, :nblk], QCLAMP, None, ALU.max), "VE")
                        p.w(v, "VE")
                        p.inc(v.reciprocal(r2x[:, :nblk], bmx[:, :nblk]), "VE")
                        p.w(v, "VE")
                        p.inc(v.tensor_scalar(r2x[:, :nblk], r2x[:, :nblk], 224.0, None, ALU.mult), "VE")
                        p.inc(v.tensor_scalar(s2x[:, :nblk], bmx[:, :nblk], 1.0 / 224.0, None, ALU.mult), "VE")
                    on("vector", _st)
                    ve_s = p.cnt["VE"]

                    def _yd(v, ve_s=ve_s):
                        p.w(v, "VE", ve_s)
                        rb = (r2x[:, :nblk].rearrange("p (c b) -> p c b", c=nch)
                              .unsqueeze(2).broadcast_to([128, nch, BLK, 64]))
                        sb_ = (s2x[:, :nblk].rearrange("p (c b) -> p c b", c=nch)
                               .unsqueeze(2).broadcast_to([128, nch, BLK, 64]))
                        p.inc(v.tensor_tensor(qdst_ap.rearrange("p (c j b) -> p c j b", c=nch, j=BLK, b=64),
                                              src_ap.rearrange("p (c j b) -> p c j b", c=nch, j=BLK, b=64),
                                              rb, op=ALU.mult), "VE")
                        p.w(v, "VE")
                        p.inc(v.tensor_tensor(dqdst_ap.rearrange("p (c j b) -> p c j b", c=nch, j=BLK, b=64),
                                              qdst_ap.rearrange("p (c j b) -> p c j b", c=nch, j=BLK, b=64),
                                              sb_, op=ALU.mult), "VE")
                    on("vector", _yd)

                def chunk_transpose(src_ap, dst_fn, nkc):
                    pe0 = p.cnt["PE"]
                    se0 = p.cnt["SE"]

                    def _tr(t, se0=se0):
                        p.w(t, "VE")
                        p.w(t, "SE")
                        for kc in range(nkc):
                            if kc >= 2:
                                t.wait_ge(sems["SE"], se0 + kc - 1)
                            pst = (psA, psB)[kc % 2]
                            p.inc(t.transpose(pst[:, :128], src_ap[:, kc * 128:(kc + 1) * 128], ident[:, :]), "PE")
                    on("tensor", _tr)

                    def _ev(s):
                        for kc in range(nkc):
                            pst = (psA, psB)[kc % 2]
                            s.wait_ge(sems["PE"], pe0 + kc + 1)
                            p.inc(s.copy(dst_fn(kc), pst[:, :128]), "SE")
                    on("scalar", _ev)

                se_xqT_by_group = {}
                prep_pe_mark = [0]

                def prep_ve(ttg0):
                    for i in range(TTG):
                        tt = ttg0 + i
                        if first_ffn:
                            def _wx(v, tt=tt):
                                p.w(v, "GPX", 16)
                            on("vector", _wx)

                        def _ln(v, tt=tt):
                            p.w(v, "SY2")
                            if prep_pe_mark[0] > 0:
                                p.w(v, "PE", prep_pe_mark[0])
                            ln_stats(v, x_tm[:, tt, :], xn[:, :], nst)
                        on("vector", _ln)
                        ln_apply(x_tm[:, tt, :], xn[:, :], nst)
                        quant_rowblock(xn[:, :], q8x[:, :], xq_tm[:, i, :], NBD, D)

                def prep_tr(ttg0):
                    for i in range(TTG):
                        chunk_transpose(xq_tm[:, i, :], lambda kc, i=i: xqT[:, i, kc, :], 16)
                    prep_pe_mark[0] = p.cnt["PE"]
                    se_xqT_by_group[ttg0] = p.cnt["SE"]

                ngroups = NT // TTG
                prep_ve(0)
                prep_tr(0)
                for ttg0 in range(0, NT, TTG):
                    g_n = TTG
                    se_xqT = se_xqT_by_group[ttg0]

                    NCCG = I // 512
                    pe_after = {}
                    ve_evac = {0: 0, 1: 0}
                    pe_mm_h = {0: 0, 1: 0}
                    for ccg in range(NCCG):
                        rank = (ccg * 512) // R1
                        coff = (ccg * 512) % R1
                        for wi in range(2):
                            wt = gudqT_g if wi == 0 else gudqT_u
                            for kg in range(4):
                                def _ldw(y, rank=rank, coff=coff, kg=kg, ccg=ccg, wi=wi, wt=wt):
                                    if NC > 1:
                                        p.w(y, "CC", 1 + wi)
                                    key = (wi, ccg - 1, kg)
                                    if key in pe_after:
                                        p.w(y, "PE", pe_after[key])
                                    src_ap = bass.AP(wt, (rank * D + kg * 4 * 128) * R1 + coff,
                                                     [[R1, 128], [128 * R1, 4], [1, 512]])
                                    p.inc(y.dma_start(out=guw[:, kg, wi, :, :], in_=src_ap),
                                          f"SYW{kg + 4 * wi}", 16)
                                on("sync", _ldw)
                                sy_w = (f"SYW{kg + 4 * wi}", p.cnt[f"SYW{kg + 4 * wi}"])

                                for half in range(2):
                                    hb = half * 4
                                    tts = list(range(half * HALF, half * HALF + HALF))

                                    def _mm(t, kg=kg, sy_w=sy_w, ccg=ccg, tts=tts, hb=hb,
                                            half=half, wi=wi):
                                        p.w(t, sy_w[0], sy_w[1])
                                        p.w(t, "SE", se_xqT)
                                        if NC > 1:
                                            p.w(t, "CC", 1 + wi)
                                        if ccg == 0 and wi == 0 and kg == 0:
                                            p.w(t, "VE")
                                        if ccg > 0 and wi == 0 and kg == 0:
                                            p.w(t, "VE", ve_evac[half])
                                        mm = None
                                        for k4 in range(4):
                                            k = kg * 4 + k4
                                            for i_ in tts:
                                                mm = t.matmul(ps_gu[hb + 2 * (i_ - tts[0]) + wi][:, :],
                                                              xqT[:, i_, k, :],
                                                              guw[:, kg, wi, k4, :],
                                                              start=(k == 0), stop=(k == 15))
                                        p.inc(mm, "PE")
                                    on("tensor", _mm)
                                    if wi == 1 and kg == 3:
                                        pe_mm_h[half] = p.cnt["PE"]
                                pe_after[(wi, ccg, kg)] = p.cnt["PE"]
                        pe_mm = p.cnt["PE"]

                        for half in range(2):
                            hb = half * 4
                            for j, i_ in enumerate(range(half * HALF, half * HALF + HALF)):
                                def _si(s, hb=hb, j=j, pe_mm=pe_mm_h[half]):
                                    p.w(s, "PE", pe_mm)
                                    p.w(s, "VE")
                                    p.inc(s.activation(sg_ev[:, :], ps_gu[hb + 2 * j][:, :], AF.Silu), "SE")
                                on("scalar", _si)
                                se_g = p.cnt["SE"]

                                def _h(v, hb=hb, j=j, i_=i_, ccg=ccg, se_g=se_g):
                                    p.w(v, "SE", se_g)
                                    p.inc(v.tensor_tensor(hbuf[:, i_, ccg * 512:(ccg + 1) * 512],
                                                          sg_ev[:, :], ps_gu[hb + 2 * j + 1][:, :],
                                                          op=ALU.mult), "VE")
                                on("vector", _h)
                            ve_evac[half] = p.cnt["VE"]

                    for i_ in range(g_n):
                        quant_rowblock(hbuf[:, i_, :], q8h[:, :], hbuf[:, i_, :], NBI, I)
                        chunk_transpose(hbuf[:, i_, :],
                                        lambda kc, i_=i_: hbuf[:, i_, kc * 128:(kc + 1) * 128], 64)

                    dn_ve = {0: p.cnt["VE"], 1: 0}
                    dn_se = {0: p.cnt["SE"], 1: p.cnt["SE"]}
                    if ttg0 + TTG < NT:
                        prep_ve(ttg0 + TTG)

                    pe_after_d = {}
                    for fh in range(2):
                        for kc in range(64):
                            buf = kc % 4

                            def _ldd(y, fh=fh, kc=kc, buf=buf):
                                if NC > 1:
                                    p.w(y, "CC", 3)
                                key = fh * 64 + kc - 4
                                if key >= 0:
                                    p.w(y, "PE", pe_after_d[key])
                                c0 = fh * 1024
                                if R2 >= 1024:
                                    src_ap = bass.AP(ddqT, ((c0 // R2) * I + kc * 128) * R2 + (c0 % R2),
                                                     [[R2, 128], [1, 1024]])
                                    dst_ap = dnw[:, buf, :]
                                else:
                                    nr = 1024 // R2
                                    src_ap = bass.AP(ddqT, ((c0 // R2) * I + kc * 128) * R2,
                                                     [[R2, 128], [I * R2, nr], [1, R2]])
                                    dst_ap = dnw[:, buf, :].rearrange("p (n r) -> p n r", n=nr)
                                p.inc(y.dma_start(out=dst_ap, in_=src_ap), f"SYD{buf}", 16)
                            on("sync", _ldd)
                            sy_d = (f"SYD{buf}", p.cnt[f"SYD{buf}"])

                            def _dmm(t, kc=kc, buf=buf, sy_d=sy_d, fh=fh):
                                p.w(t, sy_d[0], sy_d[1])
                                if NC > 1:
                                    p.w(t, "CC", 3)
                                if kc == 0:
                                    p.w(t, "SE", dn_se[fh])
                                    p.w(t, "VE", dn_ve[fh])
                                mm = None
                                for i_ in range(g_n):
                                    mm = t.matmul(ps_gu[i_ * 2][:, :], hbuf[:, i_, kc * 128:(kc + 1) * 128],
                                                  dnw[:, buf, 0:512], start=(kc == 0), stop=(kc == 63))
                                    mm = t.matmul(ps_gu[i_ * 2 + 1][:, :], hbuf[:, i_, kc * 128:(kc + 1) * 128],
                                                  dnw[:, buf, 512:1024], start=(kc == 0), stop=(kc == 63))
                                p.inc(mm, "PE")
                            on("tensor", _dmm)
                            pe_after_d[fh * 64 + kc] = p.cnt["PE"]
                        pe_dn = p.cnt["PE"]

                        for i_ in range(g_n):
                            tt = ttg0 + i_

                            def _res(v, i_=i_, tt=tt, fh=fh, pe_dn=pe_dn):
                                p.w(v, "PE", pe_dn)
                                if write_out:
                                    p.w(v, "SY2")
                                for cb in range(2):
                                    c0 = fh * 1024 + cb * 512
                                    dst = outs[:, c0:c0 + 512] if write_out else x_tm[:, tt, c0:c0 + 512]
                                    p.inc(v.scalar_tensor_tensor(dst, ps_gu[i_ * 2 + cb][:, :], 0.5,
                                                                 x_tm[:, tt, c0:c0 + 512],
                                                                 op0=ALU.mult, op1=ALU.add), "VE")
                            on("vector", _res)

                            if write_out:
                                ve_o = p.cnt["VE"]

                                def _sto(y, tt=tt, fh=fh, ve_o=ve_o):
                                    p.w(y, "VE", ve_o)
                                    c0 = fh * 1024
                                    p.inc(y.dma_start(out=out_t.ap()[tt * 128:(tt + 1) * 128, c0:c0 + 1024],
                                                      in_=outs[:, c0:c0 + 1024]), "SY2", 16)
                                on("sync", _sto)

                        if fh == 0:
                            dn_ve[1] = p.cnt["VE"]
                            if ttg0 + TTG < NT:
                                prep_tr(ttg0 + TTG)
                                dn_se[1] = p.cnt["SE"]
                flush()

        TTA = min(4, NT)

        def attn_stage():
            with contextlib.ExitStack() as ats:
                QT = ats.enter_context(nc_.sbuf_tensor("QT", [128, NH, L], BF16))
                attno = ats.enter_context(nc_.sbuf_tensor("attno", [128, NH, L], BF16))

                with contextlib.ExitStack() as qs:
                    xnb = qs.enter_context(nc_.sbuf_tensor("xnb", [128, D], BF16))
                    xnT = qs.enter_context(nc_.sbuf_tensor("xnT", [128, NT, 16, 128], BF16))
                    sq2 = qs.enter_context(nc_.sbuf_tensor("sq2", [128, 2, D], F32))
                    nst = qs.enter_context(nc_.sbuf_tensor("nst2", [128, 8], F32))
                    cgq = qs.enter_context(nc_.sbuf_tensor("cgq", [128, D], BF16))
                    sgq = qs.enter_context(nc_.sbuf_tensor("sgq", [128, D], BF16))
                    cgk = qs.enter_context(nc_.sbuf_tensor("cgk", [128, D], BF16))
                    sgk = qs.enter_context(nc_.sbuf_tensor("sgk", [128, D], BF16))
                    wqk = qs.enter_context(nc_.sbuf_tensor("wqk", [128, 8, 2, 512], BF16))
                    msq = qs.enter_context(nc_.sbuf_tensor("msq", [128, 2, 4], F32))
                    rstd = qs.enter_context(nc_.sbuf_tensor("rstd", [128, 2, 4], F32))
                    qtm = qs.enter_context(nc_.sbuf_tensor("qtm", [128, 512], BF16))
                    ktm = qs.enter_context(nc_.sbuf_tensor("ktm", [128, 512], BF16))
                    asc = qs.enter_context(nc_.sbuf_tensor("asc", [128, 2, 512], BF16))
                    bsc = qs.enter_context(nc_.sbuf_tensor("bsc", [128, 2, 512], BF16))
                    kst = qs.enter_context(nc_.sbuf_tensor("kst", [128, 4, 128], BF16))
                    vstg = qs.enter_context(nc_.sbuf_tensor("vstg", [128, 512], BF16))
                    ps = [qs.enter_context(nc_.psum_tensor(f"aps{i}", [128, 512], F32))
                          for i in range(8)]

                    for ri, (rin, dstt) in enumerate(((cg_q_in, cgq), (sg_q_in, sgq),
                                                      (cg_k_in, cgk), (sg_k_in, sgk))):
                        def _lr(y, rin=rin):
                            p.w(y, "PE")
                            p.inc(y.dma_start(out=sq2[0:1, 0, :], in_=rin.ap()), "SYX", 16)
                        on("sync", _lr)
                        sy_r = p.cnt["SYX"]

                        def _b1(t, sy_r=sy_r):
                            p.w(t, "SYX", sy_r)
                            p.w(t, "VE")
                            p.w(t, "SE")
                            mm = None
                            for j in range(4):
                                mm = t.matmul(ps[j][:, :], ones_rf[:, :], sq2[0:1, 0, j * 512:(j + 1) * 512],
                                              start=True, stop=True)
                            p.inc(mm, "PE")
                        on("tensor", _b1)
                        pe_r = p.cnt["PE"]

                        def _c1(s, dstt=dstt, pe_r=pe_r):
                            p.w(s, "PE", pe_r)
                            for j in range(4):
                                p.inc(s.copy(dstt[:, j * 512:(j + 1) * 512], ps[j][:, :]), "SE")
                        on("scalar", _c1)
                    se_rope = p.cnt["SE"]

                    psA2 = ps[6][:, :].bitcast(BF16)
                    psB2 = ps[7][:, :].bitcast(BF16)
                    pe_buf = {i: 0 for i in range(8)}
                    ms_mark = {0: 0, 1: 0}
                    sqr_mark = {0: 0, 1: 0}
                    trq_mark = {0: 0, 1: 0}

                    for tt in range(NT):
                        def _ln2(v, tt=tt):
                            p.w(v, "SE")
                            ln_stats(v, x_tm[:, tt, :], sq2[:, 0, :], nst)
                        on("vector", _ln2)
                        ln_apply(x_tm[:, tt, :], xnb[:, :], nst)

                        pe0 = p.cnt["PE"]
                        se0 = p.cnt["SE"]

                        def _trx(t, se0=se0):
                            p.w(t, "VE")
                            p.w(t, "SE")
                            for kc in range(16):
                                if kc >= 2:
                                    t.wait_ge(sems["SE"], se0 + kc - 1)
                                pst = (psA2, psB2)[kc % 2]
                                p.inc(t.transpose(pst[:, :128], xnb[:, kc * 128:(kc + 1) * 128],
                                                  ident[:, :]), "PE")
                        on("tensor", _trx)

                        def _evx(s, pe0=pe0, tt=tt):
                            for kc in range(16):
                                pst = (psA2, psB2)[kc % 2]
                                s.wait_ge(sems["PE"], pe0 + kc + 1)
                                p.inc(s.copy(xnT[:, tt, kc, :], pst[:, :128]), "SE")
                        on("scalar", _evx)
                    se_xnT = p.cnt["SE"]

                    for hfh in range(2):
                        for dqg in (2 * hfh, 2 * hfh + 1):
                          for g0 in range(0, NT, TTA):
                            for k in range(16):
                                buf = k % 8

                                def _lw(y, k=k, buf=buf, dqg=dqg):
                                    if NC > 1:
                                        p.w(y, "CC", 5)
                                    else:
                                        p.w(y, "GPC", 64)
                                    p.w(y, "PE", pe_buf[buf])
                                    for wi, wt in enumerate((wq_bf, wk_bf)):
                                        sap = bass.AP(wt, k * 128 * D + dqg * 512, [[D, 128], [1, 512]])
                                        p.inc(y.dma_start(out=wqk[:, buf, wi, :], in_=sap), f"GW{buf}", 16)
                                on("sync", _lw)
                                gw_c = (f"GW{buf}", p.cnt[f"GW{buf}"])

                                def _qkm(t, k=k, buf=buf, gw_c=gw_c, dqg=dqg, g0=g0):
                                    p.w(t, gw_c[0], gw_c[1])
                                    p.w(t, "SE", se_xnT)
                                    if k == 0:
                                        p.w(t, "VE")
                                        p.w(t, "SE")
                                    mm = None
                                    for i in range(TTA):
                                        mm = t.matmul(ps[i][:, :], xnT[:, g0 + i, k, :], wqk[:, buf, 0, :],
                                                      start=(k == 0), stop=(k == 15))
                                        mm = t.matmul(ps[4 + i][:, :], xnT[:, g0 + i, k, :], wqk[:, buf, 1, :],
                                                      start=(k == 0), stop=(k == 15))
                                    p.inc(mm, "PE")
                                on("tensor", _qkm)
                                pe_buf[buf] = p.cnt["PE"]
                            pe_qk = p.cnt["PE"]

                            for i in range(TTA):
                                tt = g0 + i
                                for which in range(2):
                                    psu = ps[i] if which == 0 else ps[4 + i]
                                    cg_t = cgq if which == 0 else cgk
                                    sg_t = sgq if which == 0 else sgk
                                    dtm = qtm if which == 0 else ktm
                                    ch = which

                                    def _sq2(s, psu=psu, ch=ch):
                                        p.w(s, "PE", pe_qk)
                                        p.w(s, "SE", se_rope)
                                        p.w(s, "VE", ms_mark[ch])
                                        p.inc(s.activation(sq2[:, ch, :512], psu[:, :], AF.Square), "SE")
                                    on("scalar", _sq2)
                                    se_q2 = p.cnt["SE"]

                                    def _ms(v, se_q2=se_q2, ch=ch):
                                        p.w(v, "SE", se_q2)
                                        p.w(v, "SE", sqr_mark[ch])
                                        sv = sq2[:, ch, :512].rearrange("p (h d) -> p h d", h=4)
                                        p.inc(v.tensor_reduce(msq[:, ch, :], sv, axis=AX.X, op=ALU.add), "VE")
                                        p.w(v, "VE")
                                        p.inc(v.tensor_scalar(msq[:, ch, :], msq[:, ch, :], 1.0 / DH, RMS_EPS,
                                                              op0=ALU.mult, op1=ALU.add), "VE")
                                    on("vector", _ms)
                                    ms_mark[ch] = p.cnt["VE"]
                                    ve_m = p.cnt["VE"]

                                    def _sqr(s, ve_m=ve_m, ch=ch):
                                        p.w(s, "VE", ve_m)
                                        p.inc(s.activation(rstd[:, ch, :], msq[:, ch, :], AF.Sqrt), "SE")
                                    on("scalar", _sqr)
                                    sqr_mark[ch] = p.cnt["SE"]
                                    se_r2 = p.cnt["SE"]

                                    def _rr(v, se_r2=se_r2, psu=psu, cg_t=cg_t, sg_t=sg_t,
                                            dtm=dtm, dqg=dqg, ch=ch):
                                        p.w(v, "SE", se_r2)
                                        p.w(v, "PE", trq_mark[ch])
                                        p.inc(v.reciprocal(rstd[:, ch, :], rstd[:, ch, :]), "VE")
                                        p.w(v, "VE")
                                        sl = slice(dqg * 512, (dqg + 1) * 512)
                                        p.inc(v.tensor_tensor(asc[:, ch, :], psu[:, :], cg_t[:, sl], op=ALU.mult), "VE")
                                        swap = bass.AP(psu.handle if hasattr(psu, "handle") else psu,
                                                       1, [[512, 128], [2, 256], [-1, 2]])
                                        p.inc(v.scalar_tensor_tensor(
                                            bsc[:, ch, :].rearrange("p (a t) -> p a t", t=2), swap, 1.0,
                                            sg_t[:, sl].rearrange("p (a t) -> p a t", t=2),
                                            op0=ALU.mult, op1=ALU.mult), "VE")
                                        p.w(v, "VE")
                                        p.inc(v.tensor_tensor(asc[:, ch, :], asc[:, ch, :], bsc[:, ch, :], op=ALU.add), "VE")
                                        p.w(v, "VE")
                                        for h4 in range(4):
                                            p.inc(v.tensor_scalar(dtm[:, h4 * 128:(h4 + 1) * 128],
                                                                  asc[:, ch, h4 * 128:(h4 + 1) * 128],
                                                                  rstd[:, ch, h4:h4 + 1], None, ALU.mult), "VE")
                                    on("vector", _rr)
                                ve_rope = p.cnt["VE"]

                                scrA = ps[i][:, :].bitcast(BF16)
                                scrB = ps[4 + i][:, :].bitcast(BF16)
                                pe1 = p.cnt["PE"]
                                se1 = p.cnt["SE"]

                                def _trq(t, ve_rope=ve_rope, scrA=scrA, scrB=scrB, se1=se1):
                                    p.w(t, "VE", ve_rope)
                                    p.w(t, "SE")
                                    for j in range(8):
                                        if j >= 2:
                                            t.wait_ge(sems["SE"], se1 + j - 1)
                                        srcb = (qtm if j < 4 else ktm)[:, (j % 4) * 128:(j % 4 + 1) * 128]
                                        p.inc(t.transpose((scrA, scrB)[j % 2][:, :128], srcb, ident[:, :]), "PE")
                                on("tensor", _trq)
                                trq_mark[0] = p.cnt["PE"]
                                trq_mark[1] = p.cnt["PE"]

                                def _evq(s, pe1=pe1, i=i, tt=tt, dqg=dqg, scrA=scrA, scrB=scrB):
                                    p.w(s, "SY2")
                                    for j in range(8):
                                        s.wait_ge(sems["PE"], pe1 + j + 1)
                                        if j < 4:
                                            dst = QT[:, dqg * 4 + j, tt * 128:(tt + 1) * 128]
                                        else:
                                            dst = kst[:, j - 4, :]
                                        p.inc(s.copy(dst, (scrA, scrB)[j % 2][:, :128]), "SE")
                                on("scalar", _evq)
                                se_k = p.cnt["SE"]

                                def _ko(y, tt=tt, dqg=dqg, se_k=se_k):
                                    p.w(y, "SE", se_k)
                                    dst_ap = bass.AP(kT_lh[dqg // 2], ((dqg % 2) * 512) * L + tt * 128,
                                                     [[L, 128], [128 * L, 4], [1, 128]])
                                    p.inc(y.dma_start(out=dst_ap, in_=kst[:, :, :]), "SY2", 16)
                                on("sync", _ko)

                        for dqg in (2 * hfh, 2 * hfh + 1):
                          for g0 in range(0, NT, TTA):
                            for k in range(16):
                                buf = k % 8

                                def _lv(y, k=k, buf=buf, dqg=dqg):
                                    if NC > 1:
                                        p.w(y, "CC", 6)
                                    else:
                                        p.w(y, "GPC", 64)
                                    p.w(y, "PE", pe_buf[buf])
                                    sap = bass.AP(wv_bf, k * 128 * D + dqg * 512, [[D, 128], [1, 512]])
                                    p.inc(y.dma_start(out=wqk[:, buf, 0, :], in_=sap), f"GW{buf}", 16)
                                on("sync", _lv)
                                gw_c = (f"GW{buf}", p.cnt[f"GW{buf}"])

                                def _vmm(t, k=k, buf=buf, gw_c=gw_c, g0=g0):
                                    p.w(t, gw_c[0], gw_c[1])
                                    p.w(t, "SE", se_xnT)
                                    if k == 0:
                                        p.w(t, "VE")
                                        p.w(t, "SE")
                                    mm = None
                                    for i in range(TTA):
                                        mm = t.matmul(ps[i][:, :], xnT[:, g0 + i, k, :], wqk[:, buf, 0, :],
                                                      start=(k == 0), stop=(k == 15))
                                    p.inc(mm, "PE")
                                on("tensor", _vmm)
                                pe_buf[buf] = p.cnt["PE"]
                            pe_v = p.cnt["PE"]

                            for i in range(TTA):
                                tt = g0 + i

                                def _vev(s, i=i, pe_v=pe_v):
                                    p.w(s, "PE", pe_v)
                                    p.w(s, "SY2")
                                    p.inc(s.copy(vstg[:, :], ps[i][:, :]), "SE")
                                on("scalar", _vev)
                                se_v = p.cnt["SE"]

                                def _vo(y, tt=tt, dqg=dqg, se_v=se_v):
                                    p.w(y, "SE", se_v)
                                    hf = dqg // 2
                                    p.inc(y.dma_start(out=v_lh[hf].ap()[tt * 128:(tt + 1) * 128,
                                                                        (dqg % 2) * 512:(dqg % 2 + 1) * 512],
                                                      in_=vstg[:, :]), "SY2", 16)
                                on("sync", _vo)

                        if NC > 1:
                            def _gkv(g, hfh=hfh):
                                p.w(g, "SY2")
                                rg = [list(range(NC))]
                                p.inc(g.collective_compute("AllGather", ALU.bypass, replica_groups=rg,
                                                           ins=[kT_lh[hfh].ap().opt()],
                                                           outs=[kT_agh[hfh].ap().opt()]), "CC")
                                p.inc(g.collective_compute("AllGather", ALU.bypass, replica_groups=rg,
                                                           ins=[v_lh[hfh].ap().opt()],
                                                           outs=[v_agh[hfh].ap().opt()]), "CC")
                            on("gpsimd", _gkv)
                    flush()
                barrier()

                with contextlib.ExitStack() as fl:
                    kbuf = fl.enter_context(nc_.sbuf_tensor("kbuf", [128, 3, L], BF16))
                    vbuf = fl.enter_context(nc_.sbuf_tensor("vbuf", [128, 3, NT, 128], BF16))
                    pbuf = fl.enter_context(nc_.sbuf_tensor("pbuf", [128, 3, L], BF16))
                    tbuf = fl.enter_context(nc_.sbuf_tensor("tbuf", [128, 3, L], F32))
                    paccum = fl.enter_context(nc_.sbuf_tensor("paccum", [128, 2, L], F32))
                    cmask = fl.enter_context(nc_.sbuf_tensor("cmask", [128, NTG, 128], BF16))
                    lr_sb = fl.enter_context(nc_.sbuf_tensor("lr_sb", [1, L], F32))
                    ao_raw = fl.enter_context(nc_.sbuf_tensor("ao_raw", [128, L], BF16))
                    ps_sT = [fl.enter_context(nc_.psum_tensor(f"ps_sT{i}", [128, L], F32)) for i in range(2)]
                    ps_oT = [fl.enter_context(nc_.psum_tensor(f"ps_oT{i}", [128, L], F32)) for i in range(2)]

                    def _ldm(y):
                        p.inc(y.dma_start(out=cmask[:, :, :],
                                          in_=cmask_in.ap().rearrange("p (g q) -> p g q", g=NTG)), "SY", 16)
                    on("sync", _ldm)
                    sy_m = p.cnt["SY"]

                    sy_kv = {}
                    se_tanh = {}
                    se_exp = {}
                    pe_av = {}
                    pe_av_kv = {}
                    ve_pacc = {}
                    last_g = NTG - 1
                    pidx = 0
                    av_q = []
                    pending_epi1 = None
                    epi_q = []
                    ve_gate = 0
                    se_gate = 0
                    se_ao_cur = 0
                    pe_lmm_mark = {0: 0, 1: 0}
                    sea_mark = {0: 0, 1: 0}
                    veg_mark = {0: 0, 1: 0}

                    NT2 = NT // 2
                    for h in range(NH):
                        for r in range(NC):
                            kvidx = h * NC + r
                            kvp = kvidx % 3

                            def _lkv(y, h=h, r=r, kvidx=kvidx, kvp=kvp):
                                hf, hh = h // 8, h % 8
                                if NC > 1:
                                    p.w(y, "CC", 9 + 2 * hf)
                                if kvidx >= 3:
                                    p.w(y, "PE", pe_av_kv[kvidx - 3])
                                ksrc = kT_agh[hf].ap()[r * HD2 + hh * 128:r * HD2 + (hh + 1) * 128, :]
                                p.inc(y.dma_start(out=kbuf[:, kvp, :], in_=ksrc), f"SYK{kvp}", 16)
                                for vh in range(2):
                                    vsrc = bass.AP(v_agh[hf],
                                                   (r * L + vh * NT2 * 128) * HD2 + hh * 128,
                                                   [[HD2, 128], [128 * HD2, NT2], [1, 128]])
                                    p.inc(y.dma_start(out=vbuf[:, kvp, vh * NT2:(vh + 1) * NT2, :],
                                                      in_=vsrc), f"SYK{kvp}", 16)
                            on("sync", _lkv)
                            sy_kv[kvidx] = (f"SYK{kvp}", p.cnt[f"SYK{kvp}"])

                            for s_ in range(NT):
                                g = s_ * NC + r
                                i0 = _i0_of(g, NC)
                                lo = i0 * 128
                                par = pidx % 2
                                pa3 = pidx % 3
                                hp = h % 2

                                def _qkm(t, r=r, s_=s_, h=h, lo=lo, par=par, kvidx=kvidx,
                                         pidx=pidx, kvp=kvp, g=g):
                                    p.w(t, sy_kv[kvidx][0], sy_kv[kvidx][1])
                                    p.w(t, "SY", sy_m)
                                    if pidx - 2 >= 0 and (pidx - 2) in se_tanh:
                                        p.w(t, "SE", se_tanh[pidx - 2])
                                    mm = None
                                    for (c0, c1) in _chunks512(lo, L):
                                        mm = t.matmul(ps_sT[par][:, c0:c1],
                                                      kbuf[:, kvp, s_ * 128:(s_ + 1) * 128],
                                                      QT[:, h, c0:c1], start=True, stop=True,
                                                      skip_group_check=True)
                                    mm = t.matmul(ps_sT[par][:, lo:lo + 128], ident[:, :],
                                                  cmask[:, g, :], start=False, stop=True,
                                                  skip_group_check=True)
                                    p.inc(mm, "PE")
                                on("tensor", _qkm)
                                pe_here = p.cnt["PE"]

                                if pending_epi1 is None and epi_q:
                                    epi_q.pop(0)()
                                while len(av_q) >= 2:
                                    av_q.pop(0)()

                                def _th(s, par=par, pa3=pa3, lo=lo, pe_here=pe_here, pidx=pidx):
                                    p.w(s, "PE", pe_here)
                                    p.inc(s.activation(tbuf[:, pa3, lo:L], ps_sT[par][:, lo:L],
                                                       AF.Tanh, scale=TANH_SCALE), "SE")
                                on("scalar", _th)
                                se_tanh[pidx] = p.cnt["SE"]
                                se_t = p.cnt["SE"]

                                def _ex(s, pa3=pa3, lo=lo, se_t=se_t, pidx=pidx):
                                    p.w(s, "SE", se_t)
                                    if pidx - 3 >= 0 and (pidx - 3) in pe_av:
                                        p.w(s, "PE", pe_av[pidx - 3])
                                    if pidx - 3 >= 0 and (pidx - 3) in ve_pacc:
                                        p.w(s, "VE", ve_pacc[pidx - 3])
                                    p.inc(s.activation(pbuf[:, pa3, lo:L], tbuf[:, pa3, lo:L],
                                                       AF.Exp, scale=50.0), "SE")
                                on("scalar", _ex)
                                se_exp[pidx] = p.cnt["SE"]
                                se_e = p.cnt["SE"]

                                if pending_epi1 is not None:
                                    pending_epi1()
                                    pending_epi1 = None

                                def _pa(v, g=g, lo=lo, pa3=pa3, se_e=se_e, pidx=pidx, hp=hp):
                                    p.w(v, "SE", se_e)
                                    if pidx - 1 >= 0 and (pidx - 1) in ve_pacc:
                                        p.w(v, "VE", ve_pacc[pidx - 1])
                                    if g == 0 and pe_lmm_mark[hp] > 0:
                                        p.w(v, "PE", pe_lmm_mark[hp])
                                    if g == 0:
                                        p.inc(v.tensor_copy(paccum[:, hp, :], pbuf[:, pa3, :]), "VE")
                                    else:
                                        p.inc(v.tensor_tensor(paccum[:, hp, lo:L], paccum[:, hp, lo:L],
                                                              pbuf[:, pa3, lo:L], op=ALU.add), "VE")
                                on("vector", _pa)
                                ve_pacc[pidx] = p.cnt["VE"]

                                def make_av(r=r, s_=s_, g=g, lo=lo, pa3=pa3, se_e=se_e, hp=hp,
                                            pidx=pidx, kvidx=kvidx, kvp=kvp, first_of_head=(g == 0)):
                                    def _av(t):
                                        p.w(t, "SE", se_e)
                                        if first_of_head:
                                            if sea_mark[hp] > 0:
                                                p.w(t, "SE", sea_mark[hp])
                                            if veg_mark[hp] > 0:
                                                p.w(t, "VE", veg_mark[hp])
                                        mm = None
                                        for (c0, c1) in _chunks512(lo, L):
                                            mm = t.matmul(ps_oT[hp][:, c0:c1], vbuf[:, kvp, s_, :],
                                                          pbuf[:, pa3, c0:c1],
                                                          start=(g == 0), stop=(g == last_g),
                                                          skip_group_check=True)
                                        p.inc(mm, "PE")
                                        pe_av[pidx] = p.cnt["PE"]
                                        pe_av_kv[kvidx] = p.cnt["PE"]
                                    def emit():
                                        on("tensor", _av)
                                    return emit
                                av_q.append(make_av())
                                pidx += 1

                        for _e in av_q:
                            _e()
                        av_q = []
                        pe_hd0 = p.cnt["PE"]
                        ve_pl = ve_pacc[pidx - 1]

                        def make_epi1(pe_hd0=pe_hd0, hp=hp):
                            def emit():
                                nonlocal se_ao_cur

                                def _aoe(s, pe_hd0=pe_hd0, hp=hp):
                                    p.w(s, "PE", pe_hd0)
                                    if ve_gate > 0:
                                        p.w(s, "VE", ve_gate)
                                    p.inc(s.copy(ao_raw[:, :], ps_oT[hp][:, :]), "SE")
                                on("scalar", _aoe)
                                se_ao_cur = p.cnt["SE"]
                                sea_mark[hp] = p.cnt["SE"]
                            return emit

                        epi_state = {}

                        def make_epi2a(hp=hp, ve_pl=ve_pl, epi_state=epi_state):
                            def emit():
                                se_ao = se_ao_cur
                                epi_state["se_ao"] = se_ao

                                def _lmm(t, ve_pl=ve_pl, se_ao=se_ao, hp=hp):
                                    p.w(t, "SE", se_ao)
                                    p.w(t, "VE", ve_pl)
                                    mm = None
                                    for (c0, c1) in _chunks512(0, L):
                                        mm = t.matmul(ps_oT[hp][0:1, c0:c1], ones_cf[:, :],
                                                      paccum[:, hp, c0:c1],
                                                      start=True, stop=True, skip_group_check=True)
                                    p.inc(mm, "PE")
                                on("tensor", _lmm)
                                pe_lmm_mark[hp] = p.cnt["PE"]
                                pe_lmm = p.cnt["PE"]

                                def _lcp(s, pe_lmm=pe_lmm, hp=hp):
                                    p.w(s, "PE", pe_lmm)
                                    p.w(s, "VE", ve_gate)
                                    p.inc(s.copy(lr_sb[:, :], ps_oT[hp][0:1, :]), "SE")
                                on("scalar", _lcp)
                                se_lcp = p.cnt["SE"]

                                def _lev(v, se_lcp=se_lcp):
                                    p.w(v, "SE", se_lcp)
                                    p.inc(v.reciprocal(lr_sb[:, :], lr_sb[:, :]), "VE")
                                on("vector", _lev)
                                epi_state["ve_l"] = p.cnt["VE"]
                            return emit

                        def make_epi2b(h=h, hp=hp, epi_state=epi_state):
                            def emit():
                                nonlocal ve_gate, se_gate
                                se_ao = epi_state["se_ao"]
                                ve_l = epi_state["ve_l"]

                                def _lbc(t, ve_l=ve_l, hp=hp):
                                    p.w(t, "VE", ve_l)
                                    mm = None
                                    for (c0, c1) in _chunks512(0, L):
                                        mm = t.matmul(ps_oT[hp][:, c0:c1], ones_rf[:, :], lr_sb[:, c0:c1],
                                                      start=True, stop=True, skip_group_check=True)
                                    p.inc(mm, "PE")
                                on("tensor", _lbc)
                                pe_lb = p.cnt["PE"]

                                def _aom(v, h=h, se_ao=se_ao, pe_lb=pe_lb, hp=hp):
                                    p.w(v, "SE", se_ao)
                                    p.w(v, "PE", pe_lb)
                                    p.inc(v.tensor_tensor(attno[:, h, :], ao_raw[:, :],
                                                          ps_oT[hp][:, :], op=ALU.mult), "VE")
                                on("vector", _aom)
                                ve_gate = p.cnt["VE"]
                                se_gate = p.cnt["SE"]
                                veg_mark[hp] = p.cnt["VE"]
                            return emit
                        pending_epi1 = make_epi1()
                        epi_q.append(make_epi2a())
                        epi_q.append(make_epi2b())
                    if pending_epi1 is not None:
                        pending_epi1()
                        pending_epi1 = None
                    while epi_q:
                        epi_q.pop(0)()
                    flush()
                barrier()

                with contextlib.ExitStack() as os_:
                    wos = os_.enter_context(nc_.sbuf_tensor("wos", [128, 4, 1024], BF16))
                    ps_o = [os_.enter_context(nc_.psum_tensor(f"ops{i}", [128, 512], F32))
                            for i in range(8)]
                    TTO = min(4, NT)

                    pe_bufo = {0: 0, 1: 0, 2: 0, 3: 0}
                    for fh in range(2):
                        for g0 in range(0, NT, TTO):
                            for kc in range(16):
                                buf = kc % 4

                                def _lwo(y, kc=kc, buf=buf, fh=fh):
                                    if NC > 1:
                                        p.w(y, "CC", 7)
                                    else:
                                        p.w(y, "GPC", 64)
                                    if pe_bufo[buf] == 0:
                                        p.w(y, "PE")
                                    else:
                                        p.w(y, "PE", pe_bufo[buf])
                                    sap = bass.AP(wo_bf, kc * 128 * D + fh * 1024, [[D, 128], [1, 1024]])
                                    p.inc(y.dma_start(out=wos[:, buf, :], in_=sap), f"GW{buf}", 16)
                                on("sync", _lwo)
                                gw_c = (f"GW{buf}", p.cnt[f"GW{buf}"])

                                def _om(t, kc=kc, buf=buf, gw_c=gw_c, g0=g0):
                                    p.w(t, gw_c[0], gw_c[1])
                                    if kc == 0:
                                        p.w(t, "VE")
                                    mm = None
                                    for i in range(TTO):
                                        tt = g0 + i
                                        mm = t.matmul(ps_o[2 * i][:, :], attno[:, kc, tt * 128:(tt + 1) * 128],
                                                      wos[:, buf, 0:512], start=(kc == 0), stop=(kc == 15))
                                        mm = t.matmul(ps_o[2 * i + 1][:, :], attno[:, kc, tt * 128:(tt + 1) * 128],
                                                      wos[:, buf, 512:1024], start=(kc == 0), stop=(kc == 15))
                                    p.inc(mm, "PE")
                                on("tensor", _om)
                                pe_bufo[buf] = p.cnt["PE"]
                            pe_o = p.cnt["PE"]

                            for i in range(TTO):
                                tt = g0 + i

                                def _ores(v, i=i, tt=tt, fh=fh, pe_o=pe_o):
                                    p.w(v, "PE", pe_o)
                                    for cb in range(2):
                                        c0 = fh * 1024 + cb * 512
                                        p.inc(v.scalar_tensor_tensor(x_tm[:, tt, c0:c0 + 512],
                                                                     ps_o[2 * i + cb][:, :], 1.0,
                                                                     x_tm[:, tt, c0:c0 + 512],
                                                                     op0=ALU.mult, op1=ALU.add), "VE")
                                on("vector", _ores)
                    flush()

        def write_x_out():
            with contextlib.ExitStack() as ws:
                stg = ws.enter_context(nc_.sbuf_tensor("stg", [128, D], F32))
                for tt in range(NT):
                    def _c(v, tt=tt):
                        p.w(v, "SY2")
                        p.inc(v.tensor_copy(stg[:, :], x_tm[:, tt, :]), "VE")
                    on("vector", _c)
                    ve_c = p.cnt["VE"]

                    def _o(y, tt=tt, ve_c=ve_c):
                        p.w(y, "VE", ve_c)
                        p.inc(y.dma_start(out=out_t.ap()[tt * 128:(tt + 1) * 128, :], in_=stg[:, :]), "SY2", 16)
                    on("sync", _o)
                flush()

        ffn_stage(first_ffn=True, write_out=False)
        barrier()
        if out_stage >= 2:
            attn_stage()
            barrier()
        if out_stage >= 3:
            ffn_stage(first_ffn=False, write_out=True)
        else:
            write_x_out()
        barrier()

        def _fin(y):
            p.w(y, "SY2")
            p.w(y, "SY")
            p.w(y, "GP")
        on("sync", _fin)
        flush()

    build_program.last_cnt = dict(p.cnt)
    return nc_


def _host_prep(inputs, T, NCORES):
    pi = _pi_src(); rho = _rho_src()
    x = np.asarray(inputs["L0_attn_x"]).reshape(-1, D)[:T]
    wq = np.asarray(inputs["L0_attn_wq"]); wk = np.asarray(inputs["L0_attn_wk"])
    wv = np.asarray(inputs["L0_attn_wv"]); wo = np.asarray(inputs["L0_attn_wo"])
    qnw = np.asarray(inputs["L0_attn_q_norm_w"]); knw = np.asarray(inputs["L0_attn_k_norm_w"])
    cos = np.asarray(inputs["L0_attn_rope_cos"]); sin = np.asarray(inputs["L0_attn_rope_sin"])
    gw = np.asarray(inputs["L0_ffn_gate_w"]); uw = np.asarray(inputs["L0_ffn_up_w"])
    dw = np.asarray(inputs["L0_ffn_down_w"])

    NTG = T // 128
    tiles = [_tiles_of(c, NTG, NCORES) for c in range(NCORES)]
    xp = np.ascontiguousarray(x[:, pi])
    wqT = np.ascontiguousarray(wq[:, pi].T); wkT = np.ascontiguousarray(wk[:, pi].T)
    wvT = np.ascontiguousarray(wv[:, pi].T)
    woT = np.ascontiguousarray(wo[pi, :].T)
    g_p = np.ascontiguousarray(gw[rho][:, pi]); u_p = np.ascontiguousarray(uw[rho][:, pi])
    d_p = np.ascontiguousarray(dw[pi][:, rho])
    R1 = I // NCORES; R2 = D // NCORES

    c16 = cos[:NH]; s16 = sin[:NH]
    crep = np.repeat(c16, 2, axis=1).reshape(-1)
    srep = np.repeat(s16, 2, axis=1).reshape(-1)
    sign = np.tile([-1.0, 1.0], D // 2)

    def mkrows(gv):
        gt = np.tile(gv, NH)
        gswap = gt.reshape(-1, 2)[:, ::-1].reshape(-1)
        cgr = (gt * crep).astype(np.float32)[None]
        sgr = (gswap * srep * sign).astype(np.float32)[None]
        return np.ascontiguousarray(cgr), np.ascontiguousarray(sgr)

    cg_q, sg_q = mkrows(qnw); cg_k, sg_k = mkrows(knw)
    ident = np.eye(128, dtype=BF)

    kk = np.arange(128)[:, None]; qq = np.arange(128)[None, :]
    diag = np.where(kk <= qq, 0.0, -3e9)
    keep = np.zeros((128, 128))
    full = np.full((128, 128), -3e9)

    in_maps = []
    for c in range(NCORES):
        rows = np.concatenate([np.arange(t * 128, (t + 1) * 128) for t in tiles[c]])
        cm = np.empty((128, NTG, 128), dtype=np.float32)
        for g in range(NTG):
            i0 = _i0_of(g, NCORES)
            G = NCORES * i0 + c
            cm[:, g, :] = keep if G > g else (diag if G == g else full)
        RS = D // NCORES
        in_maps.append({
            "x_loc": np.ascontiguousarray(xp[rows]),
            "wqT": np.ascontiguousarray(wqT[c * RS:(c + 1) * RS]),
            "wkT": np.ascontiguousarray(wkT[c * RS:(c + 1) * RS]),
            "wvT": np.ascontiguousarray(wvT[c * RS:(c + 1) * RS]),
            "woT": np.ascontiguousarray(woT[c * RS:(c + 1) * RS]),
            "gsh": np.ascontiguousarray(g_p[c * R1:(c + 1) * R1]),
            "ush": np.ascontiguousarray(u_p[c * R1:(c + 1) * R1]),
            "dsh": np.ascontiguousarray(d_p[c * R2:(c + 1) * R2]),
            "cg_q": cg_q, "sg_q": sg_q, "cg_k": cg_k, "sg_k": sg_k,
            "ident_in": ident,
            "cmask_in": np.ascontiguousarray(cm.reshape(128, NTG * 128).astype(BF)),
        })
    return in_maps, tiles, pi


def _host_finish(results, tiles, pi, T, NCORES):
    out = np.empty((T, D), dtype=np.float32)
    for c in range(NCORES):
        rows = np.concatenate([np.arange(t * 128, (t + 1) * 128) for t in tiles[c]])
        out[rows] = results[c]["out_loc"]
    inv = np.empty(D, dtype=np.int64); inv[pi] = np.arange(D)
    return np.ascontiguousarray(out[:, inv])[None]


def kernel(**inputs):
    T, NCORES = 8192, 8
    in_maps, tiles, pi = _host_prep(inputs, T, NCORES)
    nc_ = build_program(T, NCORES, out_stage=3)
    res = run_bass_kernel_spmd(nc_, in_maps, core_ids=list(range(NCORES)))
    return _host_finish(res.results, tiles, pi, T, NCORES)
